# version 02
# baseline (speedup 1.0000x reference)
import numpy as np
import ml_dtypes
from contextlib import ExitStack

import concourse.bass as bass
import concourse.mybir as mybir
from concourse.bass_utils import run_bass_kernel_spmd

F32 = mybir.dt.float32
BF16 = mybir.dt.bfloat16
I32 = mybir.dt.int32
AF = mybir.ActivationFunctionType
ALU = mybir.AluOpType
AX = mybir.AxisListType

NCORES = 8
S = 16384
D = 1024
T = 2048
HALO = 128
TB = T + HALO
NT = TB // 128
BSHIFT = 40.0
SILU1 = 1.0 / 127.0
SILU2 = 1.0 / (127.0 * 127.0)
TINY = 1e-30

PHASES = ("ln", "qkv", "attn", "ffn")
DEBUG = ()
LAST_DEBUG = {}


def build(phases=PHASES, debug=(), nc_kwargs=None, stage=99, attn_tq=8, attn_lat=True,
          attn_grp=True, attn_sub=8):
    debug = set(debug)
    nc = bass.Bass("TRN2", target_bir_lowering=False, **(nc_kwargs or {}))

    x_in = nc.dram_tensor("x_in", [TB, D], F32, kind="ExternalInput").ap()
    qkvw_qk = nc.dram_tensor("qkvw_qk", [16, 128, 1024], F32, kind="ExternalInput").ap()
    qkvw_v = nc.dram_tensor("qkvw_v", [8, 128, 1024], F32, kind="ExternalInput").ap()
    outw_in = nc.dram_tensor("outw_in", [8, 128, 1024], F32, kind="ExternalInput").ap()
    rqw_in = nc.dram_tensor("rqw_in", [4, 128, 512], F32, kind="ExternalInput").ap()
    rkvw_in = nc.dram_tensor("rkvw_in", [8, 128, 512], F32, kind="ExternalInput").ap()
    routw_in = nc.dram_tensor("routw_in", [8, 128, 256], F32, kind="ExternalInput").ap()
    wgt_in = nc.dram_tensor("wgt_in", [24, 128, 1024], I32, kind="ExternalInput").ap()
    wut_in = nc.dram_tensor("wut_in", [24, 128, 1024], I32, kind="ExternalInput").ap()
    wdt_in = nc.dram_tensor("wdt_in", [24, 128, 1024], I32, kind="ExternalInput").ap()
    bcol_in = nc.dram_tensor("bcol_in", [128, 40], F32, kind="ExternalInput").ap()
    brow_in = nc.dram_tensor("brow_in", [1, 2560], F32, kind="ExternalInput").ap()
    bdg_in = nc.dram_tensor("bdg_in", [128, 8, 32], F32, kind="ExternalInput").ap()
    ebc_in = nc.dram_tensor("ebc_in", [16, 8, 128], BF16, kind="ExternalInput").ap()
    ident_in = nc.dram_tensor("ident_in", [128, 128], BF16, kind="ExternalInput").ap()
    identf_in = nc.dram_tensor("identf_in", [64, 64], F32, kind="ExternalInput").ap()
    maskA_in = nc.dram_tensor("maskA_in", [128, 512], BF16, kind="ExternalInput").ap()
    maskB_in = nc.dram_tensor("maskB_in", [128, 512], BF16, kind="ExternalInput").ap()
    maskC_in = nc.dram_tensor("maskC_in", [128, 512], BF16, kind="ExternalInput").ap()
    maskF_in = nc.dram_tensor("maskF_in", [128, 512], BF16, kind="ExternalInput").ap()
    latmask_in = nc.dram_tensor("latmask_in", [64, 16], F32, kind="ExternalInput").ap()

    y_out = nc.dram_tensor("y_out", [T, D], F32, kind="ExternalOutput").ap()
    gin = nc.dram_tensor("gin", [8, 512], F32)
    gout = nc.dram_tensor("gout", [64, 512], F32)

    dbg = {}

    def dbg_out(name, shape):
        if name == "g0_local":
            for nm, shp in [("g0_local", [128, 8, 256]), ("g0_remote", [128, 8, 256]),
                            ("g0_mixed", [128, 8, 256]), ("g0_gate", [16, 256])]:
                if nm in debug and nm not in dbg:
                    dbg[nm] = nc.dram_tensor("dbg_" + nm, shp, F32, kind="ExternalOutput").ap()
            return dbg.get(name)
        if name in debug and name not in dbg:
            dbg[name] = nc.dram_tensor("dbg_" + name, list(shape), F32, kind="ExternalOutput").ap()
        return dbg.get(name)

    ctr = {}

    def inc(instr, sem, n=1):
        instr.then_inc(sem, n)
        ctr[sem.name] = ctr.get(sem.name, 0) + n
        return ctr[sem.name]

    def val(sem):
        return ctr.get(sem.name, 0)

    with ExitStack() as top:
        e = top.enter_context

        xn_fm = e(nc.sbuf_tensor("xn_fm", [128, 8, TB], BF16))
        s_tm = e(nc.sbuf_tensor("s_tm", [128, 8, 1024], F32))
        rq_sb = e(nc.sbuf_tensor("rq_sb", [64, 4, T], BF16))
        rkv_all = e(nc.sbuf_tensor("rkv_all", [64, 512], F32))
        ident = e(nc.sbuf_tensor("ident", [128, 128], BF16))
        identf = e(nc.sbuf_tensor("identf", [64, 64], F32))
        ones_bf = e(nc.sbuf_tensor("ones_bf", [128, 512], BF16))
        tiny_bf = e(nc.sbuf_tensor("tiny_bf", [128, 1], BF16))
        maskA = e(nc.sbuf_tensor("maskA", [128, 512], BF16))
        maskB = e(nc.sbuf_tensor("maskB", [128, 512], BF16))
        maskC = e(nc.sbuf_tensor("maskC", [128, 512], BF16))
        maskF = e(nc.sbuf_tensor("maskF", [128, 512], BF16))
        latmask = e(nc.sbuf_tensor("latmask", [64, 16], F32))
        bdg = e(nc.sbuf_tensor("bdg", [128, 8, 32], BF16))
        bcol = e(nc.sbuf_tensor("bcol", [128, 42], F32))

        P = [e(nc.psum_tensor(f"P{i}", [128, 512], F32)) for i in range(6)]

        s_sy = e(nc.semaphore("s_sy"))
        s_gp = e(nc.semaphore("s_gp"))
        s_cc = e(nc.semaphore("s_cc"))
        s_st = e(nc.semaphore("s_st"))
        s_pe = e(nc.semaphore("s_pe"))
        s_dv = e(nc.semaphore("s_dv"))
        s_sc = e(nc.semaphore("s_sc"))

        block = e(nc.Block())

        BSH = bcol[:, 40:41]
        EPS = bcol[:, 41:42]
        ZCOL = bcol[:, 36:37]

        pfree = {i: (s_pe, 0) for i in range(8)}

        def _cut():
            @block.sync
            def _(sy):
                for sem in (s_sy, s_gp, s_dv, s_sc, s_pe, s_st, s_cc):
                    if val(sem):
                        sy.wait_ge(sem, val(sem))


        @block.sync
        def _(sy):
            inc(sy.dma_start(out=ident[:, :], in_=ident_in), s_sy, 16)
            inc(sy.dma_start(out=identf[:, :], in_=identf_in), s_sy, 16)
            inc(sy.dma_start(out=maskA[:, :], in_=maskA_in), s_sy, 16)
            inc(sy.dma_start(out=maskB[:, :], in_=maskB_in), s_sy, 16)
            inc(sy.dma_start(out=maskC[:, :], in_=maskC_in), s_sy, 16)
            inc(sy.dma_start(out=maskF[:, :], in_=maskF_in), s_sy, 16)
            inc(sy.dma_start(out=latmask[:, :], in_=latmask_in), s_sy, 16)
            inc(sy.dma_start(out=bcol[:, 0:40], in_=bcol_in), s_sy, 16)
        m_consts = val(s_sy)

        @block.gpsimd
        def _(g):
            inc(g.dma_start(out=bdg[:, :, :], in_=bdg_in), s_gp, 16)
        m_consts_gp = val(s_gp)

        @block.vector
        def _(v):
            v.memset(ones_bf[:, :], 1.0)
            v.memset(tiny_bf[:, :], TINY)
            v.memset(bcol[:, 40:41], -BSHIFT)
            inc(v.memset(bcol[:, 41:42], 1e-5), s_dv, 1)
        m_memsets = val(s_dv)

        ln = ExitStack()
        x_sb = ln.enter_context(nc.sbuf_tensor("x_sb", [128, NT, 1024], F32))
        xn_tm = ln.enter_context(nc.sbuf_tensor("xn_tm", [128, 2, 1024], BF16))
        stats_all = ln.enter_context(nc.sbuf_tensor("stats_all", [128, NT, 12], F32))
        mv_all = ln.enter_context(nc.sbuf_tensor("mv_all", [128, 2 * NT], F32))
        rstd_all = ln.enter_context(nc.sbuf_tensor("rstd_all", [128, NT], F32))
        pooled = ln.enter_context(nc.sbuf_tensor("pooled", [128, 8, 8], F32))
        pooled_bf = ln.enter_context(nc.sbuf_tensor("pooled_bf", [128, 8, 8], BF16))
        rqw_sb = ln.enter_context(nc.sbuf_tensor("rqw_sb", [128, 4, 512], BF16))
        rkvw_sb = ln.enter_context(nc.sbuf_tensor("rkvw_sb", [128, 8, 512], BF16))
        rkv_own = ln.enter_context(nc.sbuf_tensor("rkv_own", [8, 512], F32))
        brkv_f = ln.enter_context(nc.sbuf_tensor("brkv_f", [1, 512], F32))
        brkv = ln.enter_context(nc.sbuf_tensor("brkv", [1, 512], BF16))
        brkv_lo = ln.enter_context(nc.sbuf_tensor("brkv_lo", [1, 512], BF16))
        PT = [ln.enter_context(nc.psum_tensor(f"PT{i}", [128, 512], F32)) for i in range(2)]

        @block.sync
        def _(sy):
            sy.wait_ge(s_dv, m_memsets)
            for t in range(NT):
                inc(sy.dma_start(out=x_sb[:, t, :], in_=x_in[t * 128:(t + 1) * 128, :]), s_sy, 16)
        m_xload0 = m_consts

        @block.gpsimd
        def _(g):
            for h in range(4):
                inc(g.dma_start(out=rqw_sb[:, h, :], in_=rqw_in[h]), s_gp, 16)
            for dc in range(8):
                inc(g.dma_start(out=rkvw_sb[:, dc, :], in_=rkvw_in[dc]), s_gp, 16)
            inc(g.dma_start(out=brkv_f[:, :], in_=brow_in[0:1, 2048:2560]), s_gp, 16)
        m_smallw = val(s_gp)

        @block.vector
        def _(v):
            v.wait_ge(s_gp, m_smallw)
            v.tensor_copy(brkv[:, :], brkv_f[:, :])
            inc(v.scalar_tensor_tensor(brkv_lo[:, :], brkv_f[:, :], 1.0, brkv[:, :],
                                       ALU.mult, ALU.subtract), s_dv, 1)
        m_bsplit0 = val(s_dv)

        if stage < 2:
            _cut()
            ln.close()
            return nc, dbg

        @block.vector
        def _(v):
            for t in range(NT):
                v.wait_ge(s_sy, m_xload0 + 16 * (t + 1))
                v.bn_stats(stats_all[:, t, 0:6], x_sb[:, t, 0:512])
                ins = v.bn_stats(stats_all[:, t, 6:12], x_sb[:, t, 512:1024])
            inc(ins, s_dv, 1)
            v.wait_ge(s_dv, val(s_dv))
            for t in range(NT):
                ins = v.bn_aggr(mv_all[:, 2 * t:2 * t + 2], stats_all[:, t, 0:12])
            inc(ins, s_dv, 1)
        m_aggr = val(s_dv)

        @block.scalar
        def _(s):
            s.wait_ge(s_dv, m_aggr)
            s.wait_ge(s_dv, m_memsets)
            inc(s.activation(rstd_all[:, :], mv_all[:, 1:2 * NT:2], AF.Sqrt,
                             bias=EPS, scale=1.0), s_sc, 1)
        m_sqrt = val(s_sc)

        @block.vector
        def _(v):
            v.wait_ge(s_sc, m_sqrt)
            inc(v.reciprocal(rstd_all[:, :], rstd_all[:, :]), s_dv, 1)
        m_rstd = val(s_dv)

        if stage < 3:
            _cut()
            ln.close()
            return nc, dbg

        apply_m = [0] * NT
        xgrp_m = [[0, 0] for _ in range(NT)]
        evac_m = [[0, 0] for _ in range(NT)]
        for t in range(NT):
            b = t % 2

            @block.vector
            def _(v, t=t, b=b):
                if t == 0:
                    v.wait_ge(s_dv, m_rstd)
                if t >= 2:
                    v.wait_ge(s_pe, xgrp_m[t - 2][1])
                apply_m[t] = inc(
                    v.tensor_scalar(xn_tm[:, b, :], x_sb[:, t, :],
                                    mv_all[:, 2 * t:2 * t + 1], rstd_all[:, t:t + 1],
                                    ALU.subtract, ALU.mult), s_dv, 1)

            @block.tensor
            def _(tn, t=t, b=b):
                tn.wait_ge(s_dv, apply_m[t])
                for g in range(2):
                    if t >= 1:
                        tn.wait_ge(s_dv, evac_m[t - 1][g])
                    for k in range(4):
                        dc = g * 4 + k
                        ins = tn.matmul(PT[g][:, k * 128:(k + 1) * 128],
                                        xn_tm[:, b, dc * 128:(dc + 1) * 128], ident[:, :],
                                        start=True, stop=True)
                    xgrp_m[t][g] = inc(ins, s_pe, 1)

            @block.vector
            def _(v, t=t):
                for g in range(2):
                    v.wait_ge(s_pe, xgrp_m[t][g])
                    evac_m[t][g] = inc(
                        v.tensor_copy(xn_fm[:, 4 * g:4 * g + 4, t * 128:(t + 1) * 128],
                                      PT[g][:, :].rearrange("p (k n) -> p k n", k=4)),
                        s_dv, 1)
        m_ln_done = evac_m[NT - 1][1]

        if dbg_out("xn", [128, 8, 256]) is not None:
            @block.gpsimd
            def _(g):
                g.wait_ge(s_dv, m_ln_done)
                inc(g.dma_start(out=dbg["xn"], in_=xn_fm[:, :, 0:256]), s_gp, 16)

        if stage < 4:
            _cut()
            ln.close()
            return nc, dbg

        @block.vector
        def _(v):
            v.wait_ge(s_dv, m_ln_done)
            for dc in range(8):
                for j in range(8):
                    ins = v.tensor_reduce(pooled[:, dc, j:j + 1],
                                          xn_fm[:, dc, HALO + 256 * j:HALO + 256 * (j + 1)],
                                          AX.X, ALU.add)
            inc(ins, s_dv, 1)
            v.wait_ge(s_dv, val(s_dv))
            inc(v.tensor_scalar(pooled_bf[:, :, :], pooled[:, :, :], 1.0 / 256.0, None,
                                ALU.mult), s_dv, 1)
        m_pooled = val(s_dv)

        @block.tensor
        def _(tn):
            tn.wait_ge(s_dv, m_pooled)
            tn.wait_ge(s_dv, m_memsets)
            tn.wait_ge(s_gp, m_smallw)
            tn.wait_ge(s_gp, m_consts_gp)
            tn.wait_ge(s_dv, m_bsplit0)
            for dc in range(8):
                tn.matmul(P[0][0:8, :], pooled_bf[:, dc, :], rkvw_sb[:, dc, :],
                          start=(dc == 0), stop=False)
            tn.matmul(P[0][0:8, :], ones_bf[0:1, 0:8], brkv[0:1, :],
                      start=False, stop=False)
            inc(tn.matmul(P[0][0:8, :], ones_bf[0:1, 0:8], brkv_lo[0:1, :],
                          start=False, stop=True), s_pe, 1)
        m_rkvmm = val(s_pe)

        @block.vector
        def _(v):
            v.wait_ge(s_pe, m_rkvmm)
            inc(v.tensor_copy(rkv_own[:, :], P[0][0:8, :]), s_dv, 1)
        m_rkvevac = val(s_dv)
        pfree[0] = (s_dv, m_rkvevac)

        if stage < 5:
            _cut()
            ln.close()
            return nc, dbg

        for h in range(4):
            for g in range(4):
                bank = (h * 4 + g) % 2

                @block.tensor
                def _(tn, h=h, g=g, bank=bank):
                    tn.wait_ge(*pfree[bank])
                    tn.wait_ge(s_dv, m_ln_done)
                    for dc in range(8):
                        ins = tn.matmul(P[bank][0:64, :],
                                        rqw_sb[:, h, dc * 64:(dc + 1) * 64],
                                        xn_fm[:, dc, HALO + g * 512:HALO + (g + 1) * 512],
                                        start=(dc == 0), stop=(dc == 7))
                    inc(ins, s_pe, 1)
                rq_mm = val(s_pe)

                @block.scalar
                def _(s, h=h, g=g, bank=bank, rq_mm=rq_mm):
                    s.wait_ge(s_pe, rq_mm)
                    m = inc(s.activation(rq_sb[:, h, g * 512:(g + 1) * 512], P[bank][0:64, :],
                                         AF.Identity, bias=bcol[0:64, 32 + h:33 + h], scale=1.0),
                            s_sc, 1)
                    pfree[bank] = (s_sc, m)
        m_rq_done = val(s_sc)

        if stage < 6:
            _cut()
            ln.close()
            return nc, dbg

        @block.sync
        def _(sy):
            sy.wait_ge(s_dv, m_rkvevac)
            inc(sy.dma_start(out=gin[:, :], in_=rkv_own[:, :]), s_sy, 16)
        m_gin = val(s_sy)

        @block.gpsimd
        def _(g):
            g.wait_ge(s_sy, m_gin)
            inc(g.collective_compute(
                "AllGather", ALU.bypass, replica_groups=[list(range(NCORES))],
                ins=[gin.ap().opt()], outs=[gout.ap().opt()]), s_cc, 1)
        m_cc = val(s_cc)

        @block.sync
        def _(sy):
            sy.wait_ge(s_cc, m_cc)
            inc(sy.dma_start(out=rkv_all[:, :], in_=gout[:, :]), s_sy, 16)
        m_rkvall = val(s_sy)

        if dbg_out("rkv_all", [64, 512]) is not None:
            @block.sync
            def _(sy):
                inc(sy.dma_start(out=dbg["rkv_all"], in_=rkv_all[:, :]), s_st, 16)
        if dbg_out("rq", [64, 4, 512]) is not None:
            @block.gpsimd
            def _(g):
                g.wait_ge(s_sc, m_rq_done)
                inc(g.dma_start(out=dbg["rq"], in_=rq_sb[:, :, 0:512]), s_gp, 16)

        ln.close()

        arena_free = [(s_sc, m_rq_done), (s_dv, m_pooled), (s_sy, m_gin)]
        m_stm_free = [(s_pe, 0)] * 8

        for half in range(2):
            if "qkv" not in phases:
                break
            kbase = half * 896
            qbase = HALO + half * 1024

            att = ExitStack()
            q_fm = att.enter_context(nc.sbuf_tensor(f"q_fm{half}", [128, 8, 1024], BF16))
            k_fm = att.enter_context(nc.sbuf_tensor(f"k_fm{half}", [128, 8, 1280], BF16))
            v_tm = att.enter_context(nc.sbuf_tensor(f"v_tm{half}", [128, 10, 1024], BF16))
            ow_sb = att.enter_context(nc.sbuf_tensor(f"ow_sb{half}", [128, 8, 1024], BF16))
            rout_sb = att.enter_context(nc.sbuf_tensor(f"rout_sb{half}", [128, 8, 256], BF16))
            rkT = att.enter_context(nc.sbuf_tensor(f"rkT{half}", [64, 4, 64], BF16))
            rkbf = att.enter_context(nc.sbuf_tensor(f"rkbf{half}", [64, 256], BF16))
            rv_aug = att.enter_context(nc.sbuf_tensor(f"rv_aug{half}", [64, 4, 64], BF16))
            otcol = att.enter_context(nc.sbuf_tensor(f"otcol{half}", [65, 1], BF16))
            bvob = att.enter_context(nc.sbuf_tensor(f"bvob{half}", [1, 2048], BF16))
            bvob_lo = att.enter_context(nc.sbuf_tensor(f"bvlo{half}", [1, 2048], BF16))
            ebc = att.enter_context(nc.sbuf_tensor(f"ebc{half}", [16, 8, 128], BF16))
            P6 = att.enter_context(nc.psum_tensor(f"P6_{half}", [128, 512], F32))
            P7 = att.enter_context(nc.psum_tensor(f"P7_{half}", [128, 512], F32))
            pfree[6] = (s_pe, 0)
            pfree[7] = (s_pe, 0)

            wqkv = ExitStack()
            wqk_buf = wqkv.enter_context(nc.sbuf_tensor(f"wqk{half}", [128, 2, 1024], BF16))
            bvf = wqkv.enter_context(nc.sbuf_tensor(f"bvf{half}", [1, 2048], F32))
            wv_sb = wqkv.enter_context(nc.sbuf_tensor(f"wv{half}", [128, 8, 1024], BF16))

            ld_qk = [0] * 16
            mm_qk = [0] * 16

            @block.gpsimd
            def _(g):
                for sem, vv in arena_free:
                    g.wait_ge(sem, vv)
                inc(g.dma_start(out=bvf[:, :], in_=brow_in[0:1, 0:2048]), s_gp, 16)
                inc(g.dma_start(out=ebc[:, :, :], in_=ebc_in), s_gp, 16)
            m_bvf = val(s_gp)

            @block.vector
            def _(v):
                v.wait_ge(s_gp, m_bvf)
                v.tensor_copy(bvob[:, :], bvf[:, :])
                inc(v.scalar_tensor_tensor(bvob_lo[:, :], bvf[:, :], 1.0, bvob[:, :],
                                           ALU.mult, ALU.subtract), s_dv, 1)
            m_bsplit = val(s_dv)

            @block.gpsimd
            def _(g):
                for j in range(2):
                    ld_qk[j] = inc(g.dma_start(out=wqk_buf[:, j, :], in_=qkvw_qk[j]), s_gp, 16)

            for j in range(16):
                is_q = j < 8
                oc = j if is_q else j - 8
                ngroups = [(0, 512), (512, 1024)] if is_q else [(0, 512), (512, 1024), (1024, 1280)]
                base = qbase if is_q else kbase
                dest = q_fm if is_q else k_fm
                banks = [0, 1, 4] if (j % 2 == 0) else [2, 3, 4]
                grp_mm = [0] * len(ngroups)

                @block.tensor
                def _(tn, j=j, ngroups=ngroups, base=base, banks=banks, grp_mm=grp_mm):
                    tn.wait_ge(s_gp, ld_qk[j])
                    tn.wait_ge(s_dv, m_ln_done)
                    for gi, (a0, a1) in enumerate(ngroups):
                        bank = banks[gi]
                        tn.wait_ge(*pfree[bank])
                        for dc in range(8):
                            ins = tn.matmul(P[bank][:, 0:a1 - a0],
                                            wqk_buf[:, j % 2, dc * 128:(dc + 1) * 128],
                                            xn_fm[:, dc, base + a0:base + a1],
                                            start=(dc == 0), stop=(dc == 7))
                        grp_mm[gi] = inc(ins, s_pe, 1)
                    mm_qk[j] = grp_mm[-1]

                @block.scalar
                def _(s, j=j, oc=oc, ngroups=ngroups, banks=banks, dest=dest, grp_mm=grp_mm):
                    colidx = oc if j < 8 else 8 + oc
                    for gi, (a0, a1) in enumerate(ngroups):
                        bank = banks[gi]
                        s.wait_ge(s_pe, grp_mm[gi])
                        m = inc(s.activation(dest[:, oc, a0:a1], P[bank][:, 0:a1 - a0],
                                             AF.Identity, bias=bcol[:, colidx:colidx + 1],
                                             scale=1.0), s_sc, 1)
                        pfree[bank] = (s_sc, m)

                if j + 2 < 16:
                    @block.gpsimd
                    def _(g, j=j):
                        g.wait_ge(s_pe, mm_qk[j])
                        ld_qk[j + 2] = inc(g.dma_start(out=wqk_buf[:, j % 2, :],
                                                       in_=qkvw_qk[j + 2]), s_gp, 16)

            @block.gpsimd
            def _(g):
                for dc in range(8):
                    inc(g.dma_start(out=wv_sb[:, dc, :], in_=qkvw_v[dc]), s_gp, 16)
            m_wv = val(s_gp)

            @block.gpsimd
            def _(g):
                for dc in range(8):
                    inc(g.dma_start(out=ow_sb[:, dc, :], in_=outw_in[dc]), s_gp, 16)
                for oc in range(8):
                    inc(g.dma_start(out=rout_sb[:, oc, :], in_=routw_in[oc]), s_gp, 16)
            m_ow = val(s_gp)

            for t in range(10):
                for gi in range(2):
                    bank = 4 + (t * 2 + gi) % 2
                    if bank == 4 + 1:
                        bank = 5

                    @block.tensor
                    def _(tn, t=t, gi=gi, bank=bank):
                        tn.wait_ge(s_gp, m_wv)
                        tn.wait_ge(s_dv, m_bsplit)
                        tn.wait_ge(*pfree[bank])
                        for dc in range(8):
                            tn.matmul(P[bank][:, :],
                                      xn_fm[:, dc, kbase + t * 128:kbase + (t + 1) * 128],
                                      wv_sb[:, dc, gi * 512:(gi + 1) * 512],
                                      start=(dc == 0), stop=False)
                        tn.matmul(P[bank][:, :], ones_bf[0:1, 0:128],
                                  bvob[0:1, gi * 512:(gi + 1) * 512],
                                  start=False, stop=False)
                        inc(tn.matmul(P[bank][:, :], ones_bf[0:1, 0:128],
                                      bvob_lo[0:1, gi * 512:(gi + 1) * 512],
                                      start=False, stop=True), s_pe, 1)
                    mmv = val(s_pe)

                    @block.vector
                    def _(v, t=t, gi=gi, bank=bank, mmv=mmv):
                        v.wait_ge(s_pe, mmv)
                        m = inc(v.tensor_copy(v_tm[:, t, gi * 512:(gi + 1) * 512],
                                              P[bank][:, :]), s_dv, 1)
                        pfree[bank] = (s_dv, m)
            m_v_done = val(s_dv)

            if half == 0 and dbg_out("qkv", [128, 3, 512]) is not None:
                @block.gpsimd
                def _(g):
                    g.wait_ge(s_sc, val(s_sc))
                    g.wait_ge(s_dv, m_v_done)
                    inc(g.dma_start(out=dbg["qkv"][:, 0:1, :], in_=q_fm[:, 0:1, 0:512]), s_gp, 16)
                    inc(g.dma_start(out=dbg["qkv"][:, 1:2, :], in_=k_fm[:, 0:1, 0:512]), s_gp, 16)
                    inc(g.dma_start(out=dbg["qkv"][:, 2:3, :], in_=v_tm[:, 0:1, 0:512]),
                        s_gp, 16)

            wqkv.close()

            if "attn" not in phases:
                att.close()
                arena_free = [(s_dv, m_v_done), (s_sc, val(s_sc))]
                continue

            scr = ExitStack()
            pt_sb = scr.enter_context(nc.sbuf_tensor(f"pt{half}", [128, 2, 512], BF16))
            local_n = scr.enter_context(nc.sbuf_tensor(f"locn{half}", [128, 8, 256], BF16))
            remote_n = scr.enter_context(nc.sbuf_tensor(f"remn{half}", [128, 8, 256], BF16))
            mixed = scr.enter_context(nc.sbuf_tensor(f"mixed{half}", [128, 8, 256], BF16))
            avst = scr.enter_context(nc.sbuf_tensor(f"avst{half}", [128, 256], F32))
            rr = scr.enter_context(nc.sbuf_tensor(f"rr{half}", [1, 2048], BF16))
            rrL = scr.enter_context(nc.sbuf_tensor(f"rrL{half}", [1, 512], BF16))
            ptl = scr.enter_context(nc.sbuf_tensor(f"ptl{half}", [65, 512], BF16))
            latst = scr.enter_context(nc.sbuf_tensor(f"latst{half}", [128, 256], F32))
            latout = scr.enter_context(nc.sbuf_tensor(f"latout{half}", [128, 2, 256], BF16))
            gsb = scr.enter_context(nc.sbuf_tensor(f"gsb{half}", [16, 256], BF16))
            gtmp = scr.enter_context(nc.sbuf_tensor(f"gtmp{half}", [16, 256], F32))
            dtmp = scr.enter_context(nc.sbuf_tensor(f"dtmp{half}", [128, 256], BF16))

            @block.vector
            def _(v):
                v.wait_ge(s_sy, m_rkvall)
                inc(v.tensor_copy(rkbf[:, :], rkv_all[:, 0:256]), s_dv, 1)
            m_rkbf = val(s_dv)

            @block.tensor
            def _(tn):
                tn.wait_ge(s_dv, m_rkbf)
                tn.wait_ge(*pfree[6])
                for h in range(4):
                    ins = tn.matmul(P6[0:64, h * 64:(h + 1) * 64],
                                    rkbf[:, h * 64:(h + 1) * 64], ident[0:64, 0:64],
                                    start=True, stop=True)
                inc(ins, s_pe, 1)
            m_rkt_mm = val(s_pe)

            @block.vector
            def _(v):
                v.wait_ge(s_pe, m_rkt_mm)
                m = inc(v.tensor_copy(rkT[:, :, :],
                                      P6[0:64, 0:256].rearrange("p (h n) -> p h n", h=4)),
                        s_dv, 1)
                pfree[6] = (s_dv, m)
                for h in range(4):
                    v.tensor_copy(rv_aug[:, h, :],
                                  rkv_all[:, 256 + h * 64:256 + (h + 1) * 64])
                v.memset(otcol[0:64, :], 1.0)
                v.memset(otcol[64:65, :], TINY)
                inc(v.memset(ptl[64:65, :], 1.0), s_dv, 1)
            m_lat_prep = val(s_dv)

            exp_m = {}
            av_m = {}
            avmult_m = {}
            recip_m = {}
            bc_m = {}
            lat_ms = {}
            m_gl_prev = 0
            m_mix_prev = 0
            m_oww_prev = 0

            for tq in range(attn_tq):
                w0 = tq + half
                tcol = (tq % 2) * 128
                use_F = (half == 0 and tq == 0)
                m0 = maskF if use_F else (maskA if tq % 2 == 0 else maskB)

                for bb in range(4):
                    avbank = 4 + bb % 2
                    heads = [4 * bb, 4 * bb + 2, 4 * bb + 1, 4 * bb + 3]

                    @block.tensor
                    def _(tn, tq=tq, bb=bb, w0=w0, heads=heads):
                        if (tq, bb) == (0, 0):
                            tn.wait_ge(s_dv, m_v_done)
                            tn.wait_ge(s_sc, m_rq_done)
                        for b_ in range(4):
                            tn.wait_ge(*pfree[b_])
                        ins = None
                        for s_i, hh in enumerate(heads):
                            r0 = 64 * (hh % 2)
                            dch = hh // 2
                            cb = [0, 1] if s_i < 2 else [2, 3]
                            scol = (s_i % 2) * 128
                            for c in range(2):
                                ko = 128 * (w0 + c)
                                ins = tn.matmul(P[cb[c]][:, scol:scol + 128],
                                                k_fm[r0:r0 + 64, dch, ko:ko + 128],
                                                q_fm[r0:r0 + 64, dch, tq * 128:(tq + 1) * 128],
                                                start=True, stop=True)
                        inc(ins, s_pe, 1)
                    m_sc_mm = val(s_pe)

                    if attn_sub < 2:
                        exp_m[(tq, bb)] = val(s_sc)
                        av_m[(tq, bb)] = val(s_pe)
                        recip_m[(tq, bb)] = val(s_dv)
                        bc_m[(tq, bb)] = val(s_pe)
                        avmult_m[(tq, bb)] = val(s_dv)
                        continue

                    @block.scalar
                    def _(s, tq=tq, bb=bb, m_sc_mm=m_sc_mm):
                        s.wait_ge(s_pe, m_sc_mm)
                        if (tq, bb) != (0, 0):
                            prev = (tq, bb - 1) if bb > 0 else (tq - 1, 3)
                            s.wait_ge(s_pe, av_m[prev])
                        s.activation(pt_sb[:, 0, 0:256], P[0][:, 0:256], AF.Exp,
                                     bias=BSH, scale=0.125)
                        s.activation(pt_sb[:, 0, 256:512], P[2][:, 0:256], AF.Exp,
                                     bias=BSH, scale=0.125)
                        s.activation(pt_sb[:, 1, 0:256], P[1][:, 0:256], AF.Exp,
                                     bias=BSH, scale=0.125)
                        inc(s.activation(pt_sb[:, 1, 256:512], P[3][:, 0:256], AF.Exp,
                                         bias=BSH, scale=0.125), s_sc, 1)
                    exp_m[(tq, bb)] = val(s_sc)
                    for b_ in range(4):
                        pfree[b_] = (s_sc, exp_m[(tq, bb)])

                    if attn_sub < 3:
                        av_m[(tq, bb)] = val(s_pe)
                        recip_m[(tq, bb)] = val(s_dv)
                        bc_m[(tq, bb)] = val(s_pe)
                        avmult_m[(tq, bb)] = val(s_dv)
                        continue

                    @block.vector
                    def _(v, tq=tq, bb=bb, m0=m0):
                        v.wait_ge(s_sc, exp_m[(tq, bb)])
                        v.tensor_tensor(pt_sb[:, 0, :], pt_sb[:, 0, :], m0[:, :], ALU.mult)
                        inc(v.tensor_tensor(pt_sb[:, 1, :], pt_sb[:, 1, :], maskC[:, :],
                                            ALU.mult), s_dv, 1)
                    m_mask = val(s_dv)

                    if attn_sub < 4:
                        av_m[(tq, bb)] = val(s_pe)
                        recip_m[(tq, bb)] = val(s_dv)
                        bc_m[(tq, bb)] = val(s_pe)
                        avmult_m[(tq, bb)] = val(s_dv)
                        continue

                    @block.tensor
                    def _(tn, tq=tq, bb=bb, w0=w0, avbank=avbank, heads=heads, m_mask=m_mask):
                        tn.wait_ge(s_dv, m_mask)
                        key2 = (tq, bb - 2) if bb >= 2 else (tq - 1, bb + 2)
                        if key2 in avmult_m:
                            tn.wait_ge(s_dv, avmult_m[key2])
                        prevd = (tq, bb - 1) if bb > 0 else (tq - 1, 3)
                        if prevd in recip_m:
                            tn.wait_ge(s_dv, recip_m[prevd])
                        else:
                            tn.wait_ge(*pfree[6])
                        for s_i, hh in enumerate(heads):
                            r0 = 64 * (hh % 2)
                            scol = (s_i % 2) * 128
                            for c in range(2):
                                tn.matmul(P[avbank][r0:r0 + 64, scol:scol + 128],
                                          v_tm[:, w0 + c, hh * 64:(hh + 1) * 64],
                                          pt_sb[:, c, s_i * 128:(s_i + 1) * 128],
                                          start=(c == 0), stop=(c == 1))
                        tn.matmul(P6[0:1, :], ones_bf[:, 0:1], pt_sb[:, 0, :],
                                  start=True, stop=False)
                        tn.matmul(P6[0:1, :], ones_bf[:, 0:1], pt_sb[:, 1, :],
                                  start=False, stop=False)
                        inc(tn.matmul(P6[0:1, :], tiny_bf[0:1, 0:1], ones_bf[0:1, 0:512],
                                      start=False, stop=True), s_pe, 1)
                    av_m[(tq, bb)] = val(s_pe)

                    if attn_sub < 5:
                        recip_m[(tq, bb)] = val(s_dv)
                        bc_m[(tq, bb)] = val(s_pe)
                        avmult_m[(tq, bb)] = val(s_dv)
                        continue

                    @block.vector
                    def _(v, tq=tq, bb=bb):
                        v.wait_ge(s_pe, av_m[(tq, bb)])
                        if (tq - 1, bb) in bc_m:
                            v.wait_ge(s_pe, bc_m[(tq - 1, bb)])
                        with nc.allow_low_precision(reason="softmax recip as one bf16 factor"):
                            inc(v.reciprocal(rr[0:1, bb * 512:(bb + 1) * 512], P6[0:1, :]),
                                s_dv, 1)
                    recip_m[(tq, bb)] = val(s_dv)

                    if attn_sub < 6:
                        bc_m[(tq, bb)] = val(s_pe)
                        avmult_m[(tq, bb)] = val(s_dv)
                        continue

                    @block.tensor
                    def _(tn, tq=tq, bb=bb, heads=heads):
                        tn.wait_ge(s_dv, recip_m[(tq, bb)])
                        key2 = (tq, bb - 2) if bb >= 2 else (tq - 1, bb + 2)
                        if key2 in avmult_m:
                            tn.wait_ge(s_dv, avmult_m[key2])
                        if bb < 2 and (tq - 1) in lat_ms:
                            tn.wait_ge(s_dv, lat_ms[tq - 1]["mult"])
                        ins = None
                        for s_i, hh in enumerate(heads):
                            r0 = 64 * (hh % 2)
                            pcol = (bb % 2) * 256 + (s_i % 2) * 128
                            ins = tn.matmul(P7[r0:r0 + 64, pcol:pcol + 128],
                                            ones_bf[0:1, 0:64],
                                            rr[0:1, bb * 512 + s_i * 128:bb * 512 + (s_i + 1) * 128],
                                            start=True, stop=True)
                        inc(ins, s_pe, 1)
                    bc_m[(tq, bb)] = val(s_pe)

                    if attn_sub < 7:
                        avmult_m[(tq, bb)] = val(s_dv)
                        continue

                    @block.scalar
                    def _(s, tq=tq, bb=bb, avbank=avbank):
                        s.wait_ge(s_pe, av_m[(tq, bb)])
                        key1 = (tq, bb - 1) if bb > 0 else (tq - 1, 3)
                        if key1 in avmult_m:
                            s.wait_ge(s_dv, avmult_m[key1])
                        inc(s.copy(avst[:, 0:256], P[avbank][:, 0:256]), s_sc, 1)
                    m_avevac = val(s_sc)
                    pfree[avbank] = (s_sc, m_avevac)

                    if attn_sub < 8:
                        avmult_m[(tq, bb)] = val(s_dv)
                        continue

                    @block.vector
                    def _(v, tq=tq, bb=bb, tcol=tcol, heads=heads, m_avevac=m_avevac,
                          m_gl_prev=m_gl_prev, m_mix_prev=m_mix_prev):
                        v.wait_ge(s_pe, bc_m[(tq, bb)])
                        v.wait_ge(s_sc, m_avevac)
                        if m_gl_prev:
                            v.wait_ge(s_pe, m_gl_prev)
                        if m_mix_prev:
                            v.wait_ge(s_dv, m_mix_prev)
                        ins = None
                        for s_i, hh in enumerate(heads):
                            r0 = 64 * (hh % 2)
                            pcol = (bb % 2) * 256 + (s_i % 2) * 128
                            ins = v.tensor_tensor(local_n[r0:r0 + 64, hh // 2, tcol:tcol + 128],
                                                  avst[r0:r0 + 64,
                                                       (s_i % 2) * 128:(s_i % 2) * 128 + 128],
                                                  P7[r0:r0 + 64, pcol:pcol + 128], ALU.mult)
                        inc(ins, s_dv, 1)
                    avmult_m[(tq, bb)] = val(s_dv)

                if not attn_lat:
                    continue
                lm = {}

                @block.tensor
                def _(tn, tq=tq):
                    tn.wait_ge(*pfree[0])
                    tn.wait_ge(s_dv, m_lat_prep)
                    ins = None
                    for h in range(4):
                        ins = tn.matmul(P[0][0:64, h * 128:(h + 1) * 128], rkT[:, h, :],
                                        rq_sb[:, h, half * 1024 + tq * 128:half * 1024 + (tq + 1) * 128],
                                        start=True, stop=True)
                    inc(ins, s_pe, 1)
                lm["lsc"] = val(s_pe)

                @block.scalar
                def _(s, tq=tq, lm=lm):
                    s.wait_ge(s_pe, lm["lsc"])
                    if (tq - 1) in lat_ms:
                        s.wait_ge(s_pe, lat_ms[tq - 1]["av"])
                    inc(s.activation(ptl[0:64, :], P[0][0:64, :], AF.Exp,
                                     bias=BSH[0:64, :], scale=0.125), s_sc, 1)
                lm["exp"] = val(s_sc)
                pfree[0] = (s_sc, lm["exp"])

                @block.vector
                def _(v, tq=tq, lm=lm):
                    v.wait_ge(s_sc, lm["exp"])
                    inc(v.tensor_scalar(ptl[0:64, :], ptl[0:64, :],
                                        latmask[:, half * 8 + tq:half * 8 + tq + 1], None,
                                        ALU.mult), s_dv, 1)
                lm["mask"] = val(s_dv)

                @block.tensor
                def _(tn, tq=tq, lm=lm):
                    tn.wait_ge(s_dv, lm["mask"])
                    tn.wait_ge(*pfree[1])
                    tn.wait_ge(*pfree[3])
                    if (tq - 1) in lat_ms:
                        tn.wait_ge(s_dv, lat_ms[tq - 1]["rec"])
                    for h in range(4):
                        r0 = 64 * (h % 2)
                        sl = (256 if h % 2 else 0) + (h // 2) * 128
                        tn.matmul(P[1][r0:r0 + 64, sl:sl + 128], rv_aug[0:64, h, 0:64],
                                  ptl[0:64, h * 128:(h + 1) * 128], start=True, stop=True)
                    ins = None
                    for h in range(4):
                        ins = tn.matmul(P[3][0:1, h * 128:(h + 1) * 128], otcol[0:65, 0:1],
                                        ptl[0:65, h * 128:(h + 1) * 128], start=True, stop=True)
                    inc(ins, s_pe, 1)
                lm["av"] = val(s_pe)

                @block.vector
                def _(v, tq=tq, lm=lm):
                    v.wait_ge(s_pe, lm["av"])
                    if (tq - 1) in lat_ms:
                        v.wait_ge(s_pe, lat_ms[tq - 1]["bc"])
                    with nc.allow_low_precision(reason="softmax recip as one bf16 factor"):
                        inc(v.reciprocal(rrL[0:1, 0:512], P[3][0:1, :]), s_dv, 1)
                lm["rec"] = val(s_dv)
                pfree[3] = (s_dv, lm["rec"])

                @block.tensor
                def _(tn, tq=tq, lm=lm):
                    tn.wait_ge(s_dv, lm["rec"])
                    tn.wait_ge(s_dv, avmult_m[(tq, 2)])
                    if (tq - 1) in lat_ms:
                        tn.wait_ge(s_dv, lat_ms[tq - 1]["mult"])
                    ins = None
                    for h in range(4):
                        r0 = 64 * (h % 2)
                        pcol = (h // 2) * 128
                        ins = tn.matmul(P7[r0:r0 + 64, pcol:pcol + 128], ones_bf[0:1, 0:64],
                                        rrL[0:1, h * 128:(h + 1) * 128], start=True, stop=True)
                    inc(ins, s_pe, 1)
                lm["bc"] = val(s_pe)

                @block.scalar
                def _(s, tq=tq, lm=lm):
                    s.wait_ge(s_pe, lm["av"])
                    if (tq - 1) in lat_ms:
                        s.wait_ge(s_dv, lat_ms[tq - 1]["mult"])
                    s.copy(latst[0:64, 0:256], P[1][0:64, 0:256])
                    inc(s.copy(latst[64:128, 0:256], P[1][64:128, 256:512]), s_sc, 1)
                lm["evac"] = val(s_sc)
                pfree[1] = (s_sc, lm["evac"])

                @block.vector
                def _(v, tq=tq, tcol=tcol, lm=lm):
                    v.wait_ge(s_pe, lm["bc"])
                    v.wait_ge(s_sc, lm["evac"])
                    ins = None
                    for h in range(4):
                        r0 = 64 * (h % 2)
                        ins = v.tensor_tensor(latout[r0:r0 + 64, h // 2, tcol:tcol + 128],
                                              latst[r0:r0 + 64, (h // 2) * 128:(h // 2) * 128 + 128],
                                              P7[r0:r0 + 64, (h // 2) * 128:(h // 2) * 128 + 128],
                                              ALU.mult)
                    inc(ins, s_dv, 1)
                lm["mult"] = val(s_dv)
                lat_ms[tq] = lm

                if tq % 2 == 1 and attn_grp:
                    grp_rout_mm = {}
                    for oc in range(8):
                        bank = 2 + oc % 2

                        @block.tensor
                        def _(tn, oc=oc, bank=bank, tq=tq):
                            tn.wait_ge(s_dv, lat_ms[tq]["mult"])
                            tn.wait_ge(s_dv, lat_ms[tq - 1]["mult"])
                            tn.wait_ge(*pfree[bank])
                            for dc in range(2):
                                ins = tn.matmul(P[bank][:, 0:256],
                                                rout_sb[:, oc, dc * 128:(dc + 1) * 128],
                                                latout[:, dc, 0:256], start=(dc == 0), stop=(dc == 1))
                            inc(ins, s_pe, 1)
                        mmv = val(s_pe)

                        @block.scalar
                        def _(s, oc=oc, bank=bank, mmv=mmv, m_mix_prev=m_mix_prev):
                            s.wait_ge(s_pe, mmv)
                            if m_mix_prev:
                                s.wait_ge(s_dv, m_mix_prev)
                            m = inc(s.activation(remote_n[:, oc, :], P[bank][:, 0:256],
                                                 AF.Identity, bias=bcol[:, 24 + oc:25 + oc],
                                                 scale=1.0), s_sc, 1)
                            pfree[bank] = (s_sc, m)
                    m_remote = val(s_sc)

                    @block.tensor
                    def _(tn, tq=tq, m_remote=m_remote):
                        tn.wait_ge(s_sc, m_remote)
                        tn.wait_ge(s_dv, avmult_m[(tq, 3)])
                        tn.wait_ge(*pfree[5])
                        tn.wait_ge(s_dv, lat_ms[tq]["rec"])
                        for dc in range(8):
                            tn.matmul(P[5][0:16, 0:256], bdg[:, dc, 0:16], local_n[:, dc, :],
                                      start=(dc == 0), stop=False)
                        for dc in range(8):
                            ins = tn.matmul(P[5][0:16, 0:256], bdg[:, dc, 16:32], remote_n[:, dc, :],
                                            start=False, stop=(dc == 7))
                        inc(ins, s_pe, 1)
                    m_gl = val(s_pe)
                    m_gl_prev = m_gl

                    @block.scalar
                    def _(s, m_gl=m_gl):
                        s.wait_ge(s_pe, m_gl)
                        inc(s.activation(gtmp[:, :], P[5][0:16, 0:256], AF.Exp,
                                         bias=ZCOL[0:16, :], scale=-1.0), s_sc, 1)
                    m_gexp = val(s_sc)
                    pfree[5] = (s_sc, m_gexp)

                    @block.vector
                    def _(v, m_gexp=m_gexp):
                        v.wait_ge(s_sc, m_gexp)
                        inc(v.tensor_scalar(gtmp[:, :], gtmp[:, :], 1.0, None, ALU.add), s_dv, 1)
                        v.wait_ge(s_dv, val(s_dv))
                        inc(v.reciprocal(gtmp[:, :], gtmp[:, :]), s_dv, 1)
                        v.wait_ge(s_dv, val(s_dv))
                        inc(v.tensor_copy(gsb[:, :], gtmp[:, :]), s_dv, 1)
                    m_gate = val(s_dv)

                    m_mix_rounds = []
                    for rnd in range(2):
                        @block.tensor
                        def _(tn, rnd=rnd, m_gate=m_gate):
                            tn.wait_ge(s_dv, m_gate)
                            tn.wait_ge(*pfree[0])
                            tn.wait_ge(*pfree[1])
                            for k in range(4):
                                dc = rnd * 4 + k
                                bank = k % 2
                                ins = tn.matmul(P[bank][:, (k // 2) * 256:(k // 2) * 256 + 256],
                                                ebc[:, dc, :], gsb[:, :], start=True, stop=True)
                            inc(ins, s_pe, 1)
                        m_gbc = val(s_pe)

                        @block.vector
                        def _(v, rnd=rnd, m_gbc=m_gbc, m_oww_prev=m_oww_prev):
                            v.wait_ge(s_pe, m_gbc)
                            if m_oww_prev:
                                v.wait_ge(s_pe, m_oww_prev)
                            for k in range(4):
                                dc = rnd * 4 + k
                                bank = k % 2
                                v.tensor_tensor(dtmp[:, :], local_n[:, dc, :],
                                                remote_n[:, dc, :], ALU.subtract)
                                v.tensor_tensor(dtmp[:, :], dtmp[:, :],
                                                P[bank][:, (k // 2) * 256:(k // 2) * 256 + 256],
                                                ALU.mult)
                                ins = v.tensor_tensor(mixed[:, dc, :], dtmp[:, :],
                                                      remote_n[:, dc, :], ALU.add)
                            inc(ins, s_dv, 1)
                        m_mix_rounds.append(val(s_dv))
                        pfree[0] = (s_dv, m_mix_rounds[-1])
                        pfree[1] = (s_dv, m_mix_rounds[-1])
                    m_mix_done = m_mix_rounds[-1]
                    m_mix_prev = m_mix_done

                    for tt in range(2):
                        tloc = (tq - 1) + tt
                        bank0, bank1 = (2, 3) if tt == 0 else (4, 5)

                        @block.tensor
                        def _(tn, tt=tt, bank0=bank0, bank1=bank1, m_mix_done=m_mix_done):
                            tn.wait_ge(s_dv, m_mix_done)
                            tn.wait_ge(s_gp, m_ow)
                            tn.wait_ge(*pfree[bank0])
                            tn.wait_ge(*pfree[bank1])
                            for dc in range(8):
                                tn.matmul(P[bank0][:, :], mixed[:, dc, tt * 128:(tt + 1) * 128],
                                          ow_sb[:, dc, 0:512], start=(dc == 0), stop=False)
                            tn.matmul(P[bank0][:, :], ones_bf[0:1, 0:128],
                                      bvob[0:1, 1024:1536], start=False, stop=False)
                            tn.matmul(P[bank0][:, :], ones_bf[0:1, 0:128],
                                      bvob_lo[0:1, 1024:1536], start=False, stop=True)
                            for dc in range(8):
                                tn.matmul(P[bank1][:, :], mixed[:, dc, tt * 128:(tt + 1) * 128],
                                          ow_sb[:, dc, 512:1024], start=(dc == 0), stop=False)
                            tn.matmul(P[bank1][:, :], ones_bf[0:1, 0:128],
                                      bvob[0:1, 1536:2048], start=False, stop=False)
                            inc(tn.matmul(P[bank1][:, :], ones_bf[0:1, 0:128],
                                          bvob_lo[0:1, 1536:2048], start=False, stop=True), s_pe, 1)
                        m_oww = val(s_pe)

                        @block.scalar
                        def _(s, tloc=tloc, bank0=bank0, bank1=bank1, m_oww=m_oww):
                            s.wait_ge(s_pe, m_oww)
                            if m_stm_free[tloc][1] > 0:
                                s.wait_ge(*m_stm_free[tloc])
                            s.copy(s_tm[:, tloc, 0:512], P[bank0][:, :])
                            m = inc(s.copy(s_tm[:, tloc, 512:1024], P[bank1][:, :]), s_sc, 1)
                            pfree[bank0] = (s_sc, m)
                            pfree[bank1] = (s_sc, m)
                    m_oww_prev = m_oww

                    if half == 0 and tq == 1:
                        if dbg_out("g0_local", [128, 8, 256]) is not None:
                            @block.gpsimd
                            def _(g, m_mix_done=m_mix_done):
                                g.wait_ge(s_dv, m_mix_done)
                                inc(g.dma_start(out=dbg["g0_local"], in_=local_n[:, :, :]), s_gp, 16)
                                inc(g.dma_start(out=dbg["g0_remote"], in_=remote_n[:, :, :]), s_gp, 16)
                                inc(g.dma_start(out=dbg["g0_mixed"], in_=mixed[:, :, :]), s_gp, 16)
                                inc(g.dma_start(out=dbg["g0_gate"], in_=gsb[:, :]), s_gp, 16)

            if attn_tq < 8 or not attn_lat or not attn_grp or attn_sub < 8:
                _cut()
                scr.close()
                att.close()
                return nc, dbg

            m_attn_sc = val(s_sc)
            m_attn_dv = val(s_dv)

            if half == 0 and dbg_out("s_tm", [128, 8, 1024]) is not None:
                @block.sync
                def _(sy):
                    sy.wait_ge(s_sc, m_attn_sc)
                    inc(sy.dma_start(out=dbg["s_tm"], in_=s_tm[:, :, :]), s_st, 16)
            if half == 0 and dbg_out("local_n", [128, 8, 256]) is not None:
                @block.gpsimd
                def _(g):
                    g.wait_ge(s_dv, m_attn_dv)
                    inc(g.dma_start(out=dbg["local_n"], in_=local_n[:, :, :]), s_gp, 16)
            if half == 0 and dbg_out("remote_n", [128, 8, 256]) is not None:
                @block.gpsimd
                def _(g):
                    g.wait_ge(s_dv, m_attn_dv)
                    inc(g.dma_start(out=dbg["remote_n"], in_=remote_n[:, :, :]), s_gp, 16)

            scr.close()
            att.close()

            if "ffn" not in phases:
                arena_free = [(s_sc, m_attn_sc), (s_dv, m_attn_dv)]
                continue

            ffn = ExitStack()
            h_sb = ffn.enter_context(nc.sbuf_tensor(f"h_sb{half}", [128, 24, 1024], BF16))
            wd_sb = ffn.enter_context(nc.sbuf_tensor(f"wd_sb{half}", [128, 24, 1024], BF16))
            wg_buf = ffn.enter_context(nc.sbuf_tensor(f"wg_buf{half}", [128, 2, 1024], BF16))
            wu_buf = ffn.enter_context(nc.sbuf_tensor(f"wu_buf{half}", [128, 2, 1024], BF16))
            sg_buf = ffn.enter_context(nc.sbuf_tensor(f"sg_buf{half}", [128, 2, 512], BF16))
            g_buf = ffn.enter_context(nc.sbuf_tensor(f"g_buf{half}", [128, 2, 512], F32))
            xre = ffn.enter_context(nc.sbuf_tensor(f"xre{half}", [128, 1024], F32))
            PF = [ffn.enter_context(nc.psum_tensor(f"PF{i}_{half}", [128, 512], F32))
                  for i in range(2)]
            pfree[6] = (s_pe, 0)
            pfree[7] = (s_pe, 0)

            attn_done = [(s_sc, m_attn_sc), (s_dv, m_attn_dv)]

            ld_gu = [0] * 24
            ld_wd = [0] * 24
            mm_gu = [0] * 24

            @block.gpsimd
            def _(g):
                for sem, vv in attn_done:
                    g.wait_ge(sem, vv)
                for j in range(2):
                    inc(g.dma_start(out=wg_buf[:, j, :], in_=wgt_in[j]), s_gp, 16)
                    ld_gu[j] = inc(g.dma_start(out=wu_buf[:, j, :], in_=wut_in[j]), s_gp, 16)

            for hc in range(24):
                bsel = hc % 2
                if bsel == 0:
                    pg = [P[0], P[1], P[2], P[3]]
                    bk = [0, 1, 2, 3]
                else:
                    pg = [P[4], P[5], PF[0], PF[1]]
                    bk = [4, 5, 6, 7]

                @block.tensor
                def _(tn, hc=hc, bsel=bsel, pg=pg, bk=bk):
                    tn.wait_ge(s_gp, ld_gu[hc])
                    for b in bk:
                        tn.wait_ge(*pfree[b])
                    for gi in range(2):
                        for dc in range(8):
                            tn.matmul(pg[gi][:, :], wg_buf[:, bsel, dc * 128:(dc + 1) * 128],
                                      xn_fm[:, dc, qbase + gi * 512:qbase + (gi + 1) * 512],
                                      start=(dc == 0), stop=(dc == 7))
                        for dc in range(8):
                            ins = tn.matmul(pg[2 + gi][:, :], wu_buf[:, bsel, dc * 128:(dc + 1) * 128],
                                            xn_fm[:, dc, qbase + gi * 512:qbase + (gi + 1) * 512],
                                            start=(dc == 0), stop=(dc == 7))
                    inc(ins, s_pe, 1)
                mm_gu[hc] = val(s_pe)

                @block.scalar
                def _(s, hc=hc, pg=pg):
                    s.wait_ge(s_pe, mm_gu[hc])
                    if hc >= 1:
                        pass
                    s.activation(sg_buf[:, 0, :], pg[0][:, :], AF.Silu, scale=SILU1)
                    inc(s.activation(sg_buf[:, 1, :], pg[1][:, :], AF.Silu, scale=SILU1),
                        s_sc, 1)
                m_silu = val(s_sc)
                pfree[bk[0]] = (s_sc, m_silu)
                pfree[bk[1]] = (s_sc, m_silu)

                @block.vector
                def _(v, hc=hc, pg=pg, m_silu=m_silu):
                    v.wait_ge(s_sc, m_silu)
                    v.tensor_tensor(h_sb[:, hc, 0:512], sg_buf[:, 0, :], pg[2][:, :], ALU.mult)
                    inc(v.tensor_tensor(h_sb[:, hc, 512:1024], sg_buf[:, 1, :], pg[3][:, :],
                                        ALU.mult), s_dv, 1)
                m_h = val(s_dv)
                pfree[bk[2]] = (s_dv, m_h)
                pfree[bk[3]] = (s_dv, m_h)

                @block.gpsimd
                def _(g, hc=hc):
                    if hc + 2 < 24:
                        g.wait_ge(s_pe, mm_gu[hc])
                        inc(g.dma_start(out=wg_buf[:, hc % 2, :], in_=wgt_in[hc + 2]), s_gp, 16)
                        ld_gu[hc + 2] = inc(g.dma_start(out=wu_buf[:, hc % 2, :],
                                                        in_=wut_in[hc + 2]), s_gp, 16)
                    ld_wd[hc] = inc(g.dma_start(out=wd_sb[:, hc, :], in_=wdt_in[hc]), s_gp, 16)
            m_h_all = val(s_dv)
            m_wd_all = ld_wd[23]

            m_epi_prev = 0
            for tt in range(8):
                bank0, bank1 = (0, 1) if tt % 2 == 0 else (2, 3)

                @block.sync
                def _(sy, tt=tt, m_epi_prev=m_epi_prev):
                    if tt == 0:
                        for sem, vv in attn_done:
                            sy.wait_ge(sem, vv)
                    if m_epi_prev:
                        sy.wait_ge(s_dv, m_epi_prev)
                    inc(sy.dma_start(out=xre[:, :],
                                     in_=x_in[qbase + tt * 128:qbase + (tt + 1) * 128, :]),
                        s_sy, 16)
                m_xre = val(s_sy)

                @block.tensor
                def _(tn, tt=tt, bank0=bank0, bank1=bank1):
                    tn.wait_ge(s_dv, m_h_all)
                    tn.wait_ge(s_gp, m_wd_all)
                    tn.wait_ge(*pfree[bank0])
                    tn.wait_ge(*pfree[bank1])
                    for hc in range(24):
                        tn.matmul(P[bank0][:, :], h_sb[:, hc, tt * 128:(tt + 1) * 128],
                                  wd_sb[:, hc, 0:512], start=(hc == 0), stop=(hc == 23))
                    for hc in range(24):
                        ins = tn.matmul(P[bank1][:, :], h_sb[:, hc, tt * 128:(tt + 1) * 128],
                                        wd_sb[:, hc, 512:1024], start=(hc == 0), stop=(hc == 23))
                    inc(ins, s_pe, 1)
                m_down = val(s_pe)

                @block.scalar
                def _(s, bank0=bank0, bank1=bank1, m_down=m_down):
                    s.wait_ge(s_pe, m_down)
                    s.activation(g_buf[:, 0, :], P[bank0][:, :], AF.Silu, scale=SILU2)
                    inc(s.activation(g_buf[:, 1, :], P[bank1][:, :], AF.Silu, scale=SILU2),
                        s_sc, 1)
                m_g = val(s_sc)
                pfree[bank0] = (s_sc, m_g)
                pfree[bank1] = (s_sc, m_g)

                @block.vector
                def _(v, tt=tt, m_g=m_g, m_xre=m_xre):
                    v.wait_ge(s_sc, m_g)
                    v.wait_ge(s_sy, m_xre)
                    for gi in range(2):
                        v.tensor_tensor(g_buf[:, gi, :], g_buf[:, gi, :],
                                        s_tm[:, tt, gi * 512:(gi + 1) * 512], ALU.mult)
                    inc(v.tensor_tensor(xre[:, :], xre[:, :],
                                        g_buf[:, :, :].rearrange("p a b -> p (a b)"), ALU.add),
                        s_dv, 1)
                m_epi = val(s_dv)
                m_epi_prev = m_epi
                m_stm_free[tt] = (s_dv, m_epi)

                @block.sync
                def _(sy, tt=tt, m_epi=m_epi):
                    sy.wait_ge(s_dv, m_epi)
                    inc(sy.dma_start(
                        out=y_out[half * 1024 + tt * 128:half * 1024 + (tt + 1) * 128, :],
                        in_=xre[:, :]), s_st, 16)

            ffn.close()
            arena_free = [(s_dv, val(s_dv)), (s_sc, val(s_sc)), (s_pe, val(s_pe))]

        @block.sync
        def _(sy):
            sy.wait_ge(s_st, val(s_st))
            sy.wait_ge(s_gp, val(s_gp))

    return nc, dbg


def _prep_inputs(inputs):
    x = np.asarray(inputs["L0_attn_x"], np.float32).reshape(S, D)
    qkv_w = np.asarray(inputs["L0_attn_qkv_w"], np.float32)
    qkv_b = np.asarray(inputs["L0_attn_qkv_b"], np.float32)
    rq_w = np.asarray(inputs["L0_attn_rq_w"], np.float32)
    rq_b = np.asarray(inputs["L0_attn_rq_b"], np.float32)
    rkv_w = np.asarray(inputs["L0_attn_rkv_w"], np.float32)
    rkv_b = np.asarray(inputs["L0_attn_rkv_b"], np.float32)
    rout_w = np.asarray(inputs["L0_attn_rout_w"], np.float32)
    rout_b = np.asarray(inputs["L0_attn_rout_b"], np.float32)
    out_w = np.asarray(inputs["L0_attn_out_w"], np.float32)
    out_b = np.asarray(inputs["L0_attn_out_b"], np.float32)
    gl_w = np.asarray(inputs["L0_attn_gl_w"], np.float32)
    gr_w = np.asarray(inputs["L0_attn_gr_w"], np.float32)
    w_gate = np.asarray(inputs["L0_ffn_w_gate"], np.int32)
    w_up = np.asarray(inputs["L0_ffn_w_up"], np.int32)
    w_down = np.asarray(inputs["L0_ffn_w_down"], np.int32)

    x_pad = np.concatenate([np.zeros((HALO, D), np.float32), x], 0)

    qkvw_qk = np.ascontiguousarray(
        qkv_w[:, 0:2048].reshape(8, 128, 16, 128).transpose(2, 1, 0, 3).reshape(16, 128, 1024))
    qkvw_v = np.ascontiguousarray(qkv_w[:, 2048:3072].reshape(8, 128, 1024))
    outw_r = np.ascontiguousarray(out_w.reshape(8, 128, 1024))
    rqw_r = np.ascontiguousarray(
        rq_w.reshape(8, 128, 4, 64).transpose(2, 1, 0, 3).reshape(4, 128, 512))
    rkvw_r = np.ascontiguousarray(rkv_w.reshape(8, 128, 512))
    routw_r = np.ascontiguousarray(
        rout_w.reshape(2, 128, 8, 128).transpose(2, 1, 0, 3).reshape(8, 128, 256))
    wgt_r = np.ascontiguousarray(
        w_gate.reshape(24, 128, 8, 128).transpose(0, 3, 2, 1).reshape(24, 128, 1024))
    wut_r = np.ascontiguousarray(
        w_up.reshape(24, 128, 8, 128).transpose(0, 3, 2, 1).reshape(24, 128, 1024))
    wdt_r = np.ascontiguousarray(w_down.T.reshape(24, 128, 1024))

    bcolv = np.zeros((128, 40), np.float32)
    bcolv[:, 0:24] = qkv_b.reshape(24, 128).T
    bcolv[:, 24:32] = rout_b.reshape(8, 128).T
    bcolv[0:64, 32:36] = rq_b.reshape(4, 64).T
    browv = np.zeros((1, 2560), np.float32)
    browv[0, 0:1024] = qkv_b[2048:3072]
    browv[0, 1024:2048] = out_b
    browv[0, 2048:2560] = rkv_b

    bdgv = np.zeros((1024, 32), np.float32)
    for h in range(16):
        bdgv[h * 64:(h + 1) * 64, h] = gl_w[:, 0]
        bdgv[h * 64:(h + 1) * 64, 16 + h] = gr_w[:, 0]
    bdgv = np.ascontiguousarray(bdgv.reshape(8, 128, 32).transpose(1, 0, 2))

    ebcv = np.zeros((16, 8, 128), ml_dtypes.bfloat16)
    for dc in range(8):
        for pp in range(128):
            ebcv[(dc * 128 + pp) // 64, dc, pp] = 1.0

    identv = np.eye(128, dtype=ml_dtypes.bfloat16)

    r = np.arange(128)
    c = np.arange(128)
    m0 = (c[:, None] >= r[None, :]).astype(np.float32)
    m0e = m0.copy()
    m0e[0, 0] = 0.0
    m1 = (c[:, None] < r[None, :]).astype(np.float32)
    maskAv = np.tile(m0e, (1, 4)).astype(ml_dtypes.bfloat16)
    maskBv = np.tile(m0, (1, 4)).astype(ml_dtypes.bfloat16)
    maskCv = np.tile(m1, (1, 4)).astype(ml_dtypes.bfloat16)
    maskF0 = np.zeros((128, 512), ml_dtypes.bfloat16)

    shared = dict(qkvw_qk=qkvw_qk, qkvw_v=qkvw_v, outw_in=outw_r, rqw_in=rqw_r,
                  rkvw_in=rkvw_r, routw_in=routw_r, wgt_in=wgt_r, wut_in=wut_r,
                  wdt_in=wdt_r, bcol_in=bcolv, brow_in=browv, bdg_in=bdgv,
                  ebc_in=ebcv, ident_in=identv, identf_in=np.eye(64, dtype=np.float32),
                  maskA_in=maskAv, maskB_in=maskBv,
                  maskC_in=maskCv)
    in_maps = []
    ll = np.arange(64)
    for cidx in range(NCORES):
        m = dict(shared)
        m["x_in"] = np.ascontiguousarray(x_pad[cidx * T:cidx * T + TB])
        m["maskF_in"] = maskF0 if cidx == 0 else maskAv
        lmv = np.zeros((64, 16), np.float32)
        for jj in range(16):
            lmv[:, jj] = (ll < cidx * 8 + jj // 2).astype(np.float32)
        m["latmask_in"] = lmv
        in_maps.append(m)
    return in_maps


def kernel(**inputs):
    global LAST_DEBUG
    in_maps = _prep_inputs(inputs)
    nc, dbg = build(phases=PHASES, debug=DEBUG)
    res = run_bass_kernel_spmd(nc, in_maps, core_ids=list(range(NCORES)))
    LAST_DEBUG = {}
    for name in dbg:
        LAST_DEBUG[name] = [res.results[cc].get("dbg_" + name) for cc in range(NCORES)]
    out = np.concatenate([res.results[cc]["y_out"] for cc in range(NCORES)], axis=0)
    return out.reshape(1, S, D).astype(np.float32)


# version 03
# speedup vs baseline: 1.1052x; 1.1052x over previous
import numpy as np
import ml_dtypes
from contextlib import ExitStack

import concourse.bass as bass
import concourse.mybir as mybir
from concourse.bass_utils import run_bass_kernel_spmd

F32 = mybir.dt.float32
BF16 = mybir.dt.bfloat16
I32 = mybir.dt.int32
AF = mybir.ActivationFunctionType
ALU = mybir.AluOpType
AX = mybir.AxisListType

NCORES = 8
S = 16384
D = 1024
T = 2048
HALO = 128
TB = T + HALO
NT = TB // 128
BSHIFT = 40.0
SILU1 = 1.0 / 127.0
SILU2 = 1.0 / (127.0 * 127.0)
TINY = 1e-30

PHASES = ("ln", "qkv", "attn", "ffn")
DEBUG = ()
LAST_DEBUG = {}


def build(phases=PHASES, debug=(), nc_kwargs=None, stage=99, attn_tq=8, attn_lat=True,
          attn_grp=True, attn_sub=8):
    debug = set(debug)
    nc = bass.Bass("TRN2", target_bir_lowering=False, **(nc_kwargs or {}))

    x_in = nc.dram_tensor("x_in", [TB, D], F32, kind="ExternalInput").ap()
    qkvw_qk = nc.dram_tensor("qkvw_qk", [16, 128, 1024], F32, kind="ExternalInput").ap()
    qkvw_v = nc.dram_tensor("qkvw_v", [8, 128, 1024], F32, kind="ExternalInput").ap()
    outw_in = nc.dram_tensor("outw_in", [8, 128, 1024], F32, kind="ExternalInput").ap()
    rqw_in = nc.dram_tensor("rqw_in", [4, 128, 512], F32, kind="ExternalInput").ap()
    rkvw_in = nc.dram_tensor("rkvw_in", [8, 128, 512], F32, kind="ExternalInput").ap()
    routw_in = nc.dram_tensor("routw_in", [8, 128, 256], F32, kind="ExternalInput").ap()
    wgt_in = nc.dram_tensor("wgt_in", [24, 128, 1024], I32, kind="ExternalInput").ap()
    wut_in = nc.dram_tensor("wut_in", [24, 128, 1024], I32, kind="ExternalInput").ap()
    wdt_in = nc.dram_tensor("wdt_in", [24, 128, 1024], I32, kind="ExternalInput").ap()
    bcol_in = nc.dram_tensor("bcol_in", [128, 40], F32, kind="ExternalInput").ap()
    brow_in = nc.dram_tensor("brow_in", [1, 2560], F32, kind="ExternalInput").ap()
    bdg_in = nc.dram_tensor("bdg_in", [128, 8, 32], F32, kind="ExternalInput").ap()
    ebc_in = nc.dram_tensor("ebc_in", [16, 8, 128], BF16, kind="ExternalInput").ap()
    ident_in = nc.dram_tensor("ident_in", [128, 128], BF16, kind="ExternalInput").ap()
    identf_in = nc.dram_tensor("identf_in", [64, 64], F32, kind="ExternalInput").ap()
    maskA_in = nc.dram_tensor("maskA_in", [128, 512], BF16, kind="ExternalInput").ap()
    maskB_in = nc.dram_tensor("maskB_in", [128, 512], BF16, kind="ExternalInput").ap()
    maskC_in = nc.dram_tensor("maskC_in", [128, 512], BF16, kind="ExternalInput").ap()
    maskF_in = nc.dram_tensor("maskF_in", [128, 512], BF16, kind="ExternalInput").ap()
    latmask_in = nc.dram_tensor("latmask_in", [64, 16], F32, kind="ExternalInput").ap()

    y_out = nc.dram_tensor("y_out", [T, D], F32, kind="ExternalOutput").ap()
    gin = nc.dram_tensor("gin", [8, 512], F32)
    gout = nc.dram_tensor("gout", [64, 512], F32)

    dbg = {}

    def dbg_out(name, shape):
        if name == "g0_local":
            for nm, shp in [("g0_local", [128, 8, 256]), ("g0_remote", [128, 8, 256]),
                            ("g0_mixed", [128, 8, 256]), ("g0_gate", [16, 256])]:
                if nm in debug and nm not in dbg:
                    dbg[nm] = nc.dram_tensor("dbg_" + nm, shp, F32, kind="ExternalOutput").ap()
            return dbg.get(name)
        if name in debug and name not in dbg:
            dbg[name] = nc.dram_tensor("dbg_" + name, list(shape), F32, kind="ExternalOutput").ap()
        return dbg.get(name)

    ctr = {}

    def inc(instr, sem, n=1):
        instr.then_inc(sem, n)
        ctr[sem.name] = ctr.get(sem.name, 0) + n
        return ctr[sem.name]

    def val(sem):
        return ctr.get(sem.name, 0)

    with ExitStack() as top:
        e = top.enter_context

        xn_fm = e(nc.sbuf_tensor("xn_fm", [128, 8, TB], BF16))
        s_tm = e(nc.sbuf_tensor("s_tm", [128, 8, 1024], F32))
        rq_sb = e(nc.sbuf_tensor("rq_sb", [64, 4, T], BF16))
        rkv_all = e(nc.sbuf_tensor("rkv_all", [64, 512], F32))
        ident = e(nc.sbuf_tensor("ident", [128, 128], BF16))
        identf = e(nc.sbuf_tensor("identf", [64, 64], F32))
        ones_bf = e(nc.sbuf_tensor("ones_bf", [128, 512], BF16))
        tiny_bf = e(nc.sbuf_tensor("tiny_bf", [128, 1], BF16))
        maskA = e(nc.sbuf_tensor("maskA", [128, 512], BF16))
        maskB = e(nc.sbuf_tensor("maskB", [128, 512], BF16))
        maskC = e(nc.sbuf_tensor("maskC", [128, 512], BF16))
        maskF = e(nc.sbuf_tensor("maskF", [128, 512], BF16))
        latmask = e(nc.sbuf_tensor("latmask", [64, 16], F32))
        bdg = e(nc.sbuf_tensor("bdg", [128, 8, 32], BF16))
        bcol = e(nc.sbuf_tensor("bcol", [128, 42], F32))

        P = [e(nc.psum_tensor(f"P{i}", [128, 512], F32)) for i in range(6)]

        s_sy = e(nc.semaphore("s_sy"))
        s_gp = e(nc.semaphore("s_gp"))
        s_cc = e(nc.semaphore("s_cc"))
        s_st = e(nc.semaphore("s_st"))
        s_pe = e(nc.semaphore("s_pe"))
        s_dv = e(nc.semaphore("s_dv"))
        s_sc = e(nc.semaphore("s_sc"))

        block = e(nc.Block())
        tn = nc.tensor
        v = nc.vector
        s = nc.scalar
        g = nc.gpsimd
        sy = nc.sync

        BSH = bcol[:, 40:41]
        EPS = bcol[:, 41:42]
        ZCOL = bcol[:, 36:37]

        pfree = {i: (s_pe, 0) for i in range(8)}

        def _cut():
            for sem in (s_sy, s_gp, s_dv, s_sc, s_pe, s_st, s_cc):
                if val(sem):
                    sy.wait_ge(sem, val(sem))


        inc(sy.dma_start(out=ident[:, :], in_=ident_in), s_sy, 16)
        inc(sy.dma_start(out=identf[:, :], in_=identf_in), s_sy, 16)
        inc(sy.dma_start(out=maskA[:, :], in_=maskA_in), s_sy, 16)
        inc(sy.dma_start(out=maskB[:, :], in_=maskB_in), s_sy, 16)
        inc(sy.dma_start(out=maskC[:, :], in_=maskC_in), s_sy, 16)
        inc(sy.dma_start(out=maskF[:, :], in_=maskF_in), s_sy, 16)
        inc(sy.dma_start(out=latmask[:, :], in_=latmask_in), s_sy, 16)
        inc(sy.dma_start(out=bcol[:, 0:40], in_=bcol_in), s_sy, 16)
        m_consts = val(s_sy)

        inc(g.dma_start(out=bdg[:, :, :], in_=bdg_in), s_gp, 16)
        m_consts_gp = val(s_gp)

        v.memset(ones_bf[:, :], 1.0)
        v.memset(tiny_bf[:, :], TINY)
        v.memset(bcol[:, 40:41], -BSHIFT)
        inc(v.memset(bcol[:, 41:42], 1e-5), s_dv, 1)
        m_memsets = val(s_dv)

        ln = ExitStack()
        x_sb = ln.enter_context(nc.sbuf_tensor("x_sb", [128, NT, 1024], F32))
        xn_tm = ln.enter_context(nc.sbuf_tensor("xn_tm", [128, 2, 1024], BF16))
        stats_all = ln.enter_context(nc.sbuf_tensor("stats_all", [128, NT, 12], F32))
        mv_all = ln.enter_context(nc.sbuf_tensor("mv_all", [128, 2 * NT], F32))
        rstd_all = ln.enter_context(nc.sbuf_tensor("rstd_all", [128, NT], F32))
        pooled = ln.enter_context(nc.sbuf_tensor("pooled", [128, 8, 8], F32))
        pooled_bf = ln.enter_context(nc.sbuf_tensor("pooled_bf", [128, 8, 8], BF16))
        rqw_sb = ln.enter_context(nc.sbuf_tensor("rqw_sb", [128, 4, 512], BF16))
        rkvw_sb = ln.enter_context(nc.sbuf_tensor("rkvw_sb", [128, 8, 512], BF16))
        rkv_own = ln.enter_context(nc.sbuf_tensor("rkv_own", [8, 512], F32))
        brkv_f = ln.enter_context(nc.sbuf_tensor("brkv_f", [1, 512], F32))
        brkv = ln.enter_context(nc.sbuf_tensor("brkv", [1, 512], BF16))
        brkv_lo = ln.enter_context(nc.sbuf_tensor("brkv_lo", [1, 512], BF16))
        PT = [ln.enter_context(nc.psum_tensor(f"PT{i}", [128, 512], F32)) for i in range(2)]

        sy.wait_ge(s_dv, m_memsets)
        for t in range(NT):
            inc(sy.dma_start(out=x_sb[:, t, :], in_=x_in[t * 128:(t + 1) * 128, :]), s_sy, 16)
        m_xload0 = m_consts

        for h in range(4):
            inc(g.dma_start(out=rqw_sb[:, h, :], in_=rqw_in[h]), s_gp, 16)
        for dc in range(8):
            inc(g.dma_start(out=rkvw_sb[:, dc, :], in_=rkvw_in[dc]), s_gp, 16)
        inc(g.dma_start(out=brkv_f[:, :], in_=brow_in[0:1, 2048:2560]), s_gp, 16)
        m_smallw = val(s_gp)

        v.wait_ge(s_gp, m_smallw)
        inc(v.tensor_copy(brkv[:, :], brkv_f[:, :]), s_dv, 1)
        v.wait_ge(s_dv, val(s_dv))
        inc(v.scalar_tensor_tensor(brkv_lo[:, :], brkv_f[:, :], 1.0, brkv[:, :],
                                   ALU.mult, ALU.subtract), s_dv, 1)
        m_bsplit0 = val(s_dv)

        if stage < 2:
            _cut()
            ln.close()
            return nc, dbg

        for t in range(NT):
            v.wait_ge(s_sy, m_xload0 + 16 * (t + 1))
            v.bn_stats(stats_all[:, t, 0:6], x_sb[:, t, 0:512])
            ins = v.bn_stats(stats_all[:, t, 6:12], x_sb[:, t, 512:1024])
        inc(ins, s_dv, 1)
        v.wait_ge(s_dv, val(s_dv))
        for t in range(NT):
            ins = v.bn_aggr(mv_all[:, 2 * t:2 * t + 2], stats_all[:, t, 0:12])
        inc(ins, s_dv, 1)
        m_aggr = val(s_dv)

        s.wait_ge(s_dv, m_aggr)
        s.wait_ge(s_dv, m_memsets)
        inc(s.activation(rstd_all[:, :], mv_all[:, 1:2 * NT:2], AF.Sqrt,
                         bias=EPS, scale=1.0), s_sc, 1)
        m_sqrt = val(s_sc)

        v.wait_ge(s_sc, m_sqrt)
        inc(v.reciprocal(rstd_all[:, :], rstd_all[:, :]), s_dv, 1)
        m_rstd = val(s_dv)

        if stage < 3:
            _cut()
            ln.close()
            return nc, dbg

        apply_m = [0] * NT
        xgrp_m = [[0, 0] for _ in range(NT)]
        evac_m = [[0, 0] for _ in range(NT)]
        for t in range(NT):
            b = t % 2

            if t == 0:
                v.wait_ge(s_dv, m_rstd)
            if t >= 2:
                v.wait_ge(s_pe, xgrp_m[t - 2][1])
            apply_m[t] = inc(
                v.tensor_scalar(xn_tm[:, b, :], x_sb[:, t, :],
                                mv_all[:, 2 * t:2 * t + 1], rstd_all[:, t:t + 1],
                                ALU.subtract, ALU.mult), s_dv, 1)

            tn.wait_ge(s_dv, apply_m[t])
            for gg in range(2):
                if t >= 1:
                    tn.wait_ge(s_dv, evac_m[t - 1][gg])
                for k in range(4):
                    dc = gg * 4 + k
                    ins = tn.matmul(PT[gg][:, k * 128:(k + 1) * 128],
                                    xn_tm[:, b, dc * 128:(dc + 1) * 128], ident[:, :],
                                    start=True, stop=True)
                xgrp_m[t][gg] = inc(ins, s_pe, 1)

            for gg in range(2):
                v.wait_ge(s_pe, xgrp_m[t][gg])
                evac_m[t][gg] = inc(
                    v.tensor_copy(xn_fm[:, 4 * gg:4 * gg + 4, t * 128:(t + 1) * 128],
                                  PT[gg][:, :].rearrange("p (k n) -> p k n", k=4)),
                    s_dv, 1)
        m_ln_done = evac_m[NT - 1][1]

        if dbg_out("xn", [128, 8, 256]) is not None:
            g.wait_ge(s_dv, m_ln_done)
            inc(g.dma_start(out=dbg["xn"], in_=xn_fm[:, :, 0:256]), s_gp, 16)

        if stage < 4:
            _cut()
            ln.close()
            return nc, dbg

        v.wait_ge(s_dv, m_ln_done)
        for dc in range(8):
            for j in range(8):
                ins = v.tensor_reduce(pooled[:, dc, j:j + 1],
                                      xn_fm[:, dc, HALO + 256 * j:HALO + 256 * (j + 1)],
                                      AX.X, ALU.add)
        inc(ins, s_dv, 1)
        v.wait_ge(s_dv, val(s_dv))
        inc(v.tensor_scalar(pooled_bf[:, :, :], pooled[:, :, :], 1.0 / 256.0, None,
                            ALU.mult), s_dv, 1)
        m_pooled = val(s_dv)

        tn.wait_ge(s_dv, m_pooled)
        tn.wait_ge(s_dv, m_memsets)
        tn.wait_ge(s_gp, m_smallw)
        tn.wait_ge(s_gp, m_consts_gp)
        tn.wait_ge(s_dv, m_bsplit0)
        for dc in range(8):
            tn.matmul(P[0][0:8, :], pooled_bf[:, dc, :], rkvw_sb[:, dc, :],
                      start=(dc == 0), stop=False)
        tn.matmul(P[0][0:8, :], ones_bf[0:1, 0:8], brkv[0:1, :],
                  start=False, stop=False)
        inc(tn.matmul(P[0][0:8, :], ones_bf[0:1, 0:8], brkv_lo[0:1, :],
                      start=False, stop=True), s_pe, 1)
        m_rkvmm = val(s_pe)

        v.wait_ge(s_pe, m_rkvmm)
        inc(v.tensor_copy(rkv_own[:, :], P[0][0:8, :]), s_dv, 1)
        m_rkvevac = val(s_dv)
        pfree[0] = (s_dv, m_rkvevac)

        if stage < 5:
            _cut()
            ln.close()
            return nc, dbg

        for h in range(4):
            for gg in range(4):
                bank = (h * 4 + gg) % 2

                tn.wait_ge(*pfree[bank])
                tn.wait_ge(s_dv, m_ln_done)
                for dc in range(8):
                    ins = tn.matmul(P[bank][0:64, :],
                                    rqw_sb[:, h, dc * 64:(dc + 1) * 64],
                                    xn_fm[:, dc, HALO + gg * 512:HALO + (gg + 1) * 512],
                                    start=(dc == 0), stop=(dc == 7))
                inc(ins, s_pe, 1)
                rq_mm = val(s_pe)

                s.wait_ge(s_pe, rq_mm)
                m = inc(s.activation(rq_sb[:, h, gg * 512:(gg + 1) * 512], P[bank][0:64, :],
                                     AF.Identity, bias=bcol[0:64, 32 + h:33 + h], scale=1.0),
                        s_sc, 1)
                pfree[bank] = (s_sc, m)
        m_rq_done = val(s_sc)

        if stage < 6:
            _cut()
            ln.close()
            return nc, dbg

        sy.wait_ge(s_dv, m_rkvevac)
        inc(sy.dma_start(out=gin[:, :], in_=rkv_own[:, :]), s_sy, 16)
        m_gin = val(s_sy)

        g.wait_ge(s_sy, m_gin)
        inc(g.collective_compute(
            "AllGather", ALU.bypass, replica_groups=[list(range(NCORES))],
            ins=[gin.ap().opt()], outs=[gout.ap().opt()]), s_cc, 1)
        m_cc = val(s_cc)

        sy.wait_ge(s_cc, m_cc)
        inc(sy.dma_start(out=rkv_all[:, :], in_=gout[:, :]), s_sy, 16)
        m_rkvall = val(s_sy)

        if dbg_out("rkv_all", [64, 512]) is not None:
            inc(sy.dma_start(out=dbg["rkv_all"], in_=rkv_all[:, :]), s_st, 16)
        if dbg_out("rq", [64, 4, 512]) is not None:
            g.wait_ge(s_sc, m_rq_done)
            inc(g.dma_start(out=dbg["rq"], in_=rq_sb[:, :, 0:512]), s_gp, 16)

        ln.close()

        arena_free = [(s_sc, m_rq_done), (s_dv, m_pooled), (s_sy, m_gin)]
        m_stm_free = [(s_pe, 0)] * 8

        for half in range(2):
            if "qkv" not in phases:
                break
            kbase = half * 896
            qbase = HALO + half * 1024

            att = ExitStack()
            q_fm = att.enter_context(nc.sbuf_tensor(f"q_fm{half}", [128, 8, 1024], BF16))
            k_fm = att.enter_context(nc.sbuf_tensor(f"k_fm{half}", [128, 8, 1280], BF16))
            v_tm = att.enter_context(nc.sbuf_tensor(f"v_tm{half}", [128, 10, 1024], BF16))
            ow_sb = att.enter_context(nc.sbuf_tensor(f"ow_sb{half}", [128, 8, 1024], BF16))
            rout_sb = att.enter_context(nc.sbuf_tensor(f"rout_sb{half}", [128, 8, 256], BF16))
            rkT = att.enter_context(nc.sbuf_tensor(f"rkT{half}", [64, 4, 64], BF16))
            rkbf = att.enter_context(nc.sbuf_tensor(f"rkbf{half}", [64, 256], BF16))
            rv_aug = att.enter_context(nc.sbuf_tensor(f"rv_aug{half}", [64, 4, 64], BF16))
            otcol = att.enter_context(nc.sbuf_tensor(f"otcol{half}", [65, 1], BF16))
            bvob = att.enter_context(nc.sbuf_tensor(f"bvob{half}", [1, 2048], BF16))
            bvob_lo = att.enter_context(nc.sbuf_tensor(f"bvlo{half}", [1, 2048], BF16))
            ebc = att.enter_context(nc.sbuf_tensor(f"ebc{half}", [16, 8, 128], BF16))
            P6 = att.enter_context(nc.psum_tensor(f"P6_{half}", [128, 512], F32))
            P7 = att.enter_context(nc.psum_tensor(f"P7_{half}", [128, 512], F32))
            pfree[6] = (s_pe, 0)
            pfree[7] = (s_pe, 0)

            wqkv = ExitStack()
            wqk_buf = wqkv.enter_context(nc.sbuf_tensor(f"wqk{half}", [128, 2, 1024], BF16))
            bvf = wqkv.enter_context(nc.sbuf_tensor(f"bvf{half}", [1, 2048], F32))
            wv_sb = wqkv.enter_context(nc.sbuf_tensor(f"wv{half}", [128, 8, 1024], BF16))

            ld_qk = [0] * 16
            mm_qk = [0] * 16

            for sem, vv in arena_free:
                g.wait_ge(sem, vv)
            inc(g.dma_start(out=bvf[:, :], in_=brow_in[0:1, 0:2048]), s_gp, 16)
            inc(g.dma_start(out=ebc[:, :, :], in_=ebc_in), s_gp, 16)
            m_bvf = val(s_gp)

            v.wait_ge(s_gp, m_bvf)
            inc(v.tensor_copy(bvob[:, :], bvf[:, :]), s_dv, 1)
            v.wait_ge(s_dv, val(s_dv))
            inc(v.scalar_tensor_tensor(bvob_lo[:, :], bvf[:, :], 1.0, bvob[:, :],
                                       ALU.mult, ALU.subtract), s_dv, 1)
            m_bsplit = val(s_dv)

            for j in range(2):
                ld_qk[j] = inc(g.dma_start(out=wqk_buf[:, j, :], in_=qkvw_qk[j]), s_gp, 16)

            for j in range(16):
                is_q = j < 8
                oc = j if is_q else j - 8
                ngroups = [(0, 512), (512, 1024)] if is_q else [(0, 512), (512, 1024), (1024, 1280)]
                base = qbase if is_q else kbase
                dest = q_fm if is_q else k_fm
                banks = [0, 1, 4] if (j % 2 == 0) else [2, 3, 4]
                grp_mm = [0] * len(ngroups)

                tn.wait_ge(s_gp, ld_qk[j])
                tn.wait_ge(s_dv, m_ln_done)
                for gi, (a0, a1) in enumerate(ngroups):
                    bank = banks[gi]
                    tn.wait_ge(*pfree[bank])
                    for dc in range(8):
                        ins = tn.matmul(P[bank][:, 0:a1 - a0],
                                        wqk_buf[:, j % 2, dc * 128:(dc + 1) * 128],
                                        xn_fm[:, dc, base + a0:base + a1],
                                        start=(dc == 0), stop=(dc == 7))
                    grp_mm[gi] = inc(ins, s_pe, 1)
                mm_qk[j] = grp_mm[-1]

                colidx = oc if j < 8 else 8 + oc
                for gi, (a0, a1) in enumerate(ngroups):
                    bank = banks[gi]
                    s.wait_ge(s_pe, grp_mm[gi])
                    m = inc(s.activation(dest[:, oc, a0:a1], P[bank][:, 0:a1 - a0],
                                         AF.Identity, bias=bcol[:, colidx:colidx + 1],
                                         scale=1.0), s_sc, 1)
                    pfree[bank] = (s_sc, m)

                if j + 2 < 16:
                    g.wait_ge(s_pe, mm_qk[j])
                    ld_qk[j + 2] = inc(g.dma_start(out=wqk_buf[:, j % 2, :],
                                                   in_=qkvw_qk[j + 2]), s_gp, 16)

            for dc in range(8):
                inc(g.dma_start(out=wv_sb[:, dc, :], in_=qkvw_v[dc]), s_gp, 16)
            m_wv = val(s_gp)

            for dc in range(8):
                inc(g.dma_start(out=ow_sb[:, dc, :], in_=outw_in[dc]), s_gp, 16)
            for oc in range(8):
                inc(g.dma_start(out=rout_sb[:, oc, :], in_=routw_in[oc]), s_gp, 16)
            m_ow = val(s_gp)

            for t in range(10):
                for gi in range(2):
                    bank = 4 + (t * 2 + gi) % 2
                    if bank == 4 + 1:
                        bank = 5

                    tn.wait_ge(s_gp, m_wv)
                    tn.wait_ge(s_dv, m_bsplit)
                    tn.wait_ge(*pfree[bank])
                    for dc in range(8):
                        tn.matmul(P[bank][:, :],
                                  xn_fm[:, dc, kbase + t * 128:kbase + (t + 1) * 128],
                                  wv_sb[:, dc, gi * 512:(gi + 1) * 512],
                                  start=(dc == 0), stop=False)
                    tn.matmul(P[bank][:, :], ones_bf[0:1, 0:128],
                              bvob[0:1, gi * 512:(gi + 1) * 512],
                              start=False, stop=False)
                    inc(tn.matmul(P[bank][:, :], ones_bf[0:1, 0:128],
                                  bvob_lo[0:1, gi * 512:(gi + 1) * 512],
                                  start=False, stop=True), s_pe, 1)
                    mmv = val(s_pe)

                    v.wait_ge(s_pe, mmv)
                    m = inc(v.tensor_copy(v_tm[:, t, gi * 512:(gi + 1) * 512],
                                          P[bank][:, :]), s_dv, 1)
                    pfree[bank] = (s_dv, m)
            m_v_done = val(s_dv)

            if half == 0 and dbg_out("qkv", [128, 3, 512]) is not None:
                g.wait_ge(s_sc, val(s_sc))
                g.wait_ge(s_dv, m_v_done)
                inc(g.dma_start(out=dbg["qkv"][:, 0:1, :], in_=q_fm[:, 0:1, 0:512]), s_gp, 16)
                inc(g.dma_start(out=dbg["qkv"][:, 1:2, :], in_=k_fm[:, 0:1, 0:512]), s_gp, 16)
                inc(g.dma_start(out=dbg["qkv"][:, 2:3, :], in_=v_tm[:, 0:1, 0:512]),
                    s_gp, 16)

            wqkv.close()

            if "attn" not in phases:
                att.close()
                arena_free = [(s_dv, m_v_done), (s_sc, val(s_sc))]
                continue

            scr = ExitStack()
            pt_sb = scr.enter_context(nc.sbuf_tensor(f"pt{half}", [128, 2, 512], BF16))
            local_n = scr.enter_context(nc.sbuf_tensor(f"locn{half}", [128, 8, 256], BF16))
            remote_n = scr.enter_context(nc.sbuf_tensor(f"remn{half}", [128, 8, 256], BF16))
            mixed = scr.enter_context(nc.sbuf_tensor(f"mixed{half}", [128, 8, 256], BF16))
            avst = scr.enter_context(nc.sbuf_tensor(f"avst{half}", [128, 256], F32))
            rr = scr.enter_context(nc.sbuf_tensor(f"rr{half}", [1, 2048], BF16))
            rrL = scr.enter_context(nc.sbuf_tensor(f"rrL{half}", [1, 512], BF16))
            ptl = scr.enter_context(nc.sbuf_tensor(f"ptl{half}", [65, 512], BF16))
            latst = scr.enter_context(nc.sbuf_tensor(f"latst{half}", [128, 256], F32))
            latout = scr.enter_context(nc.sbuf_tensor(f"latout{half}", [128, 2, 256], BF16))
            gsb = scr.enter_context(nc.sbuf_tensor(f"gsb{half}", [16, 256], BF16))
            gtmp = scr.enter_context(nc.sbuf_tensor(f"gtmp{half}", [16, 256], F32))
            dtmp = scr.enter_context(nc.sbuf_tensor(f"dtmp{half}", [128, 256], BF16))

            v.wait_ge(s_sy, m_rkvall)
            inc(v.tensor_copy(rkbf[:, :], rkv_all[:, 0:256]), s_dv, 1)
            m_rkbf = val(s_dv)

            tn.wait_ge(s_dv, m_rkbf)
            tn.wait_ge(*pfree[6])
            for h in range(4):
                ins = tn.matmul(P6[0:64, h * 64:(h + 1) * 64],
                                rkbf[:, h * 64:(h + 1) * 64], ident[0:64, 0:64],
                                start=True, stop=True)
            inc(ins, s_pe, 1)
            m_rkt_mm = val(s_pe)

            v.wait_ge(s_pe, m_rkt_mm)
            m = inc(v.tensor_copy(rkT[:, :, :],
                                  P6[0:64, 0:256].rearrange("p (h n) -> p h n", h=4)),
                    s_dv, 1)
            pfree[6] = (s_dv, m)
            for h in range(4):
                v.tensor_copy(rv_aug[:, h, :],
                              rkv_all[:, 256 + h * 64:256 + (h + 1) * 64])
            v.memset(otcol[0:64, :], 1.0)
            v.memset(otcol[64:65, :], TINY)
            inc(v.memset(ptl[64:65, :], 1.0), s_dv, 1)
            m_lat_prep = val(s_dv)

            exp_m = {}
            av_m = {}
            avmult_m = {}
            recip_m = {}
            bc_m = {}
            lat_ms = {}
            m_gl_prev = 0
            m_mix_prev = 0
            m_oww_prev = 0

            for tq in range(attn_tq):
                w0 = tq + half
                tcol = (tq % 2) * 128
                use_F = (half == 0 and tq == 0)
                m0 = maskF if use_F else (maskA if tq % 2 == 0 else maskB)

                for bb in range(4):
                    avbank = 4 + bb % 2
                    heads = [4 * bb, 4 * bb + 2, 4 * bb + 1, 4 * bb + 3]

                    if (tq, bb) == (0, 0):
                        tn.wait_ge(s_dv, m_v_done)
                        tn.wait_ge(s_sc, m_rq_done)
                    for b_ in range(4):
                        tn.wait_ge(*pfree[b_])
                    ins = None
                    for s_i, hh in enumerate(heads):
                        r0 = 64 * (hh % 2)
                        dch = hh // 2
                        cb = [0, 1] if s_i < 2 else [2, 3]
                        scol = (s_i % 2) * 128
                        for c in range(2):
                            ko = 128 * (w0 + c)
                            ins = tn.matmul(P[cb[c]][:, scol:scol + 128],
                                            k_fm[r0:r0 + 64, dch, ko:ko + 128],
                                            q_fm[r0:r0 + 64, dch, tq * 128:(tq + 1) * 128],
                                            start=True, stop=True)
                    inc(ins, s_pe, 1)
                    m_sc_mm = val(s_pe)

                    if attn_sub < 2:
                        exp_m[(tq, bb)] = val(s_sc)
                        av_m[(tq, bb)] = val(s_pe)
                        recip_m[(tq, bb)] = val(s_dv)
                        bc_m[(tq, bb)] = val(s_pe)
                        avmult_m[(tq, bb)] = val(s_dv)
                        continue

                    s.wait_ge(s_pe, m_sc_mm)
                    if (tq, bb) != (0, 0):
                        prev = (tq, bb - 1) if bb > 0 else (tq - 1, 3)
                        s.wait_ge(s_pe, av_m[prev])
                    s.activation(pt_sb[:, 0, 0:256], P[0][:, 0:256], AF.Exp,
                                 bias=BSH, scale=0.125)
                    s.activation(pt_sb[:, 0, 256:512], P[2][:, 0:256], AF.Exp,
                                 bias=BSH, scale=0.125)
                    s.activation(pt_sb[:, 1, 0:256], P[1][:, 0:256], AF.Exp,
                                 bias=BSH, scale=0.125)
                    inc(s.activation(pt_sb[:, 1, 256:512], P[3][:, 0:256], AF.Exp,
                                     bias=BSH, scale=0.125), s_sc, 1)
                    exp_m[(tq, bb)] = val(s_sc)
                    for b_ in range(4):
                        pfree[b_] = (s_sc, exp_m[(tq, bb)])

                    if attn_sub < 3:
                        av_m[(tq, bb)] = val(s_pe)
                        recip_m[(tq, bb)] = val(s_dv)
                        bc_m[(tq, bb)] = val(s_pe)
                        avmult_m[(tq, bb)] = val(s_dv)
                        continue

                    v.wait_ge(s_sc, exp_m[(tq, bb)])
                    v.tensor_tensor(pt_sb[:, 0, :], pt_sb[:, 0, :], m0[:, :], ALU.mult)
                    inc(v.tensor_tensor(pt_sb[:, 1, :], pt_sb[:, 1, :], maskC[:, :],
                                        ALU.mult), s_dv, 1)
                    m_mask = val(s_dv)

                    if attn_sub < 4:
                        av_m[(tq, bb)] = val(s_pe)
                        recip_m[(tq, bb)] = val(s_dv)
                        bc_m[(tq, bb)] = val(s_pe)
                        avmult_m[(tq, bb)] = val(s_dv)
                        continue

                    tn.wait_ge(s_dv, m_mask)
                    key2 = (tq, bb - 2) if bb >= 2 else (tq - 1, bb + 2)
                    if key2 in avmult_m:
                        tn.wait_ge(s_dv, avmult_m[key2])
                    prevd = (tq, bb - 1) if bb > 0 else (tq - 1, 3)
                    if prevd in recip_m:
                        tn.wait_ge(s_dv, recip_m[prevd])
                    else:
                        tn.wait_ge(*pfree[6])
                    for s_i, hh in enumerate(heads):
                        r0 = 64 * (hh % 2)
                        scol = (s_i % 2) * 128
                        for c in range(2):
                            tn.matmul(P[avbank][r0:r0 + 64, scol:scol + 128],
                                      v_tm[:, w0 + c, hh * 64:(hh + 1) * 64],
                                      pt_sb[:, c, s_i * 128:(s_i + 1) * 128],
                                      start=(c == 0), stop=(c == 1))
                    tn.matmul(P6[0:1, :], ones_bf[:, 0:1], pt_sb[:, 0, :],
                              start=True, stop=False)
                    tn.matmul(P6[0:1, :], ones_bf[:, 0:1], pt_sb[:, 1, :],
                              start=False, stop=False)
                    inc(tn.matmul(P6[0:1, :], tiny_bf[0:1, 0:1], ones_bf[0:1, 0:512],
                                  start=False, stop=True), s_pe, 1)
                    av_m[(tq, bb)] = val(s_pe)

                    if attn_sub < 5:
                        recip_m[(tq, bb)] = val(s_dv)
                        bc_m[(tq, bb)] = val(s_pe)
                        avmult_m[(tq, bb)] = val(s_dv)
                        continue

                    v.wait_ge(s_pe, av_m[(tq, bb)])
                    if (tq - 1, bb) in bc_m:
                        v.wait_ge(s_pe, bc_m[(tq - 1, bb)])
                    with nc.allow_low_precision(reason="softmax recip as one bf16 factor"):
                        inc(v.reciprocal(rr[0:1, bb * 512:(bb + 1) * 512], P6[0:1, :]),
                            s_dv, 1)
                    recip_m[(tq, bb)] = val(s_dv)

                    if attn_sub < 6:
                        bc_m[(tq, bb)] = val(s_pe)
                        avmult_m[(tq, bb)] = val(s_dv)
                        continue

                    tn.wait_ge(s_dv, recip_m[(tq, bb)])
                    key2 = (tq, bb - 2) if bb >= 2 else (tq - 1, bb + 2)
                    if key2 in avmult_m:
                        tn.wait_ge(s_dv, avmult_m[key2])
                    if bb < 2 and (tq - 1) in lat_ms:
                        tn.wait_ge(s_dv, lat_ms[tq - 1]["mult"])
                    ins = None
                    for s_i, hh in enumerate(heads):
                        r0 = 64 * (hh % 2)
                        pcol = (bb % 2) * 256 + (s_i % 2) * 128
                        ins = tn.matmul(P7[r0:r0 + 64, pcol:pcol + 128],
                                        ones_bf[0:1, 0:64],
                                        rr[0:1, bb * 512 + s_i * 128:bb * 512 + (s_i + 1) * 128],
                                        start=True, stop=True)
                    inc(ins, s_pe, 1)
                    bc_m[(tq, bb)] = val(s_pe)

                    if attn_sub < 7:
                        avmult_m[(tq, bb)] = val(s_dv)
                        continue

                    s.wait_ge(s_pe, av_m[(tq, bb)])
                    key1 = (tq, bb - 1) if bb > 0 else (tq - 1, 3)
                    if key1 in avmult_m:
                        s.wait_ge(s_dv, avmult_m[key1])
                    inc(s.copy(avst[:, 0:256], P[avbank][:, 0:256]), s_sc, 1)
                    m_avevac = val(s_sc)
                    pfree[avbank] = (s_sc, m_avevac)

                    if attn_sub < 8:
                        avmult_m[(tq, bb)] = val(s_dv)
                        continue

                    v.wait_ge(s_pe, bc_m[(tq, bb)])
                    v.wait_ge(s_sc, m_avevac)
                    if m_gl_prev:
                        v.wait_ge(s_pe, m_gl_prev)
                    if m_mix_prev:
                        v.wait_ge(s_dv, m_mix_prev)
                    ins = None
                    for s_i, hh in enumerate(heads):
                        r0 = 64 * (hh % 2)
                        pcol = (bb % 2) * 256 + (s_i % 2) * 128
                        ins = v.tensor_tensor(local_n[r0:r0 + 64, hh // 2, tcol:tcol + 128],
                                              avst[r0:r0 + 64,
                                                   (s_i % 2) * 128:(s_i % 2) * 128 + 128],
                                              P7[r0:r0 + 64, pcol:pcol + 128], ALU.mult)
                    inc(ins, s_dv, 1)
                    avmult_m[(tq, bb)] = val(s_dv)

                if not attn_lat:
                    continue
                lm = {}

                tn.wait_ge(*pfree[0])
                tn.wait_ge(s_dv, m_lat_prep)
                ins = None
                for h in range(4):
                    ins = tn.matmul(P[0][0:64, h * 128:(h + 1) * 128], rkT[:, h, :],
                                    rq_sb[:, h, half * 1024 + tq * 128:half * 1024 + (tq + 1) * 128],
                                    start=True, stop=True)
                inc(ins, s_pe, 1)
                lm["lsc"] = val(s_pe)

                s.wait_ge(s_pe, lm["lsc"])
                if (tq - 1) in lat_ms:
                    s.wait_ge(s_pe, lat_ms[tq - 1]["av"])
                inc(s.activation(ptl[0:64, :], P[0][0:64, :], AF.Exp,
                                 bias=BSH[0:64, :], scale=0.125), s_sc, 1)
                lm["exp"] = val(s_sc)
                pfree[0] = (s_sc, lm["exp"])

                v.wait_ge(s_sc, lm["exp"])
                inc(v.tensor_scalar(ptl[0:64, :], ptl[0:64, :],
                                    latmask[:, half * 8 + tq:half * 8 + tq + 1], None,
                                    ALU.mult), s_dv, 1)
                lm["mask"] = val(s_dv)

                tn.wait_ge(s_dv, lm["mask"])
                tn.wait_ge(*pfree[1])
                tn.wait_ge(*pfree[3])
                if (tq - 1) in lat_ms:
                    tn.wait_ge(s_dv, lat_ms[tq - 1]["rec"])
                for h in range(4):
                    r0 = 64 * (h % 2)
                    sl = (256 if h % 2 else 0) + (h // 2) * 128
                    tn.matmul(P[1][r0:r0 + 64, sl:sl + 128], rv_aug[0:64, h, 0:64],
                              ptl[0:64, h * 128:(h + 1) * 128], start=True, stop=True)
                ins = None
                for h in range(4):
                    ins = tn.matmul(P[3][0:1, h * 128:(h + 1) * 128], otcol[0:65, 0:1],
                                    ptl[0:65, h * 128:(h + 1) * 128], start=True, stop=True)
                inc(ins, s_pe, 1)
                lm["av"] = val(s_pe)

                v.wait_ge(s_pe, lm["av"])
                if (tq - 1) in lat_ms:
                    v.wait_ge(s_pe, lat_ms[tq - 1]["bc"])
                with nc.allow_low_precision(reason="softmax recip as one bf16 factor"):
                    inc(v.reciprocal(rrL[0:1, 0:512], P[3][0:1, :]), s_dv, 1)
                lm["rec"] = val(s_dv)
                pfree[3] = (s_dv, lm["rec"])

                tn.wait_ge(s_dv, lm["rec"])
                tn.wait_ge(s_dv, avmult_m[(tq, 2)])
                if (tq - 1) in lat_ms:
                    tn.wait_ge(s_dv, lat_ms[tq - 1]["mult"])
                ins = None
                for h in range(4):
                    r0 = 64 * (h % 2)
                    pcol = (h // 2) * 128
                    ins = tn.matmul(P7[r0:r0 + 64, pcol:pcol + 128], ones_bf[0:1, 0:64],
                                    rrL[0:1, h * 128:(h + 1) * 128], start=True, stop=True)
                inc(ins, s_pe, 1)
                lm["bc"] = val(s_pe)

                s.wait_ge(s_pe, lm["av"])
                if (tq - 1) in lat_ms:
                    s.wait_ge(s_dv, lat_ms[tq - 1]["mult"])
                s.copy(latst[0:64, 0:256], P[1][0:64, 0:256])
                inc(s.copy(latst[64:128, 0:256], P[1][64:128, 256:512]), s_sc, 1)
                lm["evac"] = val(s_sc)
                pfree[1] = (s_sc, lm["evac"])

                v.wait_ge(s_pe, lm["bc"])
                v.wait_ge(s_sc, lm["evac"])
                ins = None
                for h in range(4):
                    r0 = 64 * (h % 2)
                    ins = v.tensor_tensor(latout[r0:r0 + 64, h // 2, tcol:tcol + 128],
                                          latst[r0:r0 + 64, (h // 2) * 128:(h // 2) * 128 + 128],
                                          P7[r0:r0 + 64, (h // 2) * 128:(h // 2) * 128 + 128],
                                          ALU.mult)
                inc(ins, s_dv, 1)
                lm["mult"] = val(s_dv)
                lat_ms[tq] = lm

                if tq % 2 == 1 and attn_grp:
                    grp_rout_mm = {}
                    for oc in range(8):
                        bank = 2 + oc % 2

                        tn.wait_ge(s_dv, lat_ms[tq]["mult"])
                        tn.wait_ge(s_dv, lat_ms[tq - 1]["mult"])
                        tn.wait_ge(*pfree[bank])
                        for dc in range(2):
                            ins = tn.matmul(P[bank][:, 0:256],
                                            rout_sb[:, oc, dc * 128:(dc + 1) * 128],
                                            latout[:, dc, 0:256], start=(dc == 0), stop=(dc == 1))
                        inc(ins, s_pe, 1)
                        mmv = val(s_pe)

                        s.wait_ge(s_pe, mmv)
                        if m_mix_prev:
                            s.wait_ge(s_dv, m_mix_prev)
                        m = inc(s.activation(remote_n[:, oc, :], P[bank][:, 0:256],
                                             AF.Identity, bias=bcol[:, 24 + oc:25 + oc],
                                             scale=1.0), s_sc, 1)
                        pfree[bank] = (s_sc, m)
                    m_remote = val(s_sc)

                    tn.wait_ge(s_sc, m_remote)
                    tn.wait_ge(s_dv, avmult_m[(tq, 3)])
                    tn.wait_ge(*pfree[5])
                    tn.wait_ge(s_dv, lat_ms[tq]["rec"])
                    for dc in range(8):
                        tn.matmul(P[5][0:16, 0:256], bdg[:, dc, 0:16], local_n[:, dc, :],
                                  start=(dc == 0), stop=False)
                    for dc in range(8):
                        ins = tn.matmul(P[5][0:16, 0:256], bdg[:, dc, 16:32], remote_n[:, dc, :],
                                        start=False, stop=(dc == 7))
                    inc(ins, s_pe, 1)
                    m_gl = val(s_pe)
                    m_gl_prev = m_gl

                    s.wait_ge(s_pe, m_gl)
                    inc(s.activation(gtmp[:, :], P[5][0:16, 0:256], AF.Exp,
                                     bias=ZCOL[0:16, :], scale=-1.0), s_sc, 1)
                    m_gexp = val(s_sc)
                    pfree[5] = (s_sc, m_gexp)

                    v.wait_ge(s_sc, m_gexp)
                    inc(v.tensor_scalar(gtmp[:, :], gtmp[:, :], 1.0, None, ALU.add), s_dv, 1)
                    v.wait_ge(s_dv, val(s_dv))
                    inc(v.reciprocal(gtmp[:, :], gtmp[:, :]), s_dv, 1)
                    v.wait_ge(s_dv, val(s_dv))
                    inc(v.tensor_copy(gsb[:, :], gtmp[:, :]), s_dv, 1)
                    m_gate = val(s_dv)

                    m_mix_rounds = []
                    for rnd in range(2):
                        tn.wait_ge(s_dv, m_gate)
                        tn.wait_ge(*pfree[0])
                        tn.wait_ge(*pfree[1])
                        for k in range(4):
                            dc = rnd * 4 + k
                            bank = k % 2
                            ins = tn.matmul(P[bank][:, (k // 2) * 256:(k // 2) * 256 + 256],
                                            ebc[:, dc, :], gsb[:, :], start=True, stop=True)
                        inc(ins, s_pe, 1)
                        m_gbc = val(s_pe)

                        v.wait_ge(s_pe, m_gbc)
                        if m_oww_prev:
                            v.wait_ge(s_pe, m_oww_prev)
                        for k in range(4):
                            dc = rnd * 4 + k
                            bank = k % 2
                            v.tensor_tensor(dtmp[:, :], local_n[:, dc, :],
                                            remote_n[:, dc, :], ALU.subtract)
                            v.tensor_tensor(dtmp[:, :], dtmp[:, :],
                                            P[bank][:, (k // 2) * 256:(k // 2) * 256 + 256],
                                            ALU.mult)
                            ins = v.tensor_tensor(mixed[:, dc, :], dtmp[:, :],
                                                  remote_n[:, dc, :], ALU.add)
                        inc(ins, s_dv, 1)
                        m_mix_rounds.append(val(s_dv))
                        pfree[0] = (s_dv, m_mix_rounds[-1])
                        pfree[1] = (s_dv, m_mix_rounds[-1])
                    m_mix_done = m_mix_rounds[-1]
                    m_mix_prev = m_mix_done

                    for tt in range(2):
                        tloc = (tq - 1) + tt
                        bank0, bank1 = (2, 3) if tt == 0 else (4, 5)

                        tn.wait_ge(s_dv, m_mix_done)
                        tn.wait_ge(s_gp, m_ow)
                        tn.wait_ge(*pfree[bank0])
                        tn.wait_ge(*pfree[bank1])
                        for dc in range(8):
                            tn.matmul(P[bank0][:, :], mixed[:, dc, tt * 128:(tt + 1) * 128],
                                      ow_sb[:, dc, 0:512], start=(dc == 0), stop=False)
                        tn.matmul(P[bank0][:, :], ones_bf[0:1, 0:128],
                                  bvob[0:1, 1024:1536], start=False, stop=False)
                        tn.matmul(P[bank0][:, :], ones_bf[0:1, 0:128],
                                  bvob_lo[0:1, 1024:1536], start=False, stop=True)
                        for dc in range(8):
                            tn.matmul(P[bank1][:, :], mixed[:, dc, tt * 128:(tt + 1) * 128],
                                      ow_sb[:, dc, 512:1024], start=(dc == 0), stop=False)
                        tn.matmul(P[bank1][:, :], ones_bf[0:1, 0:128],
                                  bvob[0:1, 1536:2048], start=False, stop=False)
                        inc(tn.matmul(P[bank1][:, :], ones_bf[0:1, 0:128],
                                      bvob_lo[0:1, 1536:2048], start=False, stop=True), s_pe, 1)
                        m_oww = val(s_pe)

                        s.wait_ge(s_pe, m_oww)
                        if m_stm_free[tloc][1] > 0:
                            s.wait_ge(*m_stm_free[tloc])
                        s.copy(s_tm[:, tloc, 0:512], P[bank0][:, :])
                        m = inc(s.copy(s_tm[:, tloc, 512:1024], P[bank1][:, :]), s_sc, 1)
                        pfree[bank0] = (s_sc, m)
                        pfree[bank1] = (s_sc, m)
                    m_oww_prev = m_oww

                    if half == 0 and tq == 1:
                        if dbg_out("g0_local", [128, 8, 256]) is not None:
                            g.wait_ge(s_dv, m_mix_done)
                            inc(g.dma_start(out=dbg["g0_local"], in_=local_n[:, :, :]), s_gp, 16)
                            inc(g.dma_start(out=dbg["g0_remote"], in_=remote_n[:, :, :]), s_gp, 16)
                            inc(g.dma_start(out=dbg["g0_mixed"], in_=mixed[:, :, :]), s_gp, 16)
                            inc(g.dma_start(out=dbg["g0_gate"], in_=gsb[:, :]), s_gp, 16)

            if attn_tq < 8 or not attn_lat or not attn_grp or attn_sub < 8:
                _cut()
                scr.close()
                att.close()
                return nc, dbg

            m_attn_sc = val(s_sc)
            m_attn_dv = val(s_dv)

            if half == 0 and dbg_out("s_tm", [128, 8, 1024]) is not None:
                sy.wait_ge(s_sc, m_attn_sc)
                inc(sy.dma_start(out=dbg["s_tm"], in_=s_tm[:, :, :]), s_st, 16)
            if half == 0 and dbg_out("local_n", [128, 8, 256]) is not None:
                g.wait_ge(s_dv, m_attn_dv)
                inc(g.dma_start(out=dbg["local_n"], in_=local_n[:, :, :]), s_gp, 16)
            if half == 0 and dbg_out("remote_n", [128, 8, 256]) is not None:
                g.wait_ge(s_dv, m_attn_dv)
                inc(g.dma_start(out=dbg["remote_n"], in_=remote_n[:, :, :]), s_gp, 16)

            scr.close()
            att.close()

            if "ffn" not in phases:
                arena_free = [(s_sc, m_attn_sc), (s_dv, m_attn_dv)]
                continue

            ffn = ExitStack()
            h_sb = ffn.enter_context(nc.sbuf_tensor(f"h_sb{half}", [128, 24, 1024], BF16))
            wd_sb = ffn.enter_context(nc.sbuf_tensor(f"wd_sb{half}", [128, 24, 1024], BF16))
            wg_buf = ffn.enter_context(nc.sbuf_tensor(f"wg_buf{half}", [128, 2, 1024], BF16))
            wu_buf = ffn.enter_context(nc.sbuf_tensor(f"wu_buf{half}", [128, 2, 1024], BF16))
            sg_buf = ffn.enter_context(nc.sbuf_tensor(f"sg_buf{half}", [128, 2, 512], BF16))
            g_buf = ffn.enter_context(nc.sbuf_tensor(f"g_buf{half}", [128, 2, 512], F32))
            xre = ffn.enter_context(nc.sbuf_tensor(f"xre{half}", [128, 1024], F32))
            PF = [ffn.enter_context(nc.psum_tensor(f"PF{i}_{half}", [128, 512], F32))
                  for i in range(2)]
            pfree[6] = (s_pe, 0)
            pfree[7] = (s_pe, 0)

            attn_done = [(s_sc, m_attn_sc), (s_dv, m_attn_dv)]

            ld_gu = [0] * 24
            ld_wd = [0] * 24
            mm_gu = [0] * 24

            for sem, vv in attn_done:
                g.wait_ge(sem, vv)
            for j in range(2):
                inc(g.dma_start(out=wg_buf[:, j, :], in_=wgt_in[j]), s_gp, 16)
                ld_gu[j] = inc(g.dma_start(out=wu_buf[:, j, :], in_=wut_in[j]), s_gp, 16)

            for hc in range(24):
                bsel = hc % 2
                if bsel == 0:
                    pg = [P[0], P[1], P[2], P[3]]
                    bk = [0, 1, 2, 3]
                else:
                    pg = [P[4], P[5], PF[0], PF[1]]
                    bk = [4, 5, 6, 7]

                tn.wait_ge(s_gp, ld_gu[hc])
                for b in bk:
                    tn.wait_ge(*pfree[b])
                for gi in range(2):
                    for dc in range(8):
                        tn.matmul(pg[gi][:, :], wg_buf[:, bsel, dc * 128:(dc + 1) * 128],
                                  xn_fm[:, dc, qbase + gi * 512:qbase + (gi + 1) * 512],
                                  start=(dc == 0), stop=(dc == 7))
                    for dc in range(8):
                        ins = tn.matmul(pg[2 + gi][:, :], wu_buf[:, bsel, dc * 128:(dc + 1) * 128],
                                        xn_fm[:, dc, qbase + gi * 512:qbase + (gi + 1) * 512],
                                        start=(dc == 0), stop=(dc == 7))
                inc(ins, s_pe, 1)
                mm_gu[hc] = val(s_pe)

                s.wait_ge(s_pe, mm_gu[hc])
                if hc >= 1:
                    pass
                s.activation(sg_buf[:, 0, :], pg[0][:, :], AF.Silu, scale=SILU1)
                inc(s.activation(sg_buf[:, 1, :], pg[1][:, :], AF.Silu, scale=SILU1),
                    s_sc, 1)
                m_silu = val(s_sc)
                pfree[bk[0]] = (s_sc, m_silu)
                pfree[bk[1]] = (s_sc, m_silu)

                v.wait_ge(s_sc, m_silu)
                v.tensor_tensor(h_sb[:, hc, 0:512], sg_buf[:, 0, :], pg[2][:, :], ALU.mult)
                inc(v.tensor_tensor(h_sb[:, hc, 512:1024], sg_buf[:, 1, :], pg[3][:, :],
                                    ALU.mult), s_dv, 1)
                m_h = val(s_dv)
                pfree[bk[2]] = (s_dv, m_h)
                pfree[bk[3]] = (s_dv, m_h)

                if hc + 2 < 24:
                    g.wait_ge(s_pe, mm_gu[hc])
                    inc(g.dma_start(out=wg_buf[:, hc % 2, :], in_=wgt_in[hc + 2]), s_gp, 16)
                    ld_gu[hc + 2] = inc(g.dma_start(out=wu_buf[:, hc % 2, :],
                                                    in_=wut_in[hc + 2]), s_gp, 16)
                ld_wd[hc] = inc(g.dma_start(out=wd_sb[:, hc, :], in_=wdt_in[hc]), s_gp, 16)
            m_h_all = val(s_dv)
            m_wd_all = ld_wd[23]

            m_epi_prev = 0
            for tt in range(8):
                bank0, bank1 = (0, 1) if tt % 2 == 0 else (2, 3)

                if tt == 0:
                    for sem, vv in attn_done:
                        sy.wait_ge(sem, vv)
                if m_epi_prev:
                    sy.wait_ge(s_dv, m_epi_prev)
                inc(sy.dma_start(out=xre[:, :],
                                 in_=x_in[qbase + tt * 128:qbase + (tt + 1) * 128, :]),
                    s_sy, 16)
                m_xre = val(s_sy)

                tn.wait_ge(s_dv, m_h_all)
                tn.wait_ge(s_gp, m_wd_all)
                tn.wait_ge(*pfree[bank0])
                tn.wait_ge(*pfree[bank1])
                for hc in range(24):
                    tn.matmul(P[bank0][:, :], h_sb[:, hc, tt * 128:(tt + 1) * 128],
                              wd_sb[:, hc, 0:512], start=(hc == 0), stop=(hc == 23))
                for hc in range(24):
                    ins = tn.matmul(P[bank1][:, :], h_sb[:, hc, tt * 128:(tt + 1) * 128],
                                    wd_sb[:, hc, 512:1024], start=(hc == 0), stop=(hc == 23))
                inc(ins, s_pe, 1)
                m_down = val(s_pe)

                s.wait_ge(s_pe, m_down)
                s.activation(g_buf[:, 0, :], P[bank0][:, :], AF.Silu, scale=SILU2)
                inc(s.activation(g_buf[:, 1, :], P[bank1][:, :], AF.Silu, scale=SILU2),
                    s_sc, 1)
                m_g = val(s_sc)
                pfree[bank0] = (s_sc, m_g)
                pfree[bank1] = (s_sc, m_g)

                v.wait_ge(s_sc, m_g)
                v.wait_ge(s_sy, m_xre)
                for gi in range(2):
                    v.tensor_tensor(g_buf[:, gi, :], g_buf[:, gi, :],
                                    s_tm[:, tt, gi * 512:(gi + 1) * 512], ALU.mult)
                inc(v.tensor_tensor(xre[:, :], xre[:, :],
                                    g_buf[:, :, :].rearrange("p a b -> p (a b)"), ALU.add),
                    s_dv, 1)
                m_epi = val(s_dv)
                m_epi_prev = m_epi
                m_stm_free[tt] = (s_dv, m_epi)

                sy.wait_ge(s_dv, m_epi)
                inc(sy.dma_start(
                    out=y_out[half * 1024 + tt * 128:half * 1024 + (tt + 1) * 128, :],
                    in_=xre[:, :]), s_st, 16)

            ffn.close()
            arena_free = [(s_dv, val(s_dv)), (s_sc, val(s_sc)), (s_pe, val(s_pe))]

        sy.wait_ge(s_st, val(s_st))
        sy.wait_ge(s_gp, val(s_gp))

    return nc, dbg


def _prep_inputs(inputs):
    x = np.asarray(inputs["L0_attn_x"], np.float32).reshape(S, D)
    qkv_w = np.asarray(inputs["L0_attn_qkv_w"], np.float32)
    qkv_b = np.asarray(inputs["L0_attn_qkv_b"], np.float32)
    rq_w = np.asarray(inputs["L0_attn_rq_w"], np.float32)
    rq_b = np.asarray(inputs["L0_attn_rq_b"], np.float32)
    rkv_w = np.asarray(inputs["L0_attn_rkv_w"], np.float32)
    rkv_b = np.asarray(inputs["L0_attn_rkv_b"], np.float32)
    rout_w = np.asarray(inputs["L0_attn_rout_w"], np.float32)
    rout_b = np.asarray(inputs["L0_attn_rout_b"], np.float32)
    out_w = np.asarray(inputs["L0_attn_out_w"], np.float32)
    out_b = np.asarray(inputs["L0_attn_out_b"], np.float32)
    gl_w = np.asarray(inputs["L0_attn_gl_w"], np.float32)
    gr_w = np.asarray(inputs["L0_attn_gr_w"], np.float32)
    w_gate = np.asarray(inputs["L0_ffn_w_gate"], np.int32)
    w_up = np.asarray(inputs["L0_ffn_w_up"], np.int32)
    w_down = np.asarray(inputs["L0_ffn_w_down"], np.int32)

    x_pad = np.concatenate([np.zeros((HALO, D), np.float32), x], 0)

    qkvw_qk = np.ascontiguousarray(
        qkv_w[:, 0:2048].reshape(8, 128, 16, 128).transpose(2, 1, 0, 3).reshape(16, 128, 1024))
    qkvw_v = np.ascontiguousarray(qkv_w[:, 2048:3072].reshape(8, 128, 1024))
    outw_r = np.ascontiguousarray(out_w.reshape(8, 128, 1024))
    rqw_r = np.ascontiguousarray(
        rq_w.reshape(8, 128, 4, 64).transpose(2, 1, 0, 3).reshape(4, 128, 512))
    rkvw_r = np.ascontiguousarray(rkv_w.reshape(8, 128, 512))
    routw_r = np.ascontiguousarray(
        rout_w.reshape(2, 128, 8, 128).transpose(2, 1, 0, 3).reshape(8, 128, 256))
    wgt_r = np.ascontiguousarray(
        w_gate.reshape(24, 128, 8, 128).transpose(0, 3, 2, 1).reshape(24, 128, 1024))
    wut_r = np.ascontiguousarray(
        w_up.reshape(24, 128, 8, 128).transpose(0, 3, 2, 1).reshape(24, 128, 1024))
    wdt_r = np.ascontiguousarray(w_down.T.reshape(24, 128, 1024))

    bcolv = np.zeros((128, 40), np.float32)
    bcolv[:, 0:24] = qkv_b.reshape(24, 128).T
    bcolv[:, 24:32] = rout_b.reshape(8, 128).T
    bcolv[0:64, 32:36] = rq_b.reshape(4, 64).T
    browv = np.zeros((1, 2560), np.float32)
    browv[0, 0:1024] = qkv_b[2048:3072]
    browv[0, 1024:2048] = out_b
    browv[0, 2048:2560] = rkv_b

    bdgv = np.zeros((1024, 32), np.float32)
    for h in range(16):
        bdgv[h * 64:(h + 1) * 64, h] = gl_w[:, 0]
        bdgv[h * 64:(h + 1) * 64, 16 + h] = gr_w[:, 0]
    bdgv = np.ascontiguousarray(bdgv.reshape(8, 128, 32).transpose(1, 0, 2))

    ebcv = np.zeros((16, 8, 128), ml_dtypes.bfloat16)
    for dc in range(8):
        for pp in range(128):
            ebcv[(dc * 128 + pp) // 64, dc, pp] = 1.0

    identv = np.eye(128, dtype=ml_dtypes.bfloat16)

    r = np.arange(128)
    c = np.arange(128)
    m0 = (c[:, None] >= r[None, :]).astype(np.float32)
    m0e = m0.copy()
    m0e[0, 0] = 0.0
    m1 = (c[:, None] < r[None, :]).astype(np.float32)
    maskAv = np.tile(m0e, (1, 4)).astype(ml_dtypes.bfloat16)
    maskBv = np.tile(m0, (1, 4)).astype(ml_dtypes.bfloat16)
    maskCv = np.tile(m1, (1, 4)).astype(ml_dtypes.bfloat16)
    maskF0 = np.zeros((128, 512), ml_dtypes.bfloat16)

    shared = dict(qkvw_qk=qkvw_qk, qkvw_v=qkvw_v, outw_in=outw_r, rqw_in=rqw_r,
                  rkvw_in=rkvw_r, routw_in=routw_r, wgt_in=wgt_r, wut_in=wut_r,
                  wdt_in=wdt_r, bcol_in=bcolv, brow_in=browv, bdg_in=bdgv,
                  ebc_in=ebcv, ident_in=identv, identf_in=np.eye(64, dtype=np.float32),
                  maskA_in=maskAv, maskB_in=maskBv,
                  maskC_in=maskCv)
    in_maps = []
    ll = np.arange(64)
    for cidx in range(NCORES):
        m = dict(shared)
        m["x_in"] = np.ascontiguousarray(x_pad[cidx * T:cidx * T + TB])
        m["maskF_in"] = maskF0 if cidx == 0 else maskAv
        lmv = np.zeros((64, 16), np.float32)
        for jj in range(16):
            lmv[:, jj] = (ll < cidx * 8 + jj // 2).astype(np.float32)
        m["latmask_in"] = lmv
        in_maps.append(m)
    return in_maps


def kernel(**inputs):
    global LAST_DEBUG
    in_maps = _prep_inputs(inputs)
    nc, dbg = build(phases=PHASES, debug=DEBUG)
    res = run_bass_kernel_spmd(nc, in_maps, core_ids=list(range(NCORES)))
    LAST_DEBUG = {}
    for name in dbg:
        LAST_DEBUG[name] = [res.results[cc].get("dbg_" + name) for cc in range(NCORES)]
    out = np.concatenate([res.results[cc]["y_out"] for cc in range(NCORES)], axis=0)
    return out.reshape(1, S, D).astype(np.float32)


# version 09
# speedup vs baseline: 1.1072x; 1.0018x over previous
import numpy as np
import ml_dtypes
from contextlib import ExitStack

import concourse.bass as bass
import concourse.mybir as mybir
from concourse.bass_utils import run_bass_kernel_spmd

F32 = mybir.dt.float32
BF16 = mybir.dt.bfloat16
I32 = mybir.dt.int32
AF = mybir.ActivationFunctionType
ALU = mybir.AluOpType
AX = mybir.AxisListType

NCORES = 8
S = 16384
D = 1024
T = 2048
HALO = 128
TB = T + HALO
NT = TB // 128
BSHIFT = 40.0
SILU1 = 1.0 / 127.0
SILU2 = 1.0 / (127.0 * 127.0)
TINY = 1e-30

PHASES = ("ln", "qkv", "attn", "ffn")
DEBUG = ()
LAST_DEBUG = {}


def build(phases=PHASES, debug=(), nc_kwargs=None, stage=99, attn_tq=8, attn_lat=True,
          attn_grp=True, attn_sub=8):
    debug = set(debug)
    nc = bass.Bass("TRN2", target_bir_lowering=False, **(nc_kwargs or {}))

    x_in = nc.dram_tensor("x_in", [TB, D], F32, kind="ExternalInput").ap()
    qkvw_qk = nc.dram_tensor("qkvw_qk", [16, 128, 1024], F32, kind="ExternalInput").ap()
    qkvw_v = nc.dram_tensor("qkvw_v", [8, 128, 1024], F32, kind="ExternalInput").ap()
    outw_in = nc.dram_tensor("outw_in", [8, 128, 1024], F32, kind="ExternalInput").ap()
    rqw_in = nc.dram_tensor("rqw_in", [4, 128, 512], F32, kind="ExternalInput").ap()
    rkvw_in = nc.dram_tensor("rkvw_in", [8, 128, 512], F32, kind="ExternalInput").ap()
    routw_in = nc.dram_tensor("routw_in", [8, 128, 256], F32, kind="ExternalInput").ap()
    wgt_in = nc.dram_tensor("wgt_in", [24, 128, 1024], I32, kind="ExternalInput").ap()
    wut_in = nc.dram_tensor("wut_in", [24, 128, 1024], I32, kind="ExternalInput").ap()
    wdt_in = nc.dram_tensor("wdt_in", [24, 128, 1024], I32, kind="ExternalInput").ap()
    bcol_in = nc.dram_tensor("bcol_in", [128, 40], F32, kind="ExternalInput").ap()
    brow_in = nc.dram_tensor("brow_in", [1, 2560], F32, kind="ExternalInput").ap()
    bdg_in = nc.dram_tensor("bdg_in", [128, 8, 32], F32, kind="ExternalInput").ap()
    ebc_in = nc.dram_tensor("ebc_in", [16, 8, 128], BF16, kind="ExternalInput").ap()
    ident_in = nc.dram_tensor("ident_in", [128, 128], BF16, kind="ExternalInput").ap()
    identf_in = nc.dram_tensor("identf_in", [64, 64], F32, kind="ExternalInput").ap()
    maskA_in = nc.dram_tensor("maskA_in", [128, 512], BF16, kind="ExternalInput").ap()
    maskB_in = nc.dram_tensor("maskB_in", [128, 512], BF16, kind="ExternalInput").ap()
    maskC_in = nc.dram_tensor("maskC_in", [128, 512], BF16, kind="ExternalInput").ap()
    maskF_in = nc.dram_tensor("maskF_in", [128, 512], BF16, kind="ExternalInput").ap()
    latmask_in = nc.dram_tensor("latmask_in", [64, 16], F32, kind="ExternalInput").ap()

    y_out = nc.dram_tensor("y_out", [T, D], F32, kind="ExternalOutput").ap()
    gin = nc.dram_tensor("gin", [8, 512], F32)
    gout = nc.dram_tensor("gout", [64, 512], F32)

    dbg = {}

    def dbg_out(name, shape):
        if name == "g0_local":
            for nm, shp in [("g0_local", [128, 8, 256]), ("g0_remote", [128, 8, 256]),
                            ("g0_mixed", [128, 8, 256]), ("g0_gate", [16, 256])]:
                if nm in debug and nm not in dbg:
                    dbg[nm] = nc.dram_tensor("dbg_" + nm, shp, F32, kind="ExternalOutput").ap()
            return dbg.get(name)
        if name in debug and name not in dbg:
            dbg[name] = nc.dram_tensor("dbg_" + name, list(shape), F32, kind="ExternalOutput").ap()
        return dbg.get(name)

    ctr = {}

    def inc(instr, sem, n=1):
        instr.then_inc(sem, n)
        ctr[sem.name] = ctr.get(sem.name, 0) + n
        return ctr[sem.name]

    def val(sem):
        return ctr.get(sem.name, 0)

    with ExitStack() as top:
        e = top.enter_context

        xn_fm = e(nc.sbuf_tensor("xn_fm", [128, 8, TB], BF16))
        s_tm = e(nc.sbuf_tensor("s_tm", [128, 8, 1024], F32))
        rq_sb = e(nc.sbuf_tensor("rq_sb", [64, 4, T], BF16))
        rkv_all = e(nc.sbuf_tensor("rkv_all", [64, 512], F32))
        ident = e(nc.sbuf_tensor("ident", [128, 128], BF16))
        identf = e(nc.sbuf_tensor("identf", [64, 64], F32))
        ones_bf = e(nc.sbuf_tensor("ones_bf", [128, 512], BF16))
        tiny_bf = e(nc.sbuf_tensor("tiny_bf", [128, 1], BF16))
        maskA = e(nc.sbuf_tensor("maskA", [128, 512], BF16))
        maskB = e(nc.sbuf_tensor("maskB", [128, 512], BF16))
        maskC = e(nc.sbuf_tensor("maskC", [128, 512], BF16))
        maskF = e(nc.sbuf_tensor("maskF", [128, 512], BF16))
        latmask = e(nc.sbuf_tensor("latmask", [64, 16], F32))
        bdg = e(nc.sbuf_tensor("bdg", [128, 8, 32], BF16))
        bcol = e(nc.sbuf_tensor("bcol", [128, 42], F32))

        P = [e(nc.psum_tensor(f"P{i}", [128, 512], F32)) for i in range(6)]

        s_sy = e(nc.semaphore("s_sy"))
        s_gp = e(nc.semaphore("s_gp"))
        s_cc = e(nc.semaphore("s_cc"))
        s_st = e(nc.semaphore("s_st"))
        s_pe = e(nc.semaphore("s_pe"))
        s_dv = e(nc.semaphore("s_dv"))
        s_sc = e(nc.semaphore("s_sc"))

        block = e(nc.Block())
        tn = nc.tensor
        v = nc.vector
        s = nc.scalar
        g = nc.gpsimd
        sy = nc.sync

        BSH = bcol[:, 40:41]
        EPS = bcol[:, 41:42]
        ZCOL = bcol[:, 36:37]

        pfree = {i: (s_pe, 0) for i in range(8)}

        def _cut():
            for sem in (s_sy, s_gp, s_dv, s_sc, s_pe, s_st, s_cc):
                if val(sem):
                    sy.wait_ge(sem, val(sem))


        inc(sy.dma_start(out=ident[:, :], in_=ident_in), s_sy, 16)
        inc(sy.dma_start(out=identf[:, :], in_=identf_in), s_sy, 16)
        inc(sy.dma_start(out=maskA[:, :], in_=maskA_in), s_sy, 16)
        inc(sy.dma_start(out=maskB[:, :], in_=maskB_in), s_sy, 16)
        inc(sy.dma_start(out=maskC[:, :], in_=maskC_in), s_sy, 16)
        inc(sy.dma_start(out=maskF[:, :], in_=maskF_in), s_sy, 16)
        inc(sy.dma_start(out=latmask[:, :], in_=latmask_in), s_sy, 16)
        inc(sy.dma_start(out=bcol[:, 0:40], in_=bcol_in), s_sy, 16)
        m_consts = val(s_sy)

        inc(g.dma_start(out=bdg[:, :, :], in_=bdg_in), s_gp, 16)
        m_consts_gp = val(s_gp)

        v.memset(ones_bf[:, :], 1.0)
        v.memset(tiny_bf[:, :], TINY)
        v.memset(bcol[:, 40:41], -BSHIFT)
        inc(v.memset(bcol[:, 41:42], 1e-5), s_dv, 1)
        m_memsets = val(s_dv)

        ln = ExitStack()
        x_sb = ln.enter_context(nc.sbuf_tensor("x_sb", [128, NT, 1024], F32))
        xn_tm = ln.enter_context(nc.sbuf_tensor("xn_tm", [128, 2, 1024], BF16))
        stats_all = ln.enter_context(nc.sbuf_tensor("stats_all", [128, NT, 12], F32))
        mv_all = ln.enter_context(nc.sbuf_tensor("mv_all", [128, 2 * NT], F32))
        rstd_all = ln.enter_context(nc.sbuf_tensor("rstd_all", [128, NT], F32))
        pooled = ln.enter_context(nc.sbuf_tensor("pooled", [128, 8, 8], F32))
        pooled_bf = ln.enter_context(nc.sbuf_tensor("pooled_bf", [128, 8, 8], BF16))
        rqw_sb = ln.enter_context(nc.sbuf_tensor("rqw_sb", [128, 4, 512], BF16))
        rkvw_sb = ln.enter_context(nc.sbuf_tensor("rkvw_sb", [128, 8, 512], BF16))
        rkv_own = ln.enter_context(nc.sbuf_tensor("rkv_own", [8, 512], F32))
        brkv_f = ln.enter_context(nc.sbuf_tensor("brkv_f", [1, 512], F32))
        brkv = ln.enter_context(nc.sbuf_tensor("brkv", [1, 512], BF16))
        brkv_lo = ln.enter_context(nc.sbuf_tensor("brkv_lo", [1, 512], BF16))
        PT = [ln.enter_context(nc.psum_tensor(f"PT{i}", [128, 512], F32)) for i in range(2)]

        sy.wait_ge(s_dv, m_memsets)
        for t in range(NT):
            inc(sy.dma_start(out=x_sb[:, t, :], in_=x_in[t * 128:(t + 1) * 128, :]), s_sy, 16)
        m_xload0 = m_consts

        for h in range(4):
            inc(g.dma_start(out=rqw_sb[:, h, :], in_=rqw_in[h]), s_gp, 16)
        for dc in range(8):
            inc(g.dma_start(out=rkvw_sb[:, dc, :], in_=rkvw_in[dc]), s_gp, 16)
        inc(g.dma_start(out=brkv_f[:, :], in_=brow_in[0:1, 2048:2560]), s_gp, 16)
        m_smallw = val(s_gp)

        v.wait_ge(s_gp, m_smallw)
        inc(v.tensor_copy(brkv[:, :], brkv_f[:, :]), s_dv, 1)
        v.wait_ge(s_dv, val(s_dv))
        inc(v.scalar_tensor_tensor(brkv_lo[:, :], brkv_f[:, :], 1.0, brkv[:, :],
                                   ALU.mult, ALU.subtract), s_dv, 1)
        m_bsplit0 = val(s_dv)

        if stage < 2:
            _cut()
            ln.close()
            return nc, dbg

        for t in range(NT):
            v.wait_ge(s_sy, m_xload0 + 16 * (t + 1))
            v.bn_stats(stats_all[:, t, 0:6], x_sb[:, t, 0:512])
            ins = v.bn_stats(stats_all[:, t, 6:12], x_sb[:, t, 512:1024])
        inc(ins, s_dv, 1)
        v.wait_ge(s_dv, val(s_dv))
        for t in range(NT):
            ins = v.bn_aggr(mv_all[:, 2 * t:2 * t + 2], stats_all[:, t, 0:12])
        inc(ins, s_dv, 1)
        m_aggr = val(s_dv)

        s.wait_ge(s_dv, m_aggr)
        s.wait_ge(s_dv, m_memsets)
        inc(s.activation(rstd_all[:, :], mv_all[:, 1:2 * NT:2], AF.Sqrt,
                         bias=EPS, scale=1.0), s_sc, 1)
        m_sqrt = val(s_sc)

        v.wait_ge(s_sc, m_sqrt)
        inc(v.reciprocal(rstd_all[:, :], rstd_all[:, :]), s_dv, 1)
        m_rstd = val(s_dv)

        if stage < 3:
            _cut()
            ln.close()
            return nc, dbg

        apply_m = [0] * NT
        xgrp_m = [[0, 0] for _ in range(NT)]
        evac_m = [[0, 0] for _ in range(NT)]
        for t in range(NT):
            b = t % 2

            if t == 0:
                v.wait_ge(s_dv, m_rstd)
            if t >= 2:
                v.wait_ge(s_pe, xgrp_m[t - 2][1])
            apply_m[t] = inc(
                v.tensor_scalar(xn_tm[:, b, :], x_sb[:, t, :],
                                mv_all[:, 2 * t:2 * t + 1], rstd_all[:, t:t + 1],
                                ALU.subtract, ALU.mult), s_dv, 1)

            tn.wait_ge(s_dv, apply_m[t])
            for gg in range(2):
                if t >= 1:
                    tn.wait_ge(s_dv, evac_m[t - 1][gg])
                for k in range(4):
                    dc = gg * 4 + k
                    ins = tn.matmul(PT[gg][:, k * 128:(k + 1) * 128],
                                    xn_tm[:, b, dc * 128:(dc + 1) * 128], ident[:, :],
                                    start=True, stop=True)
                xgrp_m[t][gg] = inc(ins, s_pe, 1)

            for gg in range(2):
                v.wait_ge(s_pe, xgrp_m[t][gg])
                evac_m[t][gg] = inc(
                    v.tensor_copy(xn_fm[:, 4 * gg:4 * gg + 4, t * 128:(t + 1) * 128],
                                  PT[gg][:, :].rearrange("p (k n) -> p k n", k=4)),
                    s_dv, 1)
        m_ln_done = evac_m[NT - 1][1]

        if dbg_out("xn", [128, 8, 256]) is not None:
            g.wait_ge(s_dv, m_ln_done)
            inc(g.dma_start(out=dbg["xn"], in_=xn_fm[:, :, 0:256]), s_gp, 16)

        if stage < 4:
            _cut()
            ln.close()
            return nc, dbg

        v.wait_ge(s_dv, m_ln_done)
        for dc in range(8):
            for j in range(8):
                ins = v.tensor_reduce(pooled[:, dc, j:j + 1],
                                      xn_fm[:, dc, HALO + 256 * j:HALO + 256 * (j + 1)],
                                      AX.X, ALU.add)
        inc(ins, s_dv, 1)
        v.wait_ge(s_dv, val(s_dv))
        inc(v.tensor_scalar(pooled_bf[:, :, :], pooled[:, :, :], 1.0 / 256.0, None,
                            ALU.mult), s_dv, 1)
        m_pooled = val(s_dv)

        tn.wait_ge(s_dv, m_pooled)
        tn.wait_ge(s_dv, m_memsets)
        tn.wait_ge(s_gp, m_smallw)
        tn.wait_ge(s_gp, m_consts_gp)
        tn.wait_ge(s_dv, m_bsplit0)
        for dc in range(8):
            tn.matmul(P[0][0:8, :], pooled_bf[:, dc, :], rkvw_sb[:, dc, :],
                      start=(dc == 0), stop=False)
        tn.matmul(P[0][0:8, :], ones_bf[0:1, 0:8], brkv[0:1, :],
                  start=False, stop=False)
        inc(tn.matmul(P[0][0:8, :], ones_bf[0:1, 0:8], brkv_lo[0:1, :],
                      start=False, stop=True), s_pe, 1)
        m_rkvmm = val(s_pe)

        v.wait_ge(s_pe, m_rkvmm)
        inc(v.tensor_copy(rkv_own[:, :], P[0][0:8, :]), s_dv, 1)
        m_rkvevac = val(s_dv)
        pfree[0] = (s_dv, m_rkvevac)

        if stage < 5:
            _cut()
            ln.close()
            return nc, dbg

        for h in range(4):
            for gg in range(4):
                bank = (h * 4 + gg) % 2

                tn.wait_ge(*pfree[bank])
                tn.wait_ge(s_dv, m_ln_done)
                for dc in range(8):
                    ins = tn.matmul(P[bank][0:64, :],
                                    rqw_sb[:, h, dc * 64:(dc + 1) * 64],
                                    xn_fm[:, dc, HALO + gg * 512:HALO + (gg + 1) * 512],
                                    start=(dc == 0), stop=(dc == 7))
                inc(ins, s_pe, 1)
                rq_mm = val(s_pe)

                s.wait_ge(s_pe, rq_mm)
                m = inc(s.activation(rq_sb[:, h, gg * 512:(gg + 1) * 512], P[bank][0:64, :],
                                     AF.Identity, bias=bcol[0:64, 32 + h:33 + h], scale=1.0),
                        s_sc, 1)
                pfree[bank] = (s_sc, m)
        m_rq_done = val(s_sc)

        if stage < 6:
            _cut()
            ln.close()
            return nc, dbg

        sy.wait_ge(s_dv, m_rkvevac)
        inc(sy.dma_start(out=gin[:, :], in_=rkv_own[:, :]), s_sy, 16)
        m_gin = val(s_sy)

        g.wait_ge(s_sy, m_gin)
        inc(g.collective_compute(
            "AllGather", ALU.bypass, replica_groups=[list(range(NCORES))],
            ins=[gin.ap().opt()], outs=[gout.ap().opt()]), s_cc, 1)
        m_cc = val(s_cc)

        sy.wait_ge(s_cc, m_cc)
        inc(sy.dma_start(out=rkv_all[:, :], in_=gout[:, :]), s_sy, 16)
        m_rkvall = val(s_sy)

        if dbg_out("rkv_all", [64, 512]) is not None:
            inc(sy.dma_start(out=dbg["rkv_all"], in_=rkv_all[:, :]), s_st, 16)
        if dbg_out("rq", [64, 4, 512]) is not None:
            g.wait_ge(s_sc, m_rq_done)
            inc(g.dma_start(out=dbg["rq"], in_=rq_sb[:, :, 0:512]), s_gp, 16)

        ln.close()

        arena_free = [(s_sc, m_rq_done), (s_dv, m_pooled), (s_sy, m_gin)]
        m_stm_free = [(s_pe, 0)] * 8

        for half in range(2):
            if "qkv" not in phases:
                break
            kbase = half * 896
            qbase = HALO + half * 1024

            att = ExitStack()
            q_fm = att.enter_context(nc.sbuf_tensor(f"q_fm{half}", [128, 8, 1024], BF16))
            k_fm = att.enter_context(nc.sbuf_tensor(f"k_fm{half}", [128, 8, 1280], BF16))
            v_tm = att.enter_context(nc.sbuf_tensor(f"v_tm{half}", [128, 10, 1024], BF16))
            ow_sb = att.enter_context(nc.sbuf_tensor(f"ow_sb{half}", [128, 8, 1024], BF16))
            rout_sb = att.enter_context(nc.sbuf_tensor(f"rout_sb{half}", [128, 8, 256], BF16))
            rkT = att.enter_context(nc.sbuf_tensor(f"rkT{half}", [64, 4, 64], BF16))
            rkbf = att.enter_context(nc.sbuf_tensor(f"rkbf{half}", [64, 256], BF16))
            rv_aug = att.enter_context(nc.sbuf_tensor(f"rv_aug{half}", [64, 4, 64], BF16))
            otcol = att.enter_context(nc.sbuf_tensor(f"otcol{half}", [65, 1], BF16))
            bvob = att.enter_context(nc.sbuf_tensor(f"bvob{half}", [1, 2048], BF16))
            bvob_lo = att.enter_context(nc.sbuf_tensor(f"bvlo{half}", [1, 2048], BF16))
            ebc = att.enter_context(nc.sbuf_tensor(f"ebc{half}", [16, 8, 128], BF16))
            P6 = att.enter_context(nc.psum_tensor(f"P6_{half}", [128, 512], F32))
            P7 = att.enter_context(nc.psum_tensor(f"P7_{half}", [128, 512], F32))
            pfree[6] = (s_pe, 0)
            pfree[7] = (s_pe, 0)

            wqkv = ExitStack()
            wqk_buf = wqkv.enter_context(nc.sbuf_tensor(f"wqk{half}", [128, 2, 1024], BF16))
            bvf = wqkv.enter_context(nc.sbuf_tensor(f"bvf{half}", [1, 2048], F32))
            wv_sb = wqkv.enter_context(nc.sbuf_tensor(f"wv{half}", [128, 8, 1024], BF16))

            ld_qk = [0] * 16
            mm_qk = [0] * 16

            for sem, vv in arena_free:
                g.wait_ge(sem, vv)
            inc(g.dma_start(out=bvf[:, :], in_=brow_in[0:1, 0:2048]), s_gp, 16)
            inc(g.dma_start(out=ebc[:, :, :], in_=ebc_in), s_gp, 16)
            m_bvf = val(s_gp)

            v.wait_ge(s_gp, m_bvf)
            inc(v.tensor_copy(bvob[:, :], bvf[:, :]), s_dv, 1)
            v.wait_ge(s_dv, val(s_dv))
            inc(v.scalar_tensor_tensor(bvob_lo[:, :], bvf[:, :], 1.0, bvob[:, :],
                                       ALU.mult, ALU.subtract), s_dv, 1)
            m_bsplit = val(s_dv)

            for j in range(2):
                ld_qk[j] = inc(g.dma_start(out=wqk_buf[:, j, :], in_=qkvw_qk[j]), s_gp, 16)

            for j in range(16):
                is_q = j < 8
                oc = j if is_q else j - 8
                ngroups = [(0, 512), (512, 1024)] if is_q else [(0, 512), (512, 1024), (1024, 1280)]
                base = qbase if is_q else kbase
                dest = q_fm if is_q else k_fm
                banks = [0, 1, 4] if (j % 2 == 0) else [2, 3, 4]
                grp_mm = [0] * len(ngroups)

                tn.wait_ge(s_gp, ld_qk[j])
                tn.wait_ge(s_dv, m_ln_done)
                for gi, (a0, a1) in enumerate(ngroups):
                    bank = banks[gi]
                    tn.wait_ge(*pfree[bank])
                    for dc in range(8):
                        ins = tn.matmul(P[bank][:, 0:a1 - a0],
                                        wqk_buf[:, j % 2, dc * 128:(dc + 1) * 128],
                                        xn_fm[:, dc, base + a0:base + a1],
                                        start=(dc == 0), stop=(dc == 7))
                    grp_mm[gi] = inc(ins, s_pe, 1)
                mm_qk[j] = grp_mm[-1]

                colidx = oc if j < 8 else 8 + oc
                for gi, (a0, a1) in enumerate(ngroups):
                    bank = banks[gi]
                    s.wait_ge(s_pe, grp_mm[gi])
                    m = inc(s.activation(dest[:, oc, a0:a1], P[bank][:, 0:a1 - a0],
                                         AF.Identity, bias=bcol[:, colidx:colidx + 1],
                                         scale=1.0), s_sc, 1)
                    pfree[bank] = (s_sc, m)

                if j + 2 < 16:
                    g.wait_ge(s_pe, mm_qk[j])
                    ld_qk[j + 2] = inc(g.dma_start(out=wqk_buf[:, j % 2, :],
                                                   in_=qkvw_qk[j + 2]), s_gp, 16)

            for dc in range(8):
                inc(g.dma_start(out=wv_sb[:, dc, :], in_=qkvw_v[dc]), s_gp, 16)
            m_wv = val(s_gp)

            for dc in range(8):
                inc(g.dma_start(out=ow_sb[:, dc, :], in_=outw_in[dc]), s_gp, 16)
            for oc in range(8):
                inc(g.dma_start(out=rout_sb[:, oc, :], in_=routw_in[oc]), s_gp, 16)
            m_ow = val(s_gp)

            for t in range(10):
                for gi in range(2):
                    bank = 4 + (t * 2 + gi) % 2
                    if bank == 4 + 1:
                        bank = 5

                    tn.wait_ge(s_gp, m_wv)
                    tn.wait_ge(s_dv, m_bsplit)
                    tn.wait_ge(*pfree[bank])
                    for dc in range(8):
                        tn.matmul(P[bank][:, :],
                                  xn_fm[:, dc, kbase + t * 128:kbase + (t + 1) * 128],
                                  wv_sb[:, dc, gi * 512:(gi + 1) * 512],
                                  start=(dc == 0), stop=False)
                    tn.matmul(P[bank][:, :], ones_bf[0:1, 0:128],
                              bvob[0:1, gi * 512:(gi + 1) * 512],
                              start=False, stop=False)
                    inc(tn.matmul(P[bank][:, :], ones_bf[0:1, 0:128],
                                  bvob_lo[0:1, gi * 512:(gi + 1) * 512],
                                  start=False, stop=True), s_pe, 1)
                    mmv = val(s_pe)

                    v.wait_ge(s_pe, mmv)
                    m = inc(v.tensor_copy(v_tm[:, t, gi * 512:(gi + 1) * 512],
                                          P[bank][:, :]), s_dv, 1)
                    pfree[bank] = (s_dv, m)
            m_v_done = val(s_dv)

            if half == 0 and dbg_out("qkv", [128, 3, 512]) is not None:
                g.wait_ge(s_sc, val(s_sc))
                g.wait_ge(s_dv, m_v_done)
                inc(g.dma_start(out=dbg["qkv"][:, 0:1, :], in_=q_fm[:, 0:1, 0:512]), s_gp, 16)
                inc(g.dma_start(out=dbg["qkv"][:, 1:2, :], in_=k_fm[:, 0:1, 0:512]), s_gp, 16)
                inc(g.dma_start(out=dbg["qkv"][:, 2:3, :], in_=v_tm[:, 0:1, 0:512]),
                    s_gp, 16)

            wqkv.close()

            if "attn" not in phases:
                att.close()
                arena_free = [(s_dv, m_v_done), (s_sc, val(s_sc))]
                continue

            scr = ExitStack()
            pt_sb = scr.enter_context(nc.sbuf_tensor(f"pt{half}", [128, 2, 512], BF16))
            local_n = scr.enter_context(nc.sbuf_tensor(f"locn{half}", [128, 8, 256], BF16))
            remote_n = scr.enter_context(nc.sbuf_tensor(f"remn{half}", [128, 8, 256], BF16))
            mixed = scr.enter_context(nc.sbuf_tensor(f"mixed{half}", [128, 8, 256], BF16))
            avst = scr.enter_context(nc.sbuf_tensor(f"avst{half}", [128, 256], F32))
            rr = scr.enter_context(nc.sbuf_tensor(f"rr{half}", [1, 2048], BF16))
            rrL = scr.enter_context(nc.sbuf_tensor(f"rrL{half}", [1, 512], BF16))
            rrt = scr.enter_context(nc.sbuf_tensor(f"rrt{half}", [1, 512], F32))
            ptl = scr.enter_context(nc.sbuf_tensor(f"ptl{half}", [65, 512], BF16))
            latst = scr.enter_context(nc.sbuf_tensor(f"latst{half}", [128, 256], F32))
            latout = scr.enter_context(nc.sbuf_tensor(f"latout{half}", [128, 2, 256], BF16))
            gsb = scr.enter_context(nc.sbuf_tensor(f"gsb{half}", [16, 256], BF16))
            gtmp = scr.enter_context(nc.sbuf_tensor(f"gtmp{half}", [16, 256], F32))
            dtmp = scr.enter_context(nc.sbuf_tensor(f"dtmp{half}", [128, 256], BF16))

            v.wait_ge(s_sy, m_rkvall)
            inc(v.tensor_copy(rkbf[:, :], rkv_all[:, 0:256]), s_dv, 1)
            m_rkbf = val(s_dv)

            tn.wait_ge(s_dv, m_rkbf)
            tn.wait_ge(*pfree[6])
            for h in range(4):
                ins = tn.matmul(P6[0:64, h * 64:(h + 1) * 64],
                                rkbf[:, h * 64:(h + 1) * 64], ident[0:64, 0:64],
                                start=True, stop=True)
            inc(ins, s_pe, 1)
            m_rkt_mm = val(s_pe)

            v.wait_ge(s_pe, m_rkt_mm)
            m = inc(v.tensor_copy(rkT[:, :, :],
                                  P6[0:64, 0:256].rearrange("p (h n) -> p h n", h=4)),
                    s_dv, 1)
            pfree[6] = (s_dv, m)
            for h in range(4):
                v.tensor_copy(rv_aug[:, h, :],
                              rkv_all[:, 256 + h * 64:256 + (h + 1) * 64])
            v.memset(otcol[0:64, :], 1.0)
            v.memset(otcol[64:65, :], TINY)
            inc(v.memset(ptl[64:65, :], 1.0), s_dv, 1)
            m_lat_prep = val(s_dv)

            exp_m = {}
            av_m = {}
            avmult_m = {}
            recip_m = {}
            bc_m = {}
            lat_ms = {}
            m_gl_prev = 0
            m_mix_prev = 0
            m_oww_prev = 0

            for tq in range(attn_tq):
                w0 = tq + half
                tcol = (tq % 2) * 128
                use_F = (half == 0 and tq == 0)
                m0 = maskF if use_F else (maskA if tq % 2 == 0 else maskB)

                for bb in range(4):
                    avbank = 4 + bb % 2
                    heads = [4 * bb, 4 * bb + 2, 4 * bb + 1, 4 * bb + 3]

                    if (tq, bb) == (0, 0):
                        tn.wait_ge(s_dv, m_v_done)
                        tn.wait_ge(s_sc, m_rq_done)
                    for b_ in range(4):
                        tn.wait_ge(*pfree[b_])
                    ins = None
                    for s_i, hh in enumerate(heads):
                        r0 = 64 * (hh % 2)
                        dch = hh // 2
                        cb = [0, 1] if s_i < 2 else [2, 3]
                        scol = (s_i % 2) * 128
                        for c in range(2):
                            ko = 128 * (w0 + c)
                            ins = tn.matmul(P[cb[c]][:, scol:scol + 128],
                                            k_fm[r0:r0 + 64, dch, ko:ko + 128],
                                            q_fm[r0:r0 + 64, dch, tq * 128:(tq + 1) * 128],
                                            start=True, stop=True)
                    inc(ins, s_pe, 1)
                    m_sc_mm = val(s_pe)

                    if attn_sub < 2:
                        exp_m[(tq, bb)] = val(s_sc)
                        av_m[(tq, bb)] = val(s_pe)
                        recip_m[(tq, bb)] = val(s_dv)
                        bc_m[(tq, bb)] = val(s_pe)
                        avmult_m[(tq, bb)] = val(s_dv)
                        continue

                    s.wait_ge(s_pe, m_sc_mm)
                    if (tq, bb) != (0, 0):
                        prev = (tq, bb - 1) if bb > 0 else (tq - 1, 3)
                        s.wait_ge(s_pe, av_m[prev])
                    s.activation(pt_sb[:, 0, 0:256], P[0][:, 0:256], AF.Exp,
                                 bias=BSH, scale=0.125)
                    s.activation(pt_sb[:, 0, 256:512], P[2][:, 0:256], AF.Exp,
                                 bias=BSH, scale=0.125)
                    s.activation(pt_sb[:, 1, 0:256], P[1][:, 0:256], AF.Exp,
                                 bias=BSH, scale=0.125)
                    inc(s.activation(pt_sb[:, 1, 256:512], P[3][:, 0:256], AF.Exp,
                                     bias=BSH, scale=0.125), s_sc, 1)
                    exp_m[(tq, bb)] = val(s_sc)
                    for b_ in range(4):
                        pfree[b_] = (s_sc, exp_m[(tq, bb)])

                    if attn_sub < 3:
                        av_m[(tq, bb)] = val(s_pe)
                        recip_m[(tq, bb)] = val(s_dv)
                        bc_m[(tq, bb)] = val(s_pe)
                        avmult_m[(tq, bb)] = val(s_dv)
                        continue

                    v.wait_ge(s_sc, exp_m[(tq, bb)])
                    v.tensor_tensor(pt_sb[:, 0, :], pt_sb[:, 0, :], m0[:, :], ALU.mult)
                    inc(v.tensor_tensor(pt_sb[:, 1, :], pt_sb[:, 1, :], maskC[:, :],
                                        ALU.mult), s_dv, 1)
                    m_mask = val(s_dv)

                    if attn_sub < 4:
                        av_m[(tq, bb)] = val(s_pe)
                        recip_m[(tq, bb)] = val(s_dv)
                        bc_m[(tq, bb)] = val(s_pe)
                        avmult_m[(tq, bb)] = val(s_dv)
                        continue

                    tn.wait_ge(s_dv, m_mask)
                    key2 = (tq, bb - 2) if bb >= 2 else (tq - 1, bb + 2)
                    if key2 in avmult_m:
                        tn.wait_ge(s_dv, avmult_m[key2])
                    prevd = (tq, bb - 1) if bb > 0 else (tq - 1, 3)
                    if prevd in recip_m:
                        tn.wait_ge(s_dv, recip_m[prevd])
                    else:
                        tn.wait_ge(*pfree[6])
                    for s_i, hh in enumerate(heads):
                        r0 = 64 * (hh % 2)
                        scol = (s_i % 2) * 128
                        for c in range(2):
                            tn.matmul(P[avbank][r0:r0 + 64, scol:scol + 128],
                                      v_tm[:, w0 + c, hh * 64:(hh + 1) * 64],
                                      pt_sb[:, c, s_i * 128:(s_i + 1) * 128],
                                      start=(c == 0), stop=(c == 1))
                    tn.matmul(P6[0:1, :], ones_bf[:, 0:1], pt_sb[:, 0, :],
                              start=True, stop=False)
                    inc(tn.matmul(P6[0:1, :], ones_bf[:, 0:1], pt_sb[:, 1, :],
                                  start=False, stop=True), s_pe, 1)
                    av_m[(tq, bb)] = val(s_pe)

                    if attn_sub < 5:
                        recip_m[(tq, bb)] = val(s_dv)
                        bc_m[(tq, bb)] = val(s_pe)
                        avmult_m[(tq, bb)] = val(s_dv)
                        continue

                    v.wait_ge(s_pe, av_m[(tq, bb)])
                    if (tq - 1, bb) in bc_m:
                        v.wait_ge(s_pe, bc_m[(tq - 1, bb)])
                    v.tensor_scalar(rrt[0:1, :], P6[0:1, :], TINY, None, ALU.add)
                    with nc.allow_low_precision(reason="softmax recip as one bf16 factor"):
                        inc(v.reciprocal(rr[0:1, bb * 512:(bb + 1) * 512], rrt[0:1, :]),
                            s_dv, 1)
                    recip_m[(tq, bb)] = val(s_dv)

                    if attn_sub < 6:
                        bc_m[(tq, bb)] = val(s_pe)
                        avmult_m[(tq, bb)] = val(s_dv)
                        continue

                    tn.wait_ge(s_dv, recip_m[(tq, bb)])
                    key2 = (tq, bb - 2) if bb >= 2 else (tq - 1, bb + 2)
                    if key2 in avmult_m:
                        tn.wait_ge(s_dv, avmult_m[key2])
                    if bb < 2 and (tq - 1) in lat_ms:
                        tn.wait_ge(s_dv, lat_ms[tq - 1]["mult"])
                    ins = None
                    for s_i, hh in enumerate(heads):
                        r0 = 64 * (hh % 2)
                        pcol = (bb % 2) * 256 + (s_i % 2) * 128
                        ins = tn.matmul(P7[r0:r0 + 64, pcol:pcol + 128],
                                        ones_bf[0:1, 0:64],
                                        rr[0:1, bb * 512 + s_i * 128:bb * 512 + (s_i + 1) * 128],
                                        start=True, stop=True)
                    inc(ins, s_pe, 1)
                    bc_m[(tq, bb)] = val(s_pe)

                    if attn_sub < 7:
                        avmult_m[(tq, bb)] = val(s_dv)
                        continue

                    s.wait_ge(s_pe, av_m[(tq, bb)])
                    key1 = (tq, bb - 1) if bb > 0 else (tq - 1, 3)
                    if key1 in avmult_m:
                        s.wait_ge(s_dv, avmult_m[key1])
                    inc(s.copy(avst[:, 0:256], P[avbank][:, 0:256]), s_sc, 1)
                    m_avevac = val(s_sc)
                    pfree[avbank] = (s_sc, m_avevac)

                    if attn_sub < 8:
                        avmult_m[(tq, bb)] = val(s_dv)
                        continue

                    v.wait_ge(s_pe, bc_m[(tq, bb)])
                    v.wait_ge(s_sc, m_avevac)
                    if m_gl_prev:
                        v.wait_ge(s_pe, m_gl_prev)
                    if m_mix_prev:
                        v.wait_ge(s_dv, m_mix_prev)
                    ins = None
                    for s_i, hh in enumerate(heads):
                        r0 = 64 * (hh % 2)
                        pcol = (bb % 2) * 256 + (s_i % 2) * 128
                        ins = v.tensor_tensor(local_n[r0:r0 + 64, hh // 2, tcol:tcol + 128],
                                              avst[r0:r0 + 64,
                                                   (s_i % 2) * 128:(s_i % 2) * 128 + 128],
                                              P7[r0:r0 + 64, pcol:pcol + 128], ALU.mult)
                    inc(ins, s_dv, 1)
                    avmult_m[(tq, bb)] = val(s_dv)

                if not attn_lat:
                    continue
                lm = {}

                tn.wait_ge(*pfree[0])
                tn.wait_ge(s_dv, m_lat_prep)
                ins = None
                for h in range(4):
                    ins = tn.matmul(P[0][0:64, h * 128:(h + 1) * 128], rkT[:, h, :],
                                    rq_sb[:, h, half * 1024 + tq * 128:half * 1024 + (tq + 1) * 128],
                                    start=True, stop=True)
                inc(ins, s_pe, 1)
                lm["lsc"] = val(s_pe)

                s.wait_ge(s_pe, lm["lsc"])
                if (tq - 1) in lat_ms:
                    s.wait_ge(s_pe, lat_ms[tq - 1]["av"])
                inc(s.activation(ptl[0:64, :], P[0][0:64, :], AF.Exp,
                                 bias=BSH[0:64, :], scale=0.125), s_sc, 1)
                lm["exp"] = val(s_sc)
                pfree[0] = (s_sc, lm["exp"])

                v.wait_ge(s_sc, lm["exp"])
                inc(v.tensor_scalar(ptl[0:64, :], ptl[0:64, :],
                                    latmask[:, half * 8 + tq:half * 8 + tq + 1], None,
                                    ALU.mult), s_dv, 1)
                lm["mask"] = val(s_dv)

                tn.wait_ge(s_dv, lm["mask"])
                tn.wait_ge(*pfree[1])
                tn.wait_ge(*pfree[3])
                if (tq - 1) in lat_ms:
                    tn.wait_ge(s_dv, lat_ms[tq - 1]["rec"])
                for h in range(4):
                    r0 = 64 * (h % 2)
                    sl = (256 if h % 2 else 0) + (h // 2) * 128
                    tn.matmul(P[1][r0:r0 + 64, sl:sl + 128], rv_aug[0:64, h, 0:64],
                              ptl[0:64, h * 128:(h + 1) * 128], start=True, stop=True)
                ins = None
                for h in range(4):
                    ins = tn.matmul(P[3][0:1, h * 128:(h + 1) * 128], otcol[0:65, 0:1],
                                    ptl[0:65, h * 128:(h + 1) * 128], start=True, stop=True)
                inc(ins, s_pe, 1)
                lm["av"] = val(s_pe)

                v.wait_ge(s_pe, lm["av"])
                if (tq - 1) in lat_ms:
                    v.wait_ge(s_pe, lat_ms[tq - 1]["bc"])
                with nc.allow_low_precision(reason="softmax recip as one bf16 factor"):
                    inc(v.reciprocal(rrL[0:1, 0:512], P[3][0:1, :]), s_dv, 1)
                lm["rec"] = val(s_dv)
                pfree[3] = (s_dv, lm["rec"])

                tn.wait_ge(s_dv, lm["rec"])
                tn.wait_ge(s_dv, avmult_m[(tq, 2)])
                if (tq - 1) in lat_ms:
                    tn.wait_ge(s_dv, lat_ms[tq - 1]["mult"])
                ins = None
                for h in range(4):
                    r0 = 64 * (h % 2)
                    pcol = (h // 2) * 128
                    ins = tn.matmul(P7[r0:r0 + 64, pcol:pcol + 128], ones_bf[0:1, 0:64],
                                    rrL[0:1, h * 128:(h + 1) * 128], start=True, stop=True)
                inc(ins, s_pe, 1)
                lm["bc"] = val(s_pe)

                s.wait_ge(s_pe, lm["av"])
                if (tq - 1) in lat_ms:
                    s.wait_ge(s_dv, lat_ms[tq - 1]["mult"])
                s.copy(latst[0:64, 0:256], P[1][0:64, 0:256])
                inc(s.copy(latst[64:128, 0:256], P[1][64:128, 256:512]), s_sc, 1)
                lm["evac"] = val(s_sc)
                pfree[1] = (s_sc, lm["evac"])

                v.wait_ge(s_pe, lm["bc"])
                v.wait_ge(s_sc, lm["evac"])
                ins = None
                for h in range(4):
                    r0 = 64 * (h % 2)
                    ins = v.tensor_tensor(latout[r0:r0 + 64, h // 2, tcol:tcol + 128],
                                          latst[r0:r0 + 64, (h // 2) * 128:(h // 2) * 128 + 128],
                                          P7[r0:r0 + 64, (h // 2) * 128:(h // 2) * 128 + 128],
                                          ALU.mult)
                inc(ins, s_dv, 1)
                lm["mult"] = val(s_dv)
                lat_ms[tq] = lm

                if tq % 2 == 1 and attn_grp:
                    grp_rout_mm = {}
                    for oc in range(8):
                        bank = 2 + oc % 2

                        tn.wait_ge(s_dv, lat_ms[tq]["mult"])
                        tn.wait_ge(s_dv, lat_ms[tq - 1]["mult"])
                        tn.wait_ge(*pfree[bank])
                        for dc in range(2):
                            ins = tn.matmul(P[bank][:, 0:256],
                                            rout_sb[:, oc, dc * 128:(dc + 1) * 128],
                                            latout[:, dc, 0:256], start=(dc == 0), stop=(dc == 1))
                        inc(ins, s_pe, 1)
                        mmv = val(s_pe)

                        s.wait_ge(s_pe, mmv)
                        if m_mix_prev:
                            s.wait_ge(s_dv, m_mix_prev)
                        m = inc(s.activation(remote_n[:, oc, :], P[bank][:, 0:256],
                                             AF.Identity, bias=bcol[:, 24 + oc:25 + oc],
                                             scale=1.0), s_sc, 1)
                        pfree[bank] = (s_sc, m)
                    m_remote = val(s_sc)

                    tn.wait_ge(s_sc, m_remote)
                    tn.wait_ge(s_dv, avmult_m[(tq, 3)])
                    tn.wait_ge(*pfree[5])
                    tn.wait_ge(s_dv, lat_ms[tq]["rec"])
                    for dc in range(8):
                        tn.matmul(P[5][0:16, 0:256], bdg[:, dc, 0:16], local_n[:, dc, :],
                                  start=(dc == 0), stop=False)
                    for dc in range(8):
                        ins = tn.matmul(P[5][0:16, 0:256], bdg[:, dc, 16:32], remote_n[:, dc, :],
                                        start=False, stop=(dc == 7))
                    inc(ins, s_pe, 1)
                    m_gl = val(s_pe)
                    m_gl_prev = m_gl

                    s.wait_ge(s_pe, m_gl)
                    inc(s.activation(gtmp[:, :], P[5][0:16, 0:256], AF.Exp,
                                     bias=ZCOL[0:16, :], scale=-1.0), s_sc, 1)
                    m_gexp = val(s_sc)
                    pfree[5] = (s_sc, m_gexp)

                    v.wait_ge(s_sc, m_gexp)
                    inc(v.tensor_scalar(gtmp[:, :], gtmp[:, :], 1.0, None, ALU.add), s_dv, 1)
                    v.wait_ge(s_dv, val(s_dv))
                    inc(v.reciprocal(gtmp[:, :], gtmp[:, :]), s_dv, 1)
                    v.wait_ge(s_dv, val(s_dv))
                    inc(v.tensor_copy(gsb[:, :], gtmp[:, :]), s_dv, 1)
                    m_gate = val(s_dv)

                    m_mix_rounds = []
                    for rnd in range(2):
                        tn.wait_ge(s_dv, m_gate)
                        tn.wait_ge(*pfree[0])
                        tn.wait_ge(*pfree[1])
                        for k in range(4):
                            dc = rnd * 4 + k
                            bank = k % 2
                            ins = tn.matmul(P[bank][:, (k // 2) * 256:(k // 2) * 256 + 256],
                                            ebc[:, dc, :], gsb[:, :], start=True, stop=True)
                        inc(ins, s_pe, 1)
                        m_gbc = val(s_pe)

                        v.wait_ge(s_pe, m_gbc)
                        if m_oww_prev:
                            v.wait_ge(s_pe, m_oww_prev)
                        for k in range(4):
                            dc = rnd * 4 + k
                            bank = k % 2
                            v.tensor_tensor(dtmp[:, :], local_n[:, dc, :],
                                            remote_n[:, dc, :], ALU.subtract)
                            v.tensor_tensor(dtmp[:, :], dtmp[:, :],
                                            P[bank][:, (k // 2) * 256:(k // 2) * 256 + 256],
                                            ALU.mult)
                            ins = v.tensor_tensor(mixed[:, dc, :], dtmp[:, :],
                                                  remote_n[:, dc, :], ALU.add)
                        inc(ins, s_dv, 1)
                        m_mix_rounds.append(val(s_dv))
                        pfree[0] = (s_dv, m_mix_rounds[-1])
                        pfree[1] = (s_dv, m_mix_rounds[-1])
                    m_mix_done = m_mix_rounds[-1]
                    m_mix_prev = m_mix_done

                    for tt in range(2):
                        tloc = (tq - 1) + tt
                        bank0, bank1 = (2, 3) if tt == 0 else (4, 5)

                        tn.wait_ge(s_dv, m_mix_done)
                        tn.wait_ge(s_gp, m_ow)
                        tn.wait_ge(*pfree[bank0])
                        tn.wait_ge(*pfree[bank1])
                        for dc in range(8):
                            tn.matmul(P[bank0][:, :], mixed[:, dc, tt * 128:(tt + 1) * 128],
                                      ow_sb[:, dc, 0:512], start=(dc == 0), stop=False)
                        tn.matmul(P[bank0][:, :], ones_bf[0:1, 0:128],
                                  bvob[0:1, 1024:1536], start=False, stop=False)
                        tn.matmul(P[bank0][:, :], ones_bf[0:1, 0:128],
                                  bvob_lo[0:1, 1024:1536], start=False, stop=True)
                        for dc in range(8):
                            tn.matmul(P[bank1][:, :], mixed[:, dc, tt * 128:(tt + 1) * 128],
                                      ow_sb[:, dc, 512:1024], start=(dc == 0), stop=False)
                        tn.matmul(P[bank1][:, :], ones_bf[0:1, 0:128],
                                  bvob[0:1, 1536:2048], start=False, stop=False)
                        inc(tn.matmul(P[bank1][:, :], ones_bf[0:1, 0:128],
                                      bvob_lo[0:1, 1536:2048], start=False, stop=True), s_pe, 1)
                        m_oww = val(s_pe)

                        s.wait_ge(s_pe, m_oww)
                        if m_stm_free[tloc][1] > 0:
                            s.wait_ge(*m_stm_free[tloc])
                        s.copy(s_tm[:, tloc, 0:512], P[bank0][:, :])
                        m = inc(s.copy(s_tm[:, tloc, 512:1024], P[bank1][:, :]), s_sc, 1)
                        pfree[bank0] = (s_sc, m)
                        pfree[bank1] = (s_sc, m)
                    m_oww_prev = m_oww

                    if half == 0 and tq == 1:
                        if dbg_out("g0_local", [128, 8, 256]) is not None:
                            g.wait_ge(s_dv, m_mix_done)
                            inc(g.dma_start(out=dbg["g0_local"], in_=local_n[:, :, :]), s_gp, 16)
                            inc(g.dma_start(out=dbg["g0_remote"], in_=remote_n[:, :, :]), s_gp, 16)
                            inc(g.dma_start(out=dbg["g0_mixed"], in_=mixed[:, :, :]), s_gp, 16)
                            inc(g.dma_start(out=dbg["g0_gate"], in_=gsb[:, :]), s_gp, 16)

            if attn_tq < 8 or not attn_lat or not attn_grp or attn_sub < 8:
                _cut()
                scr.close()
                att.close()
                return nc, dbg

            m_attn_sc = val(s_sc)
            m_attn_dv = val(s_dv)

            if half == 0 and dbg_out("s_tm", [128, 8, 1024]) is not None:
                sy.wait_ge(s_sc, m_attn_sc)
                inc(sy.dma_start(out=dbg["s_tm"], in_=s_tm[:, :, :]), s_st, 16)
            if half == 0 and dbg_out("local_n", [128, 8, 256]) is not None:
                g.wait_ge(s_dv, m_attn_dv)
                inc(g.dma_start(out=dbg["local_n"], in_=local_n[:, :, :]), s_gp, 16)
            if half == 0 and dbg_out("remote_n", [128, 8, 256]) is not None:
                g.wait_ge(s_dv, m_attn_dv)
                inc(g.dma_start(out=dbg["remote_n"], in_=remote_n[:, :, :]), s_gp, 16)

            scr.close()
            att.close()

            if "ffn" not in phases:
                arena_free = [(s_sc, m_attn_sc), (s_dv, m_attn_dv)]
                continue

            ffn = ExitStack()
            h_sb = ffn.enter_context(nc.sbuf_tensor(f"h_sb{half}", [128, 24, 1024], BF16))
            wd_sb = ffn.enter_context(nc.sbuf_tensor(f"wd_sb{half}", [128, 24, 1024], BF16))
            wg_buf = ffn.enter_context(nc.sbuf_tensor(f"wg_buf{half}", [128, 2, 1024], BF16))
            wu_buf = ffn.enter_context(nc.sbuf_tensor(f"wu_buf{half}", [128, 2, 1024], BF16))
            sg_buf = ffn.enter_context(nc.sbuf_tensor(f"sg_buf{half}", [128, 2, 512], BF16))
            g_buf = ffn.enter_context(nc.sbuf_tensor(f"g_buf{half}", [128, 2, 512], F32))
            xre = ffn.enter_context(nc.sbuf_tensor(f"xre{half}", [128, 1024], F32))
            PF = [ffn.enter_context(nc.psum_tensor(f"PF{i}_{half}", [128, 512], F32))
                  for i in range(2)]
            pfree[6] = (s_pe, 0)
            pfree[7] = (s_pe, 0)

            attn_done = [(s_sc, m_attn_sc), (s_dv, m_attn_dv)]

            ld_gu = [0] * 24
            ld_wd = [0] * 24
            mm_gu = [0] * 24

            for sem, vv in attn_done:
                g.wait_ge(sem, vv)
            for j in range(2):
                inc(g.dma_start(out=wg_buf[:, j, :], in_=wgt_in[j]), s_gp, 16)
                ld_gu[j] = inc(g.dma_start(out=wu_buf[:, j, :], in_=wut_in[j]), s_gp, 16)

            for hc in range(24):
                bsel = hc % 2
                if bsel == 0:
                    pg = [P[0], P[1], P[2], P[3]]
                    bk = [0, 1, 2, 3]
                else:
                    pg = [P[4], P[5], PF[0], PF[1]]
                    bk = [4, 5, 6, 7]

                tn.wait_ge(s_gp, ld_gu[hc])
                for b in bk:
                    tn.wait_ge(*pfree[b])
                for gi in range(2):
                    for dc in range(8):
                        tn.matmul(pg[gi][:, :], wg_buf[:, bsel, dc * 128:(dc + 1) * 128],
                                  xn_fm[:, dc, qbase + gi * 512:qbase + (gi + 1) * 512],
                                  start=(dc == 0), stop=(dc == 7))
                    for dc in range(8):
                        ins = tn.matmul(pg[2 + gi][:, :], wu_buf[:, bsel, dc * 128:(dc + 1) * 128],
                                        xn_fm[:, dc, qbase + gi * 512:qbase + (gi + 1) * 512],
                                        start=(dc == 0), stop=(dc == 7))
                inc(ins, s_pe, 1)
                mm_gu[hc] = val(s_pe)

                s.wait_ge(s_pe, mm_gu[hc])
                if hc >= 1:
                    pass
                s.activation(sg_buf[:, 0, :], pg[0][:, :], AF.Silu, scale=SILU1)
                inc(s.activation(sg_buf[:, 1, :], pg[1][:, :], AF.Silu, scale=SILU1),
                    s_sc, 1)
                m_silu = val(s_sc)
                pfree[bk[0]] = (s_sc, m_silu)
                pfree[bk[1]] = (s_sc, m_silu)

                v.wait_ge(s_sc, m_silu)
                v.tensor_tensor(h_sb[:, hc, 0:512], sg_buf[:, 0, :], pg[2][:, :], ALU.mult)
                inc(v.tensor_tensor(h_sb[:, hc, 512:1024], sg_buf[:, 1, :], pg[3][:, :],
                                    ALU.mult), s_dv, 1)
                m_h = val(s_dv)
                pfree[bk[2]] = (s_dv, m_h)
                pfree[bk[3]] = (s_dv, m_h)

                if hc + 2 < 24:
                    g.wait_ge(s_pe, mm_gu[hc])
                    inc(g.dma_start(out=wg_buf[:, hc % 2, :], in_=wgt_in[hc + 2]), s_gp, 16)
                    ld_gu[hc + 2] = inc(g.dma_start(out=wu_buf[:, hc % 2, :],
                                                    in_=wut_in[hc + 2]), s_gp, 16)
                ld_wd[hc] = inc(g.dma_start(out=wd_sb[:, hc, :], in_=wdt_in[hc]), s_gp, 16)
            m_h_all = val(s_dv)
            m_wd_all = ld_wd[23]

            m_epi_prev = 0
            for tt in range(8):
                bank0, bank1 = (0, 1) if tt % 2 == 0 else (2, 3)

                if tt == 0:
                    for sem, vv in attn_done:
                        sy.wait_ge(sem, vv)
                if m_epi_prev:
                    sy.wait_ge(s_dv, m_epi_prev)
                inc(sy.dma_start(out=xre[:, :],
                                 in_=x_in[qbase + tt * 128:qbase + (tt + 1) * 128, :]),
                    s_sy, 16)
                m_xre = val(s_sy)

                tn.wait_ge(s_dv, m_h_all)
                tn.wait_ge(s_gp, m_wd_all)
                tn.wait_ge(*pfree[bank0])
                tn.wait_ge(*pfree[bank1])
                for hc in range(24):
                    tn.matmul(P[bank0][:, :], h_sb[:, hc, tt * 128:(tt + 1) * 128],
                              wd_sb[:, hc, 0:512], start=(hc == 0), stop=(hc == 23))
                for hc in range(24):
                    ins = tn.matmul(P[bank1][:, :], h_sb[:, hc, tt * 128:(tt + 1) * 128],
                                    wd_sb[:, hc, 512:1024], start=(hc == 0), stop=(hc == 23))
                inc(ins, s_pe, 1)
                m_down = val(s_pe)

                s.wait_ge(s_pe, m_down)
                s.activation(g_buf[:, 0, :], P[bank0][:, :], AF.Silu, scale=SILU2)
                inc(s.activation(g_buf[:, 1, :], P[bank1][:, :], AF.Silu, scale=SILU2),
                    s_sc, 1)
                m_g = val(s_sc)
                pfree[bank0] = (s_sc, m_g)
                pfree[bank1] = (s_sc, m_g)

                v.wait_ge(s_sc, m_g)
                v.wait_ge(s_sy, m_xre)
                for gi in range(2):
                    v.tensor_tensor(g_buf[:, gi, :], g_buf[:, gi, :],
                                    s_tm[:, tt, gi * 512:(gi + 1) * 512], ALU.mult)
                inc(v.tensor_tensor(xre[:, :], xre[:, :],
                                    g_buf[:, :, :].rearrange("p a b -> p (a b)"), ALU.add),
                    s_dv, 1)
                m_epi = val(s_dv)
                m_epi_prev = m_epi
                m_stm_free[tt] = (s_dv, m_epi)

                sy.wait_ge(s_dv, m_epi)
                inc(sy.dma_start(
                    out=y_out[half * 1024 + tt * 128:half * 1024 + (tt + 1) * 128, :],
                    in_=xre[:, :]), s_st, 16)

            ffn.close()
            arena_free = [(s_dv, val(s_dv)), (s_sc, val(s_sc)), (s_pe, val(s_pe))]

        sy.wait_ge(s_st, val(s_st))
        sy.wait_ge(s_gp, val(s_gp))

    return nc, dbg


def _prep_inputs(inputs):
    x = np.asarray(inputs["L0_attn_x"], np.float32).reshape(S, D)
    qkv_w = np.asarray(inputs["L0_attn_qkv_w"], np.float32)
    qkv_b = np.asarray(inputs["L0_attn_qkv_b"], np.float32)
    rq_w = np.asarray(inputs["L0_attn_rq_w"], np.float32)
    rq_b = np.asarray(inputs["L0_attn_rq_b"], np.float32)
    rkv_w = np.asarray(inputs["L0_attn_rkv_w"], np.float32)
    rkv_b = np.asarray(inputs["L0_attn_rkv_b"], np.float32)
    rout_w = np.asarray(inputs["L0_attn_rout_w"], np.float32)
    rout_b = np.asarray(inputs["L0_attn_rout_b"], np.float32)
    out_w = np.asarray(inputs["L0_attn_out_w"], np.float32)
    out_b = np.asarray(inputs["L0_attn_out_b"], np.float32)
    gl_w = np.asarray(inputs["L0_attn_gl_w"], np.float32)
    gr_w = np.asarray(inputs["L0_attn_gr_w"], np.float32)
    w_gate = np.asarray(inputs["L0_ffn_w_gate"], np.int32)
    w_up = np.asarray(inputs["L0_ffn_w_up"], np.int32)
    w_down = np.asarray(inputs["L0_ffn_w_down"], np.int32)

    x_pad = np.concatenate([np.zeros((HALO, D), np.float32), x], 0)

    qkvw_qk = np.ascontiguousarray(
        qkv_w[:, 0:2048].reshape(8, 128, 16, 128).transpose(2, 1, 0, 3).reshape(16, 128, 1024))
    qkvw_v = np.ascontiguousarray(qkv_w[:, 2048:3072].reshape(8, 128, 1024))
    outw_r = np.ascontiguousarray(out_w.reshape(8, 128, 1024))
    rqw_r = np.ascontiguousarray(
        rq_w.reshape(8, 128, 4, 64).transpose(2, 1, 0, 3).reshape(4, 128, 512))
    rkvw_r = np.ascontiguousarray(rkv_w.reshape(8, 128, 512))
    routw_r = np.ascontiguousarray(
        rout_w.reshape(2, 128, 8, 128).transpose(2, 1, 0, 3).reshape(8, 128, 256))
    wgt_r = np.ascontiguousarray(
        w_gate.reshape(24, 128, 8, 128).transpose(0, 3, 2, 1).reshape(24, 128, 1024))
    wut_r = np.ascontiguousarray(
        w_up.reshape(24, 128, 8, 128).transpose(0, 3, 2, 1).reshape(24, 128, 1024))
    wdt_r = np.ascontiguousarray(w_down.T.reshape(24, 128, 1024))

    bcolv = np.zeros((128, 40), np.float32)
    bcolv[:, 0:24] = qkv_b.reshape(24, 128).T
    bcolv[:, 24:32] = rout_b.reshape(8, 128).T
    bcolv[0:64, 32:36] = rq_b.reshape(4, 64).T
    browv = np.zeros((1, 2560), np.float32)
    browv[0, 0:1024] = qkv_b[2048:3072]
    browv[0, 1024:2048] = out_b
    browv[0, 2048:2560] = rkv_b

    bdgv = np.zeros((1024, 32), np.float32)
    for h in range(16):
        bdgv[h * 64:(h + 1) * 64, h] = gl_w[:, 0]
        bdgv[h * 64:(h + 1) * 64, 16 + h] = gr_w[:, 0]
    bdgv = np.ascontiguousarray(bdgv.reshape(8, 128, 32).transpose(1, 0, 2))

    ebcv = np.zeros((16, 8, 128), ml_dtypes.bfloat16)
    for dc in range(8):
        for pp in range(128):
            ebcv[(dc * 128 + pp) // 64, dc, pp] = 1.0

    identv = np.eye(128, dtype=ml_dtypes.bfloat16)

    r = np.arange(128)
    c = np.arange(128)
    m0 = (c[:, None] >= r[None, :]).astype(np.float32)
    m0e = m0.copy()
    m0e[0, 0] = 0.0
    m1 = (c[:, None] < r[None, :]).astype(np.float32)
    maskAv = np.tile(m0e, (1, 4)).astype(ml_dtypes.bfloat16)
    maskBv = np.tile(m0, (1, 4)).astype(ml_dtypes.bfloat16)
    maskCv = np.tile(m1, (1, 4)).astype(ml_dtypes.bfloat16)
    maskF0 = np.zeros((128, 512), ml_dtypes.bfloat16)

    shared = dict(qkvw_qk=qkvw_qk, qkvw_v=qkvw_v, outw_in=outw_r, rqw_in=rqw_r,
                  rkvw_in=rkvw_r, routw_in=routw_r, wgt_in=wgt_r, wut_in=wut_r,
                  wdt_in=wdt_r, bcol_in=bcolv, brow_in=browv, bdg_in=bdgv,
                  ebc_in=ebcv, ident_in=identv, identf_in=np.eye(64, dtype=np.float32),
                  maskA_in=maskAv, maskB_in=maskBv,
                  maskC_in=maskCv)
    in_maps = []
    ll = np.arange(64)
    for cidx in range(NCORES):
        m = dict(shared)
        m["x_in"] = np.ascontiguousarray(x_pad[cidx * T:cidx * T + TB])
        m["maskF_in"] = maskF0 if cidx == 0 else maskAv
        lmv = np.zeros((64, 16), np.float32)
        for jj in range(16):
            lmv[:, jj] = (ll < cidx * 8 + jj // 2).astype(np.float32)
        m["latmask_in"] = lmv
        in_maps.append(m)
    return in_maps


def kernel(**inputs):
    global LAST_DEBUG
    in_maps = _prep_inputs(inputs)
    nc, dbg = build(phases=PHASES, debug=DEBUG)
    res = run_bass_kernel_spmd(nc, in_maps, core_ids=list(range(NCORES)))
    LAST_DEBUG = {}
    for name in dbg:
        LAST_DEBUG[name] = [res.results[cc].get("dbg_" + name) for cc in range(NCORES)]
    out = np.concatenate([res.results[cc]["y_out"] for cc in range(NCORES)], axis=0)
    return out.reshape(1, S, D).astype(np.float32)


# version 10
# speedup vs baseline: 1.1361x; 1.0261x over previous
import numpy as np
import ml_dtypes
from contextlib import ExitStack

import concourse.bass as bass
import concourse.mybir as mybir
from concourse.bass_utils import run_bass_kernel_spmd

F32 = mybir.dt.float32
BF16 = mybir.dt.bfloat16
I32 = mybir.dt.int32
AF = mybir.ActivationFunctionType
ALU = mybir.AluOpType
AX = mybir.AxisListType

NCORES = 8
S = 16384
D = 1024
T = 2048
HALO = 128
TB = T + HALO
NT = TB // 128
BSHIFT = 40.0
SILU1 = 1.0 / 127.0
SILU2 = 1.0 / (127.0 * 127.0)
TINY = 1e-30

PHASES = ("ln", "qkv", "attn", "ffn")
DEBUG = ()
LAST_DEBUG = {}


def build(phases=PHASES, debug=(), nc_kwargs=None, stage=99, attn_tq=8, attn_lat=True,
          attn_grp=True, attn_sub=8):
    debug = set(debug)
    nc = bass.Bass("TRN2", target_bir_lowering=False, **(nc_kwargs or {}))

    x_in = nc.dram_tensor("x_in", [TB, D], F32, kind="ExternalInput").ap()
    qkvw_qk = nc.dram_tensor("qkvw_qk", [16, 128, 1024], F32, kind="ExternalInput").ap()
    qkvw_v = nc.dram_tensor("qkvw_v", [8, 128, 1024], F32, kind="ExternalInput").ap()
    outw_in = nc.dram_tensor("outw_in", [8, 128, 1024], F32, kind="ExternalInput").ap()
    rqw_in = nc.dram_tensor("rqw_in", [4, 128, 512], F32, kind="ExternalInput").ap()
    rkvw_in = nc.dram_tensor("rkvw_in", [8, 128, 512], F32, kind="ExternalInput").ap()
    routw_in = nc.dram_tensor("routw_in", [8, 128, 256], F32, kind="ExternalInput").ap()
    wgt_in = nc.dram_tensor("wgt_in", [24, 128, 1024], I32, kind="ExternalInput").ap()
    wut_in = nc.dram_tensor("wut_in", [24, 128, 1024], I32, kind="ExternalInput").ap()
    wdt_in = nc.dram_tensor("wdt_in", [24, 128, 1024], I32, kind="ExternalInput").ap()
    bcol_in = nc.dram_tensor("bcol_in", [128, 40], F32, kind="ExternalInput").ap()
    brow_in = nc.dram_tensor("brow_in", [1, 2560], F32, kind="ExternalInput").ap()
    bdg_in = nc.dram_tensor("bdg_in", [128, 8, 32], F32, kind="ExternalInput").ap()
    ebc_in = nc.dram_tensor("ebc_in", [16, 8, 128], BF16, kind="ExternalInput").ap()
    ident_in = nc.dram_tensor("ident_in", [128, 128], BF16, kind="ExternalInput").ap()
    identf_in = nc.dram_tensor("identf_in", [64, 64], F32, kind="ExternalInput").ap()
    maskA_in = nc.dram_tensor("maskA_in", [128, 512], BF16, kind="ExternalInput").ap()
    maskB_in = nc.dram_tensor("maskB_in", [128, 512], BF16, kind="ExternalInput").ap()
    maskC_in = nc.dram_tensor("maskC_in", [128, 512], BF16, kind="ExternalInput").ap()
    maskF_in = nc.dram_tensor("maskF_in", [128, 512], BF16, kind="ExternalInput").ap()
    latmask_in = nc.dram_tensor("latmask_in", [64, 16], F32, kind="ExternalInput").ap()

    y_out = nc.dram_tensor("y_out", [T, D], F32, kind="ExternalOutput").ap()
    gin = nc.dram_tensor("gin", [8, 512], F32)
    gout = nc.dram_tensor("gout", [64, 512], F32)

    dbg = {}

    def dbg_out(name, shape):
        if name == "g0_local":
            for nm, shp in [("g0_local", [128, 8, 256]), ("g0_remote", [128, 8, 256]),
                            ("g0_mixed", [128, 8, 256]), ("g0_gate", [16, 256])]:
                if nm in debug and nm not in dbg:
                    dbg[nm] = nc.dram_tensor("dbg_" + nm, shp, F32, kind="ExternalOutput").ap()
            return dbg.get(name)
        if name in debug and name not in dbg:
            dbg[name] = nc.dram_tensor("dbg_" + name, list(shape), F32, kind="ExternalOutput").ap()
        return dbg.get(name)

    ctr = {}

    def inc(instr, sem, n=1):
        instr.then_inc(sem, n)
        ctr[sem.name] = ctr.get(sem.name, 0) + n
        return ctr[sem.name]

    def val(sem):
        return ctr.get(sem.name, 0)

    with ExitStack() as top:
        e = top.enter_context

        xn_fm = e(nc.sbuf_tensor("xn_fm", [128, 8, TB], BF16))
        s_tm = e(nc.sbuf_tensor("s_tm", [128, 8, 1024], F32))
        rq_sb = e(nc.sbuf_tensor("rq_sb", [64, 4, T], BF16))
        rkv_all = e(nc.sbuf_tensor("rkv_all", [64, 512], F32))
        ident = e(nc.sbuf_tensor("ident", [128, 128], BF16))
        identf = e(nc.sbuf_tensor("identf", [64, 64], F32))
        ones_bf = e(nc.sbuf_tensor("ones_bf", [128, 512], BF16))
        tiny_bf = e(nc.sbuf_tensor("tiny_bf", [128, 1], BF16))
        maskA = e(nc.sbuf_tensor("maskA", [128, 512], BF16))
        maskB = e(nc.sbuf_tensor("maskB", [128, 512], BF16))
        maskC = e(nc.sbuf_tensor("maskC", [128, 512], BF16))
        maskF = e(nc.sbuf_tensor("maskF", [128, 512], BF16))
        latmask = e(nc.sbuf_tensor("latmask", [64, 16], F32))
        bdg = e(nc.sbuf_tensor("bdg", [128, 8, 32], BF16))
        bcol = e(nc.sbuf_tensor("bcol", [128, 42], F32))

        P = [e(nc.psum_tensor(f"P{i}", [128, 512], F32)) for i in range(6)]

        s_sy = e(nc.semaphore("s_sy"))
        s_gp = e(nc.semaphore("s_gp"))
        s_cc = e(nc.semaphore("s_cc"))
        s_st = e(nc.semaphore("s_st"))
        s_pe = e(nc.semaphore("s_pe"))
        s_dv = e(nc.semaphore("s_dv"))
        s_sc = e(nc.semaphore("s_sc"))

        block = e(nc.Block())
        tn = nc.tensor
        v = nc.vector
        s = nc.scalar
        g = nc.gpsimd
        sy = nc.sync

        BSH = bcol[:, 40:41]
        EPS = bcol[:, 41:42]
        ZCOL = bcol[:, 36:37]

        pfree = {i: (s_pe, 0) for i in range(8)}

        def _cut():
            for sem in (s_sy, s_gp, s_dv, s_sc, s_pe, s_st, s_cc):
                if val(sem):
                    sy.wait_ge(sem, val(sem))


        inc(sy.dma_start(out=ident[:, :], in_=ident_in), s_sy, 16)
        inc(sy.dma_start(out=identf[:, :], in_=identf_in), s_sy, 16)
        inc(sy.dma_start(out=maskA[:, :], in_=maskA_in), s_sy, 16)
        inc(sy.dma_start(out=maskB[:, :], in_=maskB_in), s_sy, 16)
        inc(sy.dma_start(out=maskC[:, :], in_=maskC_in), s_sy, 16)
        inc(sy.dma_start(out=maskF[:, :], in_=maskF_in), s_sy, 16)
        inc(sy.dma_start(out=latmask[:, :], in_=latmask_in), s_sy, 16)
        inc(sy.dma_start(out=bcol[:, 0:40], in_=bcol_in), s_sy, 16)
        m_consts = val(s_sy)

        inc(g.dma_start(out=bdg[:, :, :], in_=bdg_in), s_gp, 16)
        m_consts_gp = val(s_gp)

        v.memset(ones_bf[:, :], 1.0)
        v.memset(tiny_bf[:, :], TINY)
        v.memset(bcol[:, 40:41], -BSHIFT)
        inc(v.memset(bcol[:, 41:42], 1e-5), s_dv, 1)
        m_memsets = val(s_dv)

        ln = ExitStack()
        x_sb = ln.enter_context(nc.sbuf_tensor("x_sb", [128, NT, 1024], F32))
        xn_tm = ln.enter_context(nc.sbuf_tensor("xn_tm", [128, 2, 1024], BF16))
        stats_all = ln.enter_context(nc.sbuf_tensor("stats_all", [128, NT, 12], F32))
        mv_all = ln.enter_context(nc.sbuf_tensor("mv_all", [128, 2 * NT], F32))
        rstd_all = ln.enter_context(nc.sbuf_tensor("rstd_all", [128, NT], F32))
        pooled = ln.enter_context(nc.sbuf_tensor("pooled", [128, 8, 8], F32))
        pooled_bf = ln.enter_context(nc.sbuf_tensor("pooled_bf", [128, 8, 8], BF16))
        rqw_sb = ln.enter_context(nc.sbuf_tensor("rqw_sb", [128, 4, 512], BF16))
        rkvw_sb = ln.enter_context(nc.sbuf_tensor("rkvw_sb", [128, 8, 512], BF16))
        rkv_own = ln.enter_context(nc.sbuf_tensor("rkv_own", [8, 512], F32))
        brkv_f = ln.enter_context(nc.sbuf_tensor("brkv_f", [1, 512], F32))
        brkv = ln.enter_context(nc.sbuf_tensor("brkv", [1, 512], BF16))
        brkv_lo = ln.enter_context(nc.sbuf_tensor("brkv_lo", [1, 512], BF16))
        PT = [ln.enter_context(nc.psum_tensor(f"PT{i}", [128, 512], F32)) for i in range(2)]

        sy.wait_ge(s_dv, m_memsets)
        for t in range(NT):
            inc(sy.dma_start(out=x_sb[:, t, :], in_=x_in[t * 128:(t + 1) * 128, :]), s_sy, 16)
        m_xload0 = m_consts

        for h in range(4):
            inc(g.dma_start(out=rqw_sb[:, h, :], in_=rqw_in[h]), s_gp, 16)
        for dc in range(8):
            inc(g.dma_start(out=rkvw_sb[:, dc, :], in_=rkvw_in[dc]), s_gp, 16)
        inc(g.dma_start(out=brkv_f[:, :], in_=brow_in[0:1, 2048:2560]), s_gp, 16)
        m_smallw = val(s_gp)

        v.wait_ge(s_gp, m_smallw)
        inc(v.tensor_copy(brkv[:, :], brkv_f[:, :]), s_dv, 1)
        v.wait_ge(s_dv, val(s_dv))
        inc(v.scalar_tensor_tensor(brkv_lo[:, :], brkv_f[:, :], 1.0, brkv[:, :],
                                   ALU.mult, ALU.subtract), s_dv, 1)
        m_bsplit0 = val(s_dv)

        if stage < 2:
            _cut()
            ln.close()
            return nc, dbg

        for t in range(NT):
            v.wait_ge(s_sy, m_xload0 + 16 * (t + 1))
            v.bn_stats(stats_all[:, t, 0:6], x_sb[:, t, 0:512])
            ins = v.bn_stats(stats_all[:, t, 6:12], x_sb[:, t, 512:1024])
        inc(ins, s_dv, 1)
        v.wait_ge(s_dv, val(s_dv))
        for t in range(NT):
            ins = v.bn_aggr(mv_all[:, 2 * t:2 * t + 2], stats_all[:, t, 0:12])
        inc(ins, s_dv, 1)
        m_aggr = val(s_dv)

        s.wait_ge(s_dv, m_aggr)
        s.wait_ge(s_dv, m_memsets)
        inc(s.activation(rstd_all[:, :], mv_all[:, 1:2 * NT:2], AF.Sqrt,
                         bias=EPS, scale=1.0), s_sc, 1)
        m_sqrt = val(s_sc)

        v.wait_ge(s_sc, m_sqrt)
        inc(v.reciprocal(rstd_all[:, :], rstd_all[:, :]), s_dv, 1)
        m_rstd = val(s_dv)

        if stage < 3:
            _cut()
            ln.close()
            return nc, dbg

        apply_m = [0] * NT
        xgrp_m = [[0, 0] for _ in range(NT)]
        evac_m = [[0, 0] for _ in range(NT)]
        for t in range(NT):
            b = t % 2

            if t == 0:
                v.wait_ge(s_dv, m_rstd)
            if t >= 2:
                v.wait_ge(s_pe, xgrp_m[t - 2][1])
            apply_m[t] = inc(
                v.tensor_scalar(xn_tm[:, b, :], x_sb[:, t, :],
                                mv_all[:, 2 * t:2 * t + 1], rstd_all[:, t:t + 1],
                                ALU.subtract, ALU.mult), s_dv, 1)

            tn.wait_ge(s_dv, apply_m[t])
            for gg in range(2):
                if t >= 1:
                    tn.wait_ge(s_dv, evac_m[t - 1][gg])
                for k in range(4):
                    dc = gg * 4 + k
                    ins = tn.matmul(PT[gg][:, k * 128:(k + 1) * 128],
                                    xn_tm[:, b, dc * 128:(dc + 1) * 128], ident[:, :],
                                    start=True, stop=True)
                xgrp_m[t][gg] = inc(ins, s_pe, 1)

            for gg in range(2):
                v.wait_ge(s_pe, xgrp_m[t][gg])
                evac_m[t][gg] = inc(
                    v.tensor_copy(xn_fm[:, 4 * gg:4 * gg + 4, t * 128:(t + 1) * 128],
                                  PT[gg][:, :].rearrange("p (k n) -> p k n", k=4)),
                    s_dv, 1)
        m_ln_done = evac_m[NT - 1][1]

        if dbg_out("xn", [128, 8, 256]) is not None:
            g.wait_ge(s_dv, m_ln_done)
            inc(g.dma_start(out=dbg["xn"], in_=xn_fm[:, :, 0:256]), s_gp, 16)

        if stage < 4:
            _cut()
            ln.close()
            return nc, dbg

        v.wait_ge(s_dv, m_ln_done)
        for dc in range(8):
            for j in range(8):
                ins = v.tensor_reduce(pooled[:, dc, j:j + 1],
                                      xn_fm[:, dc, HALO + 256 * j:HALO + 256 * (j + 1)],
                                      AX.X, ALU.add)
        inc(ins, s_dv, 1)
        v.wait_ge(s_dv, val(s_dv))
        inc(v.tensor_scalar(pooled_bf[:, :, :], pooled[:, :, :], 1.0 / 256.0, None,
                            ALU.mult), s_dv, 1)
        m_pooled = val(s_dv)

        tn.wait_ge(s_dv, m_pooled)
        tn.wait_ge(s_dv, m_memsets)
        tn.wait_ge(s_gp, m_smallw)
        tn.wait_ge(s_gp, m_consts_gp)
        tn.wait_ge(s_dv, m_bsplit0)
        for dc in range(8):
            tn.matmul(P[0][0:8, :], pooled_bf[:, dc, :], rkvw_sb[:, dc, :],
                      start=(dc == 0), stop=False)
        tn.matmul(P[0][0:8, :], ones_bf[0:1, 0:8], brkv[0:1, :],
                  start=False, stop=False)
        inc(tn.matmul(P[0][0:8, :], ones_bf[0:1, 0:8], brkv_lo[0:1, :],
                      start=False, stop=True), s_pe, 1)
        m_rkvmm = val(s_pe)

        v.wait_ge(s_pe, m_rkvmm)
        inc(v.tensor_copy(rkv_own[:, :], P[0][0:8, :]), s_dv, 1)
        m_rkvevac = val(s_dv)
        pfree[0] = (s_dv, m_rkvevac)

        if stage < 5:
            _cut()
            ln.close()
            return nc, dbg

        for h in range(4):
            for gg in range(4):
                bank = (h * 4 + gg) % 2

                tn.wait_ge(*pfree[bank])
                tn.wait_ge(s_dv, m_ln_done)
                for dc in range(8):
                    ins = tn.matmul(P[bank][0:64, :],
                                    rqw_sb[:, h, dc * 64:(dc + 1) * 64],
                                    xn_fm[:, dc, HALO + gg * 512:HALO + (gg + 1) * 512],
                                    start=(dc == 0), stop=(dc == 7))
                inc(ins, s_pe, 1)
                rq_mm = val(s_pe)

                s.wait_ge(s_pe, rq_mm)
                m = inc(s.activation(rq_sb[:, h, gg * 512:(gg + 1) * 512], P[bank][0:64, :],
                                     AF.Identity, bias=bcol[0:64, 32 + h:33 + h], scale=1.0),
                        s_sc, 1)
                pfree[bank] = (s_sc, m)
        m_rq_done = val(s_sc)

        if stage < 6:
            _cut()
            ln.close()
            return nc, dbg

        sy.wait_ge(s_dv, m_rkvevac)
        inc(sy.dma_start(out=gin[:, :], in_=rkv_own[:, :]), s_sy, 16)
        m_gin = val(s_sy)

        g.wait_ge(s_sy, m_gin)
        inc(g.collective_compute(
            "AllGather", ALU.bypass, replica_groups=[list(range(NCORES))],
            ins=[gin.ap().opt()], outs=[gout.ap().opt()]), s_cc, 1)
        m_cc = val(s_cc)

        sy.wait_ge(s_cc, m_cc)
        inc(sy.dma_start(out=rkv_all[:, :], in_=gout[:, :]), s_sy, 16)
        m_rkvall = val(s_sy)

        if dbg_out("rkv_all", [64, 512]) is not None:
            inc(sy.dma_start(out=dbg["rkv_all"], in_=rkv_all[:, :]), s_st, 16)
        if dbg_out("rq", [64, 4, 512]) is not None:
            g.wait_ge(s_sc, m_rq_done)
            inc(g.dma_start(out=dbg["rq"], in_=rq_sb[:, :, 0:512]), s_gp, 16)

        ln.close()

        arena_free = [(s_sc, m_rq_done), (s_dv, m_pooled), (s_sy, m_gin)]
        m_stm_free = [(s_pe, 0)] * 8

        for half in range(2):
            if "qkv" not in phases:
                break
            kbase = half * 896
            qbase = HALO + half * 1024

            att = ExitStack()
            q_fm = att.enter_context(nc.sbuf_tensor(f"q_fm{half}", [128, 8, 1024], BF16))
            k_fm = att.enter_context(nc.sbuf_tensor(f"k_fm{half}", [128, 8, 1280], BF16))
            v_tm = att.enter_context(nc.sbuf_tensor(f"v_tm{half}", [128, 10, 1024], BF16))
            ow_sb = att.enter_context(nc.sbuf_tensor(f"ow_sb{half}", [128, 8, 1024], BF16))
            rout_sb = att.enter_context(nc.sbuf_tensor(f"rout_sb{half}", [128, 8, 256], BF16))
            rkT = att.enter_context(nc.sbuf_tensor(f"rkT{half}", [64, 4, 64], BF16))
            rkbf = att.enter_context(nc.sbuf_tensor(f"rkbf{half}", [64, 256], BF16))
            rv_aug = att.enter_context(nc.sbuf_tensor(f"rv_aug{half}", [64, 4, 64], BF16))
            otcol = att.enter_context(nc.sbuf_tensor(f"otcol{half}", [65, 1], BF16))
            bvob = att.enter_context(nc.sbuf_tensor(f"bvob{half}", [1, 2048], BF16))
            bvob_lo = att.enter_context(nc.sbuf_tensor(f"bvlo{half}", [1, 2048], BF16))
            ebc = att.enter_context(nc.sbuf_tensor(f"ebc{half}", [16, 8, 128], BF16))
            P6 = att.enter_context(nc.psum_tensor(f"P6_{half}", [128, 512], F32))
            P7 = att.enter_context(nc.psum_tensor(f"P7_{half}", [128, 512], F32))
            pfree[6] = (s_pe, 0)
            pfree[7] = (s_pe, 0)

            wqkv = ExitStack()
            wqk_buf = wqkv.enter_context(nc.sbuf_tensor(f"wqk{half}", [128, 2, 1024], BF16))
            bvf = wqkv.enter_context(nc.sbuf_tensor(f"bvf{half}", [1, 2048], F32))
            wv_sb = wqkv.enter_context(nc.sbuf_tensor(f"wv{half}", [128, 8, 1024], BF16))

            ld_qk = [0] * 16
            mm_qk = [0] * 16

            for sem, vv in arena_free:
                g.wait_ge(sem, vv)
            inc(g.dma_start(out=bvf[:, :], in_=brow_in[0:1, 0:2048]), s_gp, 16)
            inc(g.dma_start(out=ebc[:, :, :], in_=ebc_in), s_gp, 16)
            m_bvf = val(s_gp)

            v.wait_ge(s_gp, m_bvf)
            inc(v.tensor_copy(bvob[:, :], bvf[:, :]), s_dv, 1)
            v.wait_ge(s_dv, val(s_dv))
            inc(v.scalar_tensor_tensor(bvob_lo[:, :], bvf[:, :], 1.0, bvob[:, :],
                                       ALU.mult, ALU.subtract), s_dv, 1)
            m_bsplit = val(s_dv)

            for j in range(2):
                ld_qk[j] = inc(g.dma_start(out=wqk_buf[:, j, :], in_=qkvw_qk[j]), s_gp, 16)

            for j in range(16):
                is_q = j < 8
                oc = j if is_q else j - 8
                ngroups = [(0, 512), (512, 1024)] if is_q else [(0, 512), (512, 1024), (1024, 1280)]
                base = qbase if is_q else kbase
                dest = q_fm if is_q else k_fm
                banks = [0, 1, 4] if (j % 2 == 0) else [2, 3, 4]
                grp_mm = [0] * len(ngroups)

                tn.wait_ge(s_gp, ld_qk[j])
                tn.wait_ge(s_dv, m_ln_done)
                for gi, (a0, a1) in enumerate(ngroups):
                    bank = banks[gi]
                    tn.wait_ge(*pfree[bank])
                    for dc in range(8):
                        ins = tn.matmul(P[bank][:, 0:a1 - a0],
                                        wqk_buf[:, j % 2, dc * 128:(dc + 1) * 128],
                                        xn_fm[:, dc, base + a0:base + a1],
                                        start=(dc == 0), stop=(dc == 7))
                    grp_mm[gi] = inc(ins, s_pe, 1)
                mm_qk[j] = grp_mm[-1]

                colidx = oc if j < 8 else 8 + oc
                for gi, (a0, a1) in enumerate(ngroups):
                    bank = banks[gi]
                    s.wait_ge(s_pe, grp_mm[gi])
                    m = inc(s.activation(dest[:, oc, a0:a1], P[bank][:, 0:a1 - a0],
                                         AF.Identity, bias=bcol[:, colidx:colidx + 1],
                                         scale=1.0), s_sc, 1)
                    pfree[bank] = (s_sc, m)

                if j + 2 < 16:
                    g.wait_ge(s_pe, mm_qk[j])
                    ld_qk[j + 2] = inc(g.dma_start(out=wqk_buf[:, j % 2, :],
                                                   in_=qkvw_qk[j + 2]), s_gp, 16)

            for dc in range(8):
                inc(g.dma_start(out=wv_sb[:, dc, :], in_=qkvw_v[dc]), s_gp, 16)
            m_wv = val(s_gp)

            for dc in range(8):
                inc(g.dma_start(out=ow_sb[:, dc, :], in_=outw_in[dc]), s_gp, 16)
            for oc in range(8):
                inc(g.dma_start(out=rout_sb[:, oc, :], in_=routw_in[oc]), s_gp, 16)
            m_ow = val(s_gp)

            for t in range(10):
                for gi in range(2):
                    bank = 4 + (t * 2 + gi) % 2
                    if bank == 4 + 1:
                        bank = 5

                    tn.wait_ge(s_gp, m_wv)
                    tn.wait_ge(s_dv, m_bsplit)
                    tn.wait_ge(*pfree[bank])
                    for dc in range(8):
                        tn.matmul(P[bank][:, :],
                                  xn_fm[:, dc, kbase + t * 128:kbase + (t + 1) * 128],
                                  wv_sb[:, dc, gi * 512:(gi + 1) * 512],
                                  start=(dc == 0), stop=False)
                    tn.matmul(P[bank][:, :], ones_bf[0:1, 0:128],
                              bvob[0:1, gi * 512:(gi + 1) * 512],
                              start=False, stop=False)
                    inc(tn.matmul(P[bank][:, :], ones_bf[0:1, 0:128],
                                  bvob_lo[0:1, gi * 512:(gi + 1) * 512],
                                  start=False, stop=True), s_pe, 1)
                    mmv = val(s_pe)

                    v.wait_ge(s_pe, mmv)
                    m = inc(v.tensor_copy(v_tm[:, t, gi * 512:(gi + 1) * 512],
                                          P[bank][:, :]), s_dv, 1)
                    pfree[bank] = (s_dv, m)
            m_v_done = val(s_dv)

            if half == 0 and dbg_out("qkv", [128, 3, 512]) is not None:
                g.wait_ge(s_sc, val(s_sc))
                g.wait_ge(s_dv, m_v_done)
                inc(g.dma_start(out=dbg["qkv"][:, 0:1, :], in_=q_fm[:, 0:1, 0:512]), s_gp, 16)
                inc(g.dma_start(out=dbg["qkv"][:, 1:2, :], in_=k_fm[:, 0:1, 0:512]), s_gp, 16)
                inc(g.dma_start(out=dbg["qkv"][:, 2:3, :], in_=v_tm[:, 0:1, 0:512]),
                    s_gp, 16)

            wqkv.close()

            if "attn" not in phases:
                att.close()
                arena_free = [(s_dv, m_v_done), (s_sc, val(s_sc))]
                continue

            scr = ExitStack()
            pt_sb = scr.enter_context(nc.sbuf_tensor(f"pt{half}", [128, 2, 512], BF16))
            local_n = scr.enter_context(nc.sbuf_tensor(f"locn{half}", [128, 8, 256], BF16))
            remote_n = scr.enter_context(nc.sbuf_tensor(f"remn{half}", [128, 8, 256], BF16))
            mixed = scr.enter_context(nc.sbuf_tensor(f"mixed{half}", [128, 8, 256], BF16))
            avst = scr.enter_context(nc.sbuf_tensor(f"avst{half}", [128, 256], F32))
            rr = scr.enter_context(nc.sbuf_tensor(f"rr{half}", [1, 2048], BF16))
            rrL = scr.enter_context(nc.sbuf_tensor(f"rrL{half}", [1, 512], BF16))
            rrt = scr.enter_context(nc.sbuf_tensor(f"rrt{half}", [1, 512], F32))
            ptl = scr.enter_context(nc.sbuf_tensor(f"ptl{half}", [65, 512], BF16))
            latst = scr.enter_context(nc.sbuf_tensor(f"latst{half}", [128, 256], F32))
            latout = scr.enter_context(nc.sbuf_tensor(f"latout{half}", [128, 2, 256], BF16))
            gsb = scr.enter_context(nc.sbuf_tensor(f"gsb{half}", [16, 256], BF16))
            gtmp = scr.enter_context(nc.sbuf_tensor(f"gtmp{half}", [16, 256], F32))
            dtmp = scr.enter_context(nc.sbuf_tensor(f"dtmp{half}", [128, 256], BF16))

            v.wait_ge(s_sy, m_rkvall)
            inc(v.tensor_copy(rkbf[:, :], rkv_all[:, 0:256]), s_dv, 1)
            m_rkbf = val(s_dv)

            tn.wait_ge(s_dv, m_rkbf)
            tn.wait_ge(*pfree[6])
            for h in range(4):
                ins = tn.matmul(P6[0:64, h * 64:(h + 1) * 64],
                                rkbf[:, h * 64:(h + 1) * 64], ident[0:64, 0:64],
                                start=True, stop=True)
            inc(ins, s_pe, 1)
            m_rkt_mm = val(s_pe)

            v.wait_ge(s_pe, m_rkt_mm)
            m = inc(v.tensor_copy(rkT[:, :, :],
                                  P6[0:64, 0:256].rearrange("p (h n) -> p h n", h=4)),
                    s_dv, 1)
            pfree[6] = (s_dv, m)
            for h in range(4):
                v.tensor_copy(rv_aug[:, h, :],
                              rkv_all[:, 256 + h * 64:256 + (h + 1) * 64])
            v.memset(otcol[0:64, :], 1.0)
            v.memset(otcol[64:65, :], TINY)
            inc(v.memset(ptl[64:65, :], 1.0), s_dv, 1)
            m_lat_prep = val(s_dv)

            exp_m = {}
            av_m = {}
            avmult_m = {}
            recip_m = {}
            bc_m = {}
            lat_ms = {}
            m_gl_prev = 0
            m_mix_prev = 0
            m_oww_prev = 0

            for tq in range(attn_tq):
                w0 = tq + half
                tcol = (tq % 2) * 128
                use_F = (half == 0 and tq == 0)
                m0 = maskF if use_F else (maskA if tq % 2 == 0 else maskB)

                def emit_scores(bb_):
                    par_ = bb_ % 2
                    heads_ = [4 * bb_, 4 * bb_ + 2, 4 * bb_ + 1, 4 * bb_ + 3]
                    ins_ = None
                    for s_i_, hh_ in enumerate(heads_):
                        r0_ = 64 * (hh_ % 2)
                        dch_ = hh_ // 2
                        cb_ = [0, 1] if s_i_ < 2 else [2, 3]
                        scol_ = par_ * 256 + (s_i_ % 2) * 128
                        for c_ in range(2):
                            ko_ = 128 * (w0 + c_)
                            ins_ = tn.matmul(P[cb_[c_]][:, scol_:scol_ + 128],
                                             k_fm[r0_:r0_ + 64, dch_, ko_:ko_ + 128],
                                             q_fm[r0_:r0_ + 64, dch_, tq * 128:(tq + 1) * 128],
                                             start=True, stop=True)
                    inc(ins_, s_pe, 1)
                    return val(s_pe)

                sc_mm = {}
                for bb in range(4):
                    avbank = 4 + bb % 2
                    par = bb % 2
                    heads = [4 * bb, 4 * bb + 2, 4 * bb + 1, 4 * bb + 3]

                    if bb == 0:
                        if (tq, bb) == (0, 0):
                            tn.wait_ge(s_dv, m_v_done)
                            tn.wait_ge(s_sc, m_rq_done)
                        for b_ in range(4):
                            tn.wait_ge(*pfree[b_])
                        sc_mm[0] = emit_scores(0)
                        sc_mm[1] = emit_scores(1)
                    elif bb < 3:
                        tn.wait_ge(s_sc, exp_m[(tq, bb - 1)])
                        sc_mm[bb + 1] = emit_scores(bb + 1)
                    m_sc_mm = sc_mm[bb]

                    if attn_sub < 2:
                        exp_m[(tq, bb)] = val(s_sc)
                        av_m[(tq, bb)] = val(s_pe)
                        recip_m[(tq, bb)] = val(s_dv)
                        bc_m[(tq, bb)] = val(s_pe)
                        avmult_m[(tq, bb)] = val(s_dv)
                        continue

                    s.wait_ge(s_pe, m_sc_mm)
                    if (tq, bb) != (0, 0):
                        prev = (tq, bb - 1) if bb > 0 else (tq - 1, 3)
                        s.wait_ge(s_pe, av_m[prev])
                    pc = par * 256
                    s.activation(pt_sb[:, 0, 0:256], P[0][:, pc:pc + 256], AF.Exp,
                                 bias=BSH, scale=0.125)
                    s.activation(pt_sb[:, 0, 256:512], P[2][:, pc:pc + 256], AF.Exp,
                                 bias=BSH, scale=0.125)
                    s.activation(pt_sb[:, 1, 0:256], P[1][:, pc:pc + 256], AF.Exp,
                                 bias=BSH, scale=0.125)
                    inc(s.activation(pt_sb[:, 1, 256:512], P[3][:, pc:pc + 256], AF.Exp,
                                     bias=BSH, scale=0.125), s_sc, 1)
                    exp_m[(tq, bb)] = val(s_sc)
                    for b_ in range(4):
                        pfree[b_] = (s_sc, exp_m[(tq, bb)])

                    if attn_sub < 3:
                        av_m[(tq, bb)] = val(s_pe)
                        recip_m[(tq, bb)] = val(s_dv)
                        bc_m[(tq, bb)] = val(s_pe)
                        avmult_m[(tq, bb)] = val(s_dv)
                        continue

                    v.wait_ge(s_sc, exp_m[(tq, bb)])
                    v.tensor_tensor(pt_sb[:, 0, :], pt_sb[:, 0, :], m0[:, :], ALU.mult)
                    inc(v.tensor_tensor(pt_sb[:, 1, :], pt_sb[:, 1, :], maskC[:, :],
                                        ALU.mult), s_dv, 1)
                    m_mask = val(s_dv)

                    if attn_sub < 4:
                        av_m[(tq, bb)] = val(s_pe)
                        recip_m[(tq, bb)] = val(s_dv)
                        bc_m[(tq, bb)] = val(s_pe)
                        avmult_m[(tq, bb)] = val(s_dv)
                        continue

                    tn.wait_ge(s_dv, m_mask)
                    key2 = (tq, bb - 2) if bb >= 2 else (tq - 1, bb + 2)
                    if key2 in avmult_m:
                        tn.wait_ge(s_dv, avmult_m[key2])
                    prevd = (tq, bb - 1) if bb > 0 else (tq - 1, 3)
                    if prevd in recip_m:
                        tn.wait_ge(s_dv, recip_m[prevd])
                    else:
                        tn.wait_ge(*pfree[6])
                    for s_i, hh in enumerate(heads):
                        r0 = 64 * (hh % 2)
                        scol = (s_i % 2) * 128
                        for c in range(2):
                            tn.matmul(P[avbank][r0:r0 + 64, scol:scol + 128],
                                      v_tm[:, w0 + c, hh * 64:(hh + 1) * 64],
                                      pt_sb[:, c, s_i * 128:(s_i + 1) * 128],
                                      start=(c == 0), stop=(c == 1))
                    tn.matmul(P6[0:1, :], ones_bf[:, 0:1], pt_sb[:, 0, :],
                              start=True, stop=False)
                    inc(tn.matmul(P6[0:1, :], ones_bf[:, 0:1], pt_sb[:, 1, :],
                                  start=False, stop=True), s_pe, 1)
                    av_m[(tq, bb)] = val(s_pe)

                    if attn_sub < 5:
                        recip_m[(tq, bb)] = val(s_dv)
                        bc_m[(tq, bb)] = val(s_pe)
                        avmult_m[(tq, bb)] = val(s_dv)
                        continue

                    v.wait_ge(s_pe, av_m[(tq, bb)])
                    if (tq - 1, bb) in bc_m:
                        v.wait_ge(s_pe, bc_m[(tq - 1, bb)])
                    v.tensor_scalar(rrt[0:1, :], P6[0:1, :], TINY, None, ALU.add)
                    with nc.allow_low_precision(reason="softmax recip as one bf16 factor"):
                        inc(v.reciprocal(rr[0:1, bb * 512:(bb + 1) * 512], rrt[0:1, :]),
                            s_dv, 1)
                    recip_m[(tq, bb)] = val(s_dv)

                    if attn_sub < 6:
                        bc_m[(tq, bb)] = val(s_pe)
                        avmult_m[(tq, bb)] = val(s_dv)
                        continue

                    tn.wait_ge(s_dv, recip_m[(tq, bb)])
                    key2 = (tq, bb - 2) if bb >= 2 else (tq - 1, bb + 2)
                    if key2 in avmult_m:
                        tn.wait_ge(s_dv, avmult_m[key2])
                    if bb < 2 and (tq - 1) in lat_ms:
                        tn.wait_ge(s_dv, lat_ms[tq - 1]["mult"])
                    ins = None
                    for s_i, hh in enumerate(heads):
                        r0 = 64 * (hh % 2)
                        pcol = (bb % 2) * 256 + (s_i % 2) * 128
                        ins = tn.matmul(P7[r0:r0 + 64, pcol:pcol + 128],
                                        ones_bf[0:1, 0:64],
                                        rr[0:1, bb * 512 + s_i * 128:bb * 512 + (s_i + 1) * 128],
                                        start=True, stop=True)
                    inc(ins, s_pe, 1)
                    bc_m[(tq, bb)] = val(s_pe)

                    if attn_sub < 7:
                        avmult_m[(tq, bb)] = val(s_dv)
                        continue

                    s.wait_ge(s_pe, av_m[(tq, bb)])
                    key1 = (tq, bb - 1) if bb > 0 else (tq - 1, 3)
                    if key1 in avmult_m:
                        s.wait_ge(s_dv, avmult_m[key1])
                    inc(s.copy(avst[:, 0:256], P[avbank][:, 0:256]), s_sc, 1)
                    m_avevac = val(s_sc)
                    pfree[avbank] = (s_sc, m_avevac)

                    if attn_sub < 8:
                        avmult_m[(tq, bb)] = val(s_dv)
                        continue

                    v.wait_ge(s_pe, bc_m[(tq, bb)])
                    v.wait_ge(s_sc, m_avevac)
                    if m_gl_prev:
                        v.wait_ge(s_pe, m_gl_prev)
                    if m_mix_prev:
                        v.wait_ge(s_dv, m_mix_prev)
                    ins = None
                    for s_i, hh in enumerate(heads):
                        r0 = 64 * (hh % 2)
                        pcol = (bb % 2) * 256 + (s_i % 2) * 128
                        ins = v.tensor_tensor(local_n[r0:r0 + 64, hh // 2, tcol:tcol + 128],
                                              avst[r0:r0 + 64,
                                                   (s_i % 2) * 128:(s_i % 2) * 128 + 128],
                                              P7[r0:r0 + 64, pcol:pcol + 128], ALU.mult)
                    inc(ins, s_dv, 1)
                    avmult_m[(tq, bb)] = val(s_dv)

                if not attn_lat:
                    continue
                lm = {}

                tn.wait_ge(*pfree[0])
                tn.wait_ge(s_dv, m_lat_prep)
                ins = None
                for h in range(4):
                    ins = tn.matmul(P[0][0:64, h * 128:(h + 1) * 128], rkT[:, h, :],
                                    rq_sb[:, h, half * 1024 + tq * 128:half * 1024 + (tq + 1) * 128],
                                    start=True, stop=True)
                inc(ins, s_pe, 1)
                lm["lsc"] = val(s_pe)

                s.wait_ge(s_pe, lm["lsc"])
                if (tq - 1) in lat_ms:
                    s.wait_ge(s_pe, lat_ms[tq - 1]["av"])
                inc(s.activation(ptl[0:64, :], P[0][0:64, :], AF.Exp,
                                 bias=BSH[0:64, :], scale=0.125), s_sc, 1)
                lm["exp"] = val(s_sc)
                pfree[0] = (s_sc, lm["exp"])

                v.wait_ge(s_sc, lm["exp"])
                inc(v.tensor_scalar(ptl[0:64, :], ptl[0:64, :],
                                    latmask[:, half * 8 + tq:half * 8 + tq + 1], None,
                                    ALU.mult), s_dv, 1)
                lm["mask"] = val(s_dv)

                tn.wait_ge(s_dv, lm["mask"])
                tn.wait_ge(*pfree[1])
                tn.wait_ge(*pfree[3])
                if (tq - 1) in lat_ms:
                    tn.wait_ge(s_dv, lat_ms[tq - 1]["rec"])
                for h in range(4):
                    r0 = 64 * (h % 2)
                    sl = (256 if h % 2 else 0) + (h // 2) * 128
                    tn.matmul(P[1][r0:r0 + 64, sl:sl + 128], rv_aug[0:64, h, 0:64],
                              ptl[0:64, h * 128:(h + 1) * 128], start=True, stop=True)
                ins = None
                for h in range(4):
                    ins = tn.matmul(P[3][0:1, h * 128:(h + 1) * 128], otcol[0:65, 0:1],
                                    ptl[0:65, h * 128:(h + 1) * 128], start=True, stop=True)
                inc(ins, s_pe, 1)
                lm["av"] = val(s_pe)

                v.wait_ge(s_pe, lm["av"])
                if (tq - 1) in lat_ms:
                    v.wait_ge(s_pe, lat_ms[tq - 1]["bc"])
                with nc.allow_low_precision(reason="softmax recip as one bf16 factor"):
                    inc(v.reciprocal(rrL[0:1, 0:512], P[3][0:1, :]), s_dv, 1)
                lm["rec"] = val(s_dv)
                pfree[3] = (s_dv, lm["rec"])

                tn.wait_ge(s_dv, lm["rec"])
                tn.wait_ge(s_dv, avmult_m[(tq, 2)])
                if (tq - 1) in lat_ms:
                    tn.wait_ge(s_dv, lat_ms[tq - 1]["mult"])
                ins = None
                for h in range(4):
                    r0 = 64 * (h % 2)
                    pcol = (h // 2) * 128
                    ins = tn.matmul(P7[r0:r0 + 64, pcol:pcol + 128], ones_bf[0:1, 0:64],
                                    rrL[0:1, h * 128:(h + 1) * 128], start=True, stop=True)
                inc(ins, s_pe, 1)
                lm["bc"] = val(s_pe)

                s.wait_ge(s_pe, lm["av"])
                if (tq - 1) in lat_ms:
                    s.wait_ge(s_dv, lat_ms[tq - 1]["mult"])
                s.copy(latst[0:64, 0:256], P[1][0:64, 0:256])
                inc(s.copy(latst[64:128, 0:256], P[1][64:128, 256:512]), s_sc, 1)
                lm["evac"] = val(s_sc)
                pfree[1] = (s_sc, lm["evac"])

                v.wait_ge(s_pe, lm["bc"])
                v.wait_ge(s_sc, lm["evac"])
                ins = None
                for h in range(4):
                    r0 = 64 * (h % 2)
                    ins = v.tensor_tensor(latout[r0:r0 + 64, h // 2, tcol:tcol + 128],
                                          latst[r0:r0 + 64, (h // 2) * 128:(h // 2) * 128 + 128],
                                          P7[r0:r0 + 64, (h // 2) * 128:(h // 2) * 128 + 128],
                                          ALU.mult)
                inc(ins, s_dv, 1)
                lm["mult"] = val(s_dv)
                lat_ms[tq] = lm

                if tq % 2 == 1 and attn_grp:
                    grp_rout_mm = {}
                    for oc in range(8):
                        bank = 2 + oc % 2

                        tn.wait_ge(s_dv, lat_ms[tq]["mult"])
                        tn.wait_ge(s_dv, lat_ms[tq - 1]["mult"])
                        tn.wait_ge(*pfree[bank])
                        for dc in range(2):
                            ins = tn.matmul(P[bank][:, 0:256],
                                            rout_sb[:, oc, dc * 128:(dc + 1) * 128],
                                            latout[:, dc, 0:256], start=(dc == 0), stop=(dc == 1))
                        inc(ins, s_pe, 1)
                        mmv = val(s_pe)

                        s.wait_ge(s_pe, mmv)
                        if m_mix_prev:
                            s.wait_ge(s_dv, m_mix_prev)
                        m = inc(s.activation(remote_n[:, oc, :], P[bank][:, 0:256],
                                             AF.Identity, bias=bcol[:, 24 + oc:25 + oc],
                                             scale=1.0), s_sc, 1)
                        pfree[bank] = (s_sc, m)
                    m_remote = val(s_sc)

                    tn.wait_ge(s_sc, m_remote)
                    tn.wait_ge(s_dv, avmult_m[(tq, 3)])
                    tn.wait_ge(*pfree[5])
                    tn.wait_ge(s_dv, lat_ms[tq]["rec"])
                    for dc in range(8):
                        tn.matmul(P[5][0:16, 0:256], bdg[:, dc, 0:16], local_n[:, dc, :],
                                  start=(dc == 0), stop=False)
                    for dc in range(8):
                        ins = tn.matmul(P[5][0:16, 0:256], bdg[:, dc, 16:32], remote_n[:, dc, :],
                                        start=False, stop=(dc == 7))
                    inc(ins, s_pe, 1)
                    m_gl = val(s_pe)
                    m_gl_prev = m_gl

                    s.wait_ge(s_pe, m_gl)
                    inc(s.activation(gtmp[:, :], P[5][0:16, 0:256], AF.Exp,
                                     bias=ZCOL[0:16, :], scale=-1.0), s_sc, 1)
                    m_gexp = val(s_sc)
                    pfree[5] = (s_sc, m_gexp)

                    v.wait_ge(s_sc, m_gexp)
                    inc(v.tensor_scalar(gtmp[:, :], gtmp[:, :], 1.0, None, ALU.add), s_dv, 1)
                    v.wait_ge(s_dv, val(s_dv))
                    inc(v.reciprocal(gtmp[:, :], gtmp[:, :]), s_dv, 1)
                    v.wait_ge(s_dv, val(s_dv))
                    inc(v.tensor_copy(gsb[:, :], gtmp[:, :]), s_dv, 1)
                    m_gate = val(s_dv)

                    m_mix_rounds = []
                    for rnd in range(2):
                        tn.wait_ge(s_dv, m_gate)
                        tn.wait_ge(*pfree[0])
                        tn.wait_ge(*pfree[1])
                        for k in range(4):
                            dc = rnd * 4 + k
                            bank = k % 2
                            ins = tn.matmul(P[bank][:, (k // 2) * 256:(k // 2) * 256 + 256],
                                            ebc[:, dc, :], gsb[:, :], start=True, stop=True)
                        inc(ins, s_pe, 1)
                        m_gbc = val(s_pe)

                        v.wait_ge(s_pe, m_gbc)
                        if m_oww_prev:
                            v.wait_ge(s_pe, m_oww_prev)
                        for k in range(4):
                            dc = rnd * 4 + k
                            bank = k % 2
                            v.tensor_tensor(dtmp[:, :], local_n[:, dc, :],
                                            remote_n[:, dc, :], ALU.subtract)
                            v.tensor_tensor(dtmp[:, :], dtmp[:, :],
                                            P[bank][:, (k // 2) * 256:(k // 2) * 256 + 256],
                                            ALU.mult)
                            ins = v.tensor_tensor(mixed[:, dc, :], dtmp[:, :],
                                                  remote_n[:, dc, :], ALU.add)
                        inc(ins, s_dv, 1)
                        m_mix_rounds.append(val(s_dv))
                        pfree[0] = (s_dv, m_mix_rounds[-1])
                        pfree[1] = (s_dv, m_mix_rounds[-1])
                    m_mix_done = m_mix_rounds[-1]
                    m_mix_prev = m_mix_done

                    for tt in range(2):
                        tloc = (tq - 1) + tt
                        bank0, bank1 = (2, 3) if tt == 0 else (4, 5)

                        tn.wait_ge(s_dv, m_mix_done)
                        tn.wait_ge(s_gp, m_ow)
                        tn.wait_ge(*pfree[bank0])
                        tn.wait_ge(*pfree[bank1])
                        for dc in range(8):
                            tn.matmul(P[bank0][:, :], mixed[:, dc, tt * 128:(tt + 1) * 128],
                                      ow_sb[:, dc, 0:512], start=(dc == 0), stop=False)
                        tn.matmul(P[bank0][:, :], ones_bf[0:1, 0:128],
                                  bvob[0:1, 1024:1536], start=False, stop=False)
                        tn.matmul(P[bank0][:, :], ones_bf[0:1, 0:128],
                                  bvob_lo[0:1, 1024:1536], start=False, stop=True)
                        for dc in range(8):
                            tn.matmul(P[bank1][:, :], mixed[:, dc, tt * 128:(tt + 1) * 128],
                                      ow_sb[:, dc, 512:1024], start=(dc == 0), stop=False)
                        tn.matmul(P[bank1][:, :], ones_bf[0:1, 0:128],
                                  bvob[0:1, 1536:2048], start=False, stop=False)
                        inc(tn.matmul(P[bank1][:, :], ones_bf[0:1, 0:128],
                                      bvob_lo[0:1, 1536:2048], start=False, stop=True), s_pe, 1)
                        m_oww = val(s_pe)

                        s.wait_ge(s_pe, m_oww)
                        if m_stm_free[tloc][1] > 0:
                            s.wait_ge(*m_stm_free[tloc])
                        s.copy(s_tm[:, tloc, 0:512], P[bank0][:, :])
                        m = inc(s.copy(s_tm[:, tloc, 512:1024], P[bank1][:, :]), s_sc, 1)
                        pfree[bank0] = (s_sc, m)
                        pfree[bank1] = (s_sc, m)
                    m_oww_prev = m_oww

                    if half == 0 and tq == 1:
                        if dbg_out("g0_local", [128, 8, 256]) is not None:
                            g.wait_ge(s_dv, m_mix_done)
                            inc(g.dma_start(out=dbg["g0_local"], in_=local_n[:, :, :]), s_gp, 16)
                            inc(g.dma_start(out=dbg["g0_remote"], in_=remote_n[:, :, :]), s_gp, 16)
                            inc(g.dma_start(out=dbg["g0_mixed"], in_=mixed[:, :, :]), s_gp, 16)
                            inc(g.dma_start(out=dbg["g0_gate"], in_=gsb[:, :]), s_gp, 16)

            if attn_tq < 8 or not attn_lat or not attn_grp or attn_sub < 8:
                _cut()
                scr.close()
                att.close()
                return nc, dbg

            m_attn_sc = val(s_sc)
            m_attn_dv = val(s_dv)

            if half == 0 and dbg_out("s_tm", [128, 8, 1024]) is not None:
                sy.wait_ge(s_sc, m_attn_sc)
                inc(sy.dma_start(out=dbg["s_tm"], in_=s_tm[:, :, :]), s_st, 16)
            if half == 0 and dbg_out("local_n", [128, 8, 256]) is not None:
                g.wait_ge(s_dv, m_attn_dv)
                inc(g.dma_start(out=dbg["local_n"], in_=local_n[:, :, :]), s_gp, 16)
            if half == 0 and dbg_out("remote_n", [128, 8, 256]) is not None:
                g.wait_ge(s_dv, m_attn_dv)
                inc(g.dma_start(out=dbg["remote_n"], in_=remote_n[:, :, :]), s_gp, 16)

            scr.close()
            att.close()

            if "ffn" not in phases:
                arena_free = [(s_sc, m_attn_sc), (s_dv, m_attn_dv)]
                continue

            ffn = ExitStack()
            h_sb = ffn.enter_context(nc.sbuf_tensor(f"h_sb{half}", [128, 24, 1024], BF16))
            wd_sb = ffn.enter_context(nc.sbuf_tensor(f"wd_sb{half}", [128, 24, 1024], BF16))
            wg_buf = ffn.enter_context(nc.sbuf_tensor(f"wg_buf{half}", [128, 2, 1024], BF16))
            wu_buf = ffn.enter_context(nc.sbuf_tensor(f"wu_buf{half}", [128, 2, 1024], BF16))
            sg_buf = ffn.enter_context(nc.sbuf_tensor(f"sg_buf{half}", [128, 2, 512], BF16))
            g_buf = ffn.enter_context(nc.sbuf_tensor(f"g_buf{half}", [128, 2, 512], F32))
            xre = ffn.enter_context(nc.sbuf_tensor(f"xre{half}", [128, 1024], F32))
            PF = [ffn.enter_context(nc.psum_tensor(f"PF{i}_{half}", [128, 512], F32))
                  for i in range(2)]
            pfree[6] = (s_pe, 0)
            pfree[7] = (s_pe, 0)

            attn_done = [(s_sc, m_attn_sc), (s_dv, m_attn_dv)]

            ld_gu = [0] * 24
            ld_wd = [0] * 24
            mm_gu = [0] * 24

            for sem, vv in attn_done:
                g.wait_ge(sem, vv)
            for j in range(2):
                inc(g.dma_start(out=wg_buf[:, j, :], in_=wgt_in[j]), s_gp, 16)
                ld_gu[j] = inc(g.dma_start(out=wu_buf[:, j, :], in_=wut_in[j]), s_gp, 16)

            for hc in range(24):
                bsel = hc % 2
                if bsel == 0:
                    pg = [P[0], P[1], P[2], P[3]]
                    bk = [0, 1, 2, 3]
                else:
                    pg = [P[4], P[5], PF[0], PF[1]]
                    bk = [4, 5, 6, 7]

                tn.wait_ge(s_gp, ld_gu[hc])
                for b in bk:
                    tn.wait_ge(*pfree[b])
                for gi in range(2):
                    for dc in range(8):
                        tn.matmul(pg[gi][:, :], wg_buf[:, bsel, dc * 128:(dc + 1) * 128],
                                  xn_fm[:, dc, qbase + gi * 512:qbase + (gi + 1) * 512],
                                  start=(dc == 0), stop=(dc == 7))
                    for dc in range(8):
                        ins = tn.matmul(pg[2 + gi][:, :], wu_buf[:, bsel, dc * 128:(dc + 1) * 128],
                                        xn_fm[:, dc, qbase + gi * 512:qbase + (gi + 1) * 512],
                                        start=(dc == 0), stop=(dc == 7))
                inc(ins, s_pe, 1)
                mm_gu[hc] = val(s_pe)

                s.wait_ge(s_pe, mm_gu[hc])
                if hc >= 1:
                    pass
                s.activation(sg_buf[:, 0, :], pg[0][:, :], AF.Silu, scale=SILU1)
                inc(s.activation(sg_buf[:, 1, :], pg[1][:, :], AF.Silu, scale=SILU1),
                    s_sc, 1)
                m_silu = val(s_sc)
                pfree[bk[0]] = (s_sc, m_silu)
                pfree[bk[1]] = (s_sc, m_silu)

                v.wait_ge(s_sc, m_silu)
                v.tensor_tensor(h_sb[:, hc, 0:512], sg_buf[:, 0, :], pg[2][:, :], ALU.mult)
                inc(v.tensor_tensor(h_sb[:, hc, 512:1024], sg_buf[:, 1, :], pg[3][:, :],
                                    ALU.mult), s_dv, 1)
                m_h = val(s_dv)
                pfree[bk[2]] = (s_dv, m_h)
                pfree[bk[3]] = (s_dv, m_h)

                if hc + 2 < 24:
                    g.wait_ge(s_pe, mm_gu[hc])
                    inc(g.dma_start(out=wg_buf[:, hc % 2, :], in_=wgt_in[hc + 2]), s_gp, 16)
                    ld_gu[hc + 2] = inc(g.dma_start(out=wu_buf[:, hc % 2, :],
                                                    in_=wut_in[hc + 2]), s_gp, 16)
                ld_wd[hc] = inc(g.dma_start(out=wd_sb[:, hc, :], in_=wdt_in[hc]), s_gp, 16)
            m_h_all = val(s_dv)
            m_wd_all = ld_wd[23]

            m_epi_prev = 0
            for tt in range(8):
                bank0, bank1 = (0, 1) if tt % 2 == 0 else (2, 3)

                if tt == 0:
                    for sem, vv in attn_done:
                        sy.wait_ge(sem, vv)
                if m_epi_prev:
                    sy.wait_ge(s_dv, m_epi_prev)
                inc(sy.dma_start(out=xre[:, :],
                                 in_=x_in[qbase + tt * 128:qbase + (tt + 1) * 128, :]),
                    s_sy, 16)
                m_xre = val(s_sy)

                tn.wait_ge(s_dv, m_h_all)
                tn.wait_ge(s_gp, m_wd_all)
                tn.wait_ge(*pfree[bank0])
                tn.wait_ge(*pfree[bank1])
                for hc in range(24):
                    tn.matmul(P[bank0][:, :], h_sb[:, hc, tt * 128:(tt + 1) * 128],
                              wd_sb[:, hc, 0:512], start=(hc == 0), stop=(hc == 23))
                for hc in range(24):
                    ins = tn.matmul(P[bank1][:, :], h_sb[:, hc, tt * 128:(tt + 1) * 128],
                                    wd_sb[:, hc, 512:1024], start=(hc == 0), stop=(hc == 23))
                inc(ins, s_pe, 1)
                m_down = val(s_pe)

                s.wait_ge(s_pe, m_down)
                s.activation(g_buf[:, 0, :], P[bank0][:, :], AF.Silu, scale=SILU2)
                inc(s.activation(g_buf[:, 1, :], P[bank1][:, :], AF.Silu, scale=SILU2),
                    s_sc, 1)
                m_g = val(s_sc)
                pfree[bank0] = (s_sc, m_g)
                pfree[bank1] = (s_sc, m_g)

                v.wait_ge(s_sc, m_g)
                v.wait_ge(s_sy, m_xre)
                for gi in range(2):
                    v.tensor_tensor(g_buf[:, gi, :], g_buf[:, gi, :],
                                    s_tm[:, tt, gi * 512:(gi + 1) * 512], ALU.mult)
                inc(v.tensor_tensor(xre[:, :], xre[:, :],
                                    g_buf[:, :, :].rearrange("p a b -> p (a b)"), ALU.add),
                    s_dv, 1)
                m_epi = val(s_dv)
                m_epi_prev = m_epi
                m_stm_free[tt] = (s_dv, m_epi)

                sy.wait_ge(s_dv, m_epi)
                inc(sy.dma_start(
                    out=y_out[half * 1024 + tt * 128:half * 1024 + (tt + 1) * 128, :],
                    in_=xre[:, :]), s_st, 16)

            ffn.close()
            arena_free = [(s_dv, val(s_dv)), (s_sc, val(s_sc)), (s_pe, val(s_pe))]

        sy.wait_ge(s_st, val(s_st))
        sy.wait_ge(s_gp, val(s_gp))

    return nc, dbg


def _prep_inputs(inputs):
    x = np.asarray(inputs["L0_attn_x"], np.float32).reshape(S, D)
    qkv_w = np.asarray(inputs["L0_attn_qkv_w"], np.float32)
    qkv_b = np.asarray(inputs["L0_attn_qkv_b"], np.float32)
    rq_w = np.asarray(inputs["L0_attn_rq_w"], np.float32)
    rq_b = np.asarray(inputs["L0_attn_rq_b"], np.float32)
    rkv_w = np.asarray(inputs["L0_attn_rkv_w"], np.float32)
    rkv_b = np.asarray(inputs["L0_attn_rkv_b"], np.float32)
    rout_w = np.asarray(inputs["L0_attn_rout_w"], np.float32)
    rout_b = np.asarray(inputs["L0_attn_rout_b"], np.float32)
    out_w = np.asarray(inputs["L0_attn_out_w"], np.float32)
    out_b = np.asarray(inputs["L0_attn_out_b"], np.float32)
    gl_w = np.asarray(inputs["L0_attn_gl_w"], np.float32)
    gr_w = np.asarray(inputs["L0_attn_gr_w"], np.float32)
    w_gate = np.asarray(inputs["L0_ffn_w_gate"], np.int32)
    w_up = np.asarray(inputs["L0_ffn_w_up"], np.int32)
    w_down = np.asarray(inputs["L0_ffn_w_down"], np.int32)

    x_pad = np.concatenate([np.zeros((HALO, D), np.float32), x], 0)

    qkvw_qk = np.ascontiguousarray(
        qkv_w[:, 0:2048].reshape(8, 128, 16, 128).transpose(2, 1, 0, 3).reshape(16, 128, 1024))
    qkvw_v = np.ascontiguousarray(qkv_w[:, 2048:3072].reshape(8, 128, 1024))
    outw_r = np.ascontiguousarray(out_w.reshape(8, 128, 1024))
    rqw_r = np.ascontiguousarray(
        rq_w.reshape(8, 128, 4, 64).transpose(2, 1, 0, 3).reshape(4, 128, 512))
    rkvw_r = np.ascontiguousarray(rkv_w.reshape(8, 128, 512))
    routw_r = np.ascontiguousarray(
        rout_w.reshape(2, 128, 8, 128).transpose(2, 1, 0, 3).reshape(8, 128, 256))
    wgt_r = np.ascontiguousarray(
        w_gate.reshape(24, 128, 8, 128).transpose(0, 3, 2, 1).reshape(24, 128, 1024))
    wut_r = np.ascontiguousarray(
        w_up.reshape(24, 128, 8, 128).transpose(0, 3, 2, 1).reshape(24, 128, 1024))
    wdt_r = np.ascontiguousarray(w_down.T.reshape(24, 128, 1024))

    bcolv = np.zeros((128, 40), np.float32)
    bcolv[:, 0:24] = qkv_b.reshape(24, 128).T
    bcolv[:, 24:32] = rout_b.reshape(8, 128).T
    bcolv[0:64, 32:36] = rq_b.reshape(4, 64).T
    browv = np.zeros((1, 2560), np.float32)
    browv[0, 0:1024] = qkv_b[2048:3072]
    browv[0, 1024:2048] = out_b
    browv[0, 2048:2560] = rkv_b

    bdgv = np.zeros((1024, 32), np.float32)
    for h in range(16):
        bdgv[h * 64:(h + 1) * 64, h] = gl_w[:, 0]
        bdgv[h * 64:(h + 1) * 64, 16 + h] = gr_w[:, 0]
    bdgv = np.ascontiguousarray(bdgv.reshape(8, 128, 32).transpose(1, 0, 2))

    ebcv = np.zeros((16, 8, 128), ml_dtypes.bfloat16)
    for dc in range(8):
        for pp in range(128):
            ebcv[(dc * 128 + pp) // 64, dc, pp] = 1.0

    identv = np.eye(128, dtype=ml_dtypes.bfloat16)

    r = np.arange(128)
    c = np.arange(128)
    m0 = (c[:, None] >= r[None, :]).astype(np.float32)
    m0e = m0.copy()
    m0e[0, 0] = 0.0
    m1 = (c[:, None] < r[None, :]).astype(np.float32)
    maskAv = np.tile(m0e, (1, 4)).astype(ml_dtypes.bfloat16)
    maskBv = np.tile(m0, (1, 4)).astype(ml_dtypes.bfloat16)
    maskCv = np.tile(m1, (1, 4)).astype(ml_dtypes.bfloat16)
    maskF0 = np.zeros((128, 512), ml_dtypes.bfloat16)

    shared = dict(qkvw_qk=qkvw_qk, qkvw_v=qkvw_v, outw_in=outw_r, rqw_in=rqw_r,
                  rkvw_in=rkvw_r, routw_in=routw_r, wgt_in=wgt_r, wut_in=wut_r,
                  wdt_in=wdt_r, bcol_in=bcolv, brow_in=browv, bdg_in=bdgv,
                  ebc_in=ebcv, ident_in=identv, identf_in=np.eye(64, dtype=np.float32),
                  maskA_in=maskAv, maskB_in=maskBv,
                  maskC_in=maskCv)
    in_maps = []
    ll = np.arange(64)
    for cidx in range(NCORES):
        m = dict(shared)
        m["x_in"] = np.ascontiguousarray(x_pad[cidx * T:cidx * T + TB])
        m["maskF_in"] = maskF0 if cidx == 0 else maskAv
        lmv = np.zeros((64, 16), np.float32)
        for jj in range(16):
            lmv[:, jj] = (ll < cidx * 8 + jj // 2).astype(np.float32)
        m["latmask_in"] = lmv
        in_maps.append(m)
    return in_maps


def kernel(**inputs):
    global LAST_DEBUG
    in_maps = _prep_inputs(inputs)
    nc, dbg = build(phases=PHASES, debug=DEBUG)
    res = run_bass_kernel_spmd(nc, in_maps, core_ids=list(range(NCORES)))
    LAST_DEBUG = {}
    for name in dbg:
        LAST_DEBUG[name] = [res.results[cc].get("dbg_" + name) for cc in range(NCORES)]
    out = np.concatenate([res.results[cc]["y_out"] for cc in range(NCORES)], axis=0)
    return out.reshape(1, S, D).astype(np.float32)


# version 12 (final)
# speedup vs baseline: 1.1491x; 1.0115x over previous
import numpy as np
import ml_dtypes
from contextlib import ExitStack

import concourse.bass as bass
import concourse.mybir as mybir
from concourse.bass_utils import run_bass_kernel_spmd

F32 = mybir.dt.float32
BF16 = mybir.dt.bfloat16
I32 = mybir.dt.int32
AF = mybir.ActivationFunctionType
ALU = mybir.AluOpType
AX = mybir.AxisListType

NCORES = 8
S = 16384
D = 1024
T = 2048
HALO = 128
TB = T + HALO
NT = TB // 128
BSHIFT = 40.0
SILU1 = 1.0 / 127.0
SILU2 = 1.0 / (127.0 * 127.0)
TINY = 1e-30

PHASES = ("ln", "qkv", "attn", "ffn")
DEBUG = ()
LAST_DEBUG = {}


def build(phases=PHASES, debug=(), nc_kwargs=None, stage=99, attn_tq=8, attn_lat=True,
          attn_grp=True, attn_sub=8):
    debug = set(debug)
    nc = bass.Bass("TRN2", target_bir_lowering=False, **(nc_kwargs or {}))

    x_in = nc.dram_tensor("x_in", [TB, D], F32, kind="ExternalInput").ap()
    qkvw_qk = nc.dram_tensor("qkvw_qk", [16, 128, 1024], F32, kind="ExternalInput").ap()
    qkvw_v = nc.dram_tensor("qkvw_v", [8, 128, 1024], F32, kind="ExternalInput").ap()
    outw_in = nc.dram_tensor("outw_in", [8, 128, 1024], F32, kind="ExternalInput").ap()
    rqw_in = nc.dram_tensor("rqw_in", [4, 128, 512], F32, kind="ExternalInput").ap()
    rkvw_in = nc.dram_tensor("rkvw_in", [8, 128, 512], F32, kind="ExternalInput").ap()
    routw_in = nc.dram_tensor("routw_in", [8, 128, 256], F32, kind="ExternalInput").ap()
    wgt_in = nc.dram_tensor("wgt_in", [24, 128, 1024], I32, kind="ExternalInput").ap()
    wut_in = nc.dram_tensor("wut_in", [24, 128, 1024], I32, kind="ExternalInput").ap()
    wdt_in = nc.dram_tensor("wdt_in", [24, 128, 1024], I32, kind="ExternalInput").ap()
    bcol_in = nc.dram_tensor("bcol_in", [128, 40], F32, kind="ExternalInput").ap()
    brow_in = nc.dram_tensor("brow_in", [1, 2560], F32, kind="ExternalInput").ap()
    bdg_in = nc.dram_tensor("bdg_in", [128, 8, 32], F32, kind="ExternalInput").ap()
    ebc_in = nc.dram_tensor("ebc_in", [16, 8, 128], BF16, kind="ExternalInput").ap()
    ident_in = nc.dram_tensor("ident_in", [128, 128], BF16, kind="ExternalInput").ap()
    identf_in = nc.dram_tensor("identf_in", [64, 64], F32, kind="ExternalInput").ap()
    maskA_in = nc.dram_tensor("maskA_in", [128, 512], BF16, kind="ExternalInput").ap()
    maskB_in = nc.dram_tensor("maskB_in", [128, 512], BF16, kind="ExternalInput").ap()
    maskC_in = nc.dram_tensor("maskC_in", [128, 512], BF16, kind="ExternalInput").ap()
    maskF_in = nc.dram_tensor("maskF_in", [128, 512], BF16, kind="ExternalInput").ap()
    latmask_in = nc.dram_tensor("latmask_in", [64, 16], F32, kind="ExternalInput").ap()

    y_out = nc.dram_tensor("y_out", [T, D], F32, kind="ExternalOutput").ap()
    gin = nc.dram_tensor("gin", [8, 512], F32)
    gout = nc.dram_tensor("gout", [64, 512], F32)

    dbg = {}

    def dbg_out(name, shape):
        if name == "g0_local":
            for nm, shp in [("g0_local", [128, 8, 256]), ("g0_remote", [128, 8, 256]),
                            ("g0_mixed", [128, 8, 256]), ("g0_gate", [16, 256])]:
                if nm in debug and nm not in dbg:
                    dbg[nm] = nc.dram_tensor("dbg_" + nm, shp, F32, kind="ExternalOutput").ap()
            return dbg.get(name)
        if name in debug and name not in dbg:
            dbg[name] = nc.dram_tensor("dbg_" + name, list(shape), F32, kind="ExternalOutput").ap()
        return dbg.get(name)

    ctr = {}

    def inc(instr, sem, n=1):
        instr.then_inc(sem, n)
        ctr[sem.name] = ctr.get(sem.name, 0) + n
        return ctr[sem.name]

    def val(sem):
        return ctr.get(sem.name, 0)

    with ExitStack() as top:
        e = top.enter_context

        xn_fm = e(nc.sbuf_tensor("xn_fm", [128, 8, TB], BF16))
        s_tm = e(nc.sbuf_tensor("s_tm", [128, 8, 1024], F32))
        rq_sb = e(nc.sbuf_tensor("rq_sb", [64, 4, T], BF16))
        rkv_all = e(nc.sbuf_tensor("rkv_all", [64, 512], F32))
        ident = e(nc.sbuf_tensor("ident", [128, 128], BF16))
        identf = e(nc.sbuf_tensor("identf", [64, 64], F32))
        ones_bf = e(nc.sbuf_tensor("ones_bf", [128, 128], BF16))
        tiny_bf = e(nc.sbuf_tensor("tiny_bf", [128, 1], BF16))
        maskA = e(nc.sbuf_tensor("maskA", [128, 512], BF16))
        maskB = e(nc.sbuf_tensor("maskB", [128, 512], BF16))
        maskC = e(nc.sbuf_tensor("maskC", [128, 512], BF16))
        maskF = e(nc.sbuf_tensor("maskF", [128, 512], BF16))
        latmask = e(nc.sbuf_tensor("latmask", [64, 16], F32))
        bdg = e(nc.sbuf_tensor("bdg", [128, 8, 32], BF16))
        bcol = e(nc.sbuf_tensor("bcol", [128, 42], F32))
        bias_rep = e(nc.sbuf_tensor("bias_rep", [128, 2048], F32))

        P = [e(nc.psum_tensor(f"P{i}", [128, 512], F32)) for i in range(6)]

        s_sy = e(nc.semaphore("s_sy"))
        s_gp = e(nc.semaphore("s_gp"))
        s_cc = e(nc.semaphore("s_cc"))
        s_st = e(nc.semaphore("s_st"))
        s_pe = e(nc.semaphore("s_pe"))
        s_dv = e(nc.semaphore("s_dv"))
        s_sc = e(nc.semaphore("s_sc"))

        block = e(nc.Block())
        tn = nc.tensor
        v = nc.vector
        s = nc.scalar
        g = nc.gpsimd
        sy = nc.sync

        BSH = bcol[:, 40:41]
        EPS = bcol[:, 41:42]
        ZCOL = bcol[:, 36:37]

        pfree = {i: (s_pe, 0) for i in range(8)}

        def _cut():
            for sem in (s_sy, s_gp, s_dv, s_sc, s_pe, s_st, s_cc):
                if val(sem):
                    sy.wait_ge(sem, val(sem))


        inc(sy.dma_start(out=ident[:, :], in_=ident_in), s_sy, 16)
        inc(sy.dma_start(out=identf[:, :], in_=identf_in), s_sy, 16)
        inc(sy.dma_start(out=maskA[:, :], in_=maskA_in), s_sy, 16)
        inc(sy.dma_start(out=maskB[:, :], in_=maskB_in), s_sy, 16)
        inc(sy.dma_start(out=maskC[:, :], in_=maskC_in), s_sy, 16)
        inc(sy.dma_start(out=maskF[:, :], in_=maskF_in), s_sy, 16)
        inc(sy.dma_start(out=latmask[:, :], in_=latmask_in), s_sy, 16)
        inc(sy.dma_start(out=bcol[:, 0:40], in_=bcol_in), s_sy, 16)
        m_consts = val(s_sy)

        inc(g.dma_start(out=bdg[:, :, :], in_=bdg_in), s_gp, 16)
        m_consts_gp = val(s_gp)

        v.memset(ones_bf[:, :], 1.0)
        v.memset(tiny_bf[:, :], TINY)
        v.memset(bcol[:, 40:41], -BSHIFT)
        inc(v.memset(bcol[:, 41:42], 1e-5), s_dv, 1)
        m_memsets = val(s_dv)

        ln = ExitStack()
        x_sb = ln.enter_context(nc.sbuf_tensor("x_sb", [128, NT, 1024], F32))
        xn_tm = ln.enter_context(nc.sbuf_tensor("xn_tm", [128, 2, 1024], BF16))
        stats_all = ln.enter_context(nc.sbuf_tensor("stats_all", [128, NT, 12], F32))
        mv_all = ln.enter_context(nc.sbuf_tensor("mv_all", [128, 2 * NT], F32))
        rstd_all = ln.enter_context(nc.sbuf_tensor("rstd_all", [128, NT], F32))
        pooled = ln.enter_context(nc.sbuf_tensor("pooled", [128, 8, 8], F32))
        pooled_bf = ln.enter_context(nc.sbuf_tensor("pooled_bf", [128, 8, 8], BF16))
        rqw_sb = ln.enter_context(nc.sbuf_tensor("rqw_sb", [128, 4, 512], BF16))
        rkvw_sb = ln.enter_context(nc.sbuf_tensor("rkvw_sb", [128, 8, 512], BF16))
        rkv_own = ln.enter_context(nc.sbuf_tensor("rkv_own", [8, 512], F32))
        brkv_f = ln.enter_context(nc.sbuf_tensor("brkv_f", [1, 512], F32))
        brkv = ln.enter_context(nc.sbuf_tensor("brkv", [1, 512], BF16))
        brkv_lo = ln.enter_context(nc.sbuf_tensor("brkv_lo", [1, 512], BF16))
        bvf = ln.enter_context(nc.sbuf_tensor("bvf", [1, 2048], F32))
        bvob = ln.enter_context(nc.sbuf_tensor("bvob", [1, 2048], BF16))
        bvob_lo = ln.enter_context(nc.sbuf_tensor("bvob_lo", [1, 2048], BF16))
        PT = [ln.enter_context(nc.psum_tensor(f"PT{i}", [128, 512], F32)) for i in range(2)]

        sy.wait_ge(s_dv, m_memsets)
        for t in range(NT):
            inc(sy.dma_start(out=x_sb[:, t, :], in_=x_in[t * 128:(t + 1) * 128, :]), s_sy, 16)
        m_xload0 = m_consts

        for h in range(4):
            inc(g.dma_start(out=rqw_sb[:, h, :], in_=rqw_in[h]), s_gp, 16)
        for dc in range(8):
            inc(g.dma_start(out=rkvw_sb[:, dc, :], in_=rkvw_in[dc]), s_gp, 16)
        inc(g.dma_start(out=brkv_f[:, :], in_=brow_in[0:1, 2048:2560]), s_gp, 16)
        inc(g.dma_start(out=bvf[:, :], in_=brow_in[0:1, 0:2048]), s_gp, 16)
        m_smallw = val(s_gp)

        v.wait_ge(s_gp, m_smallw)
        inc(v.tensor_copy(brkv[:, :], brkv_f[:, :]), s_dv, 1)
        v.wait_ge(s_dv, val(s_dv))
        inc(v.scalar_tensor_tensor(brkv_lo[:, :], brkv_f[:, :], 1.0, brkv[:, :],
                                   ALU.mult, ALU.subtract), s_dv, 1)
        inc(v.tensor_copy(bvob[:, :], bvf[:, :]), s_dv, 1)
        v.wait_ge(s_dv, val(s_dv))
        inc(v.scalar_tensor_tensor(bvob_lo[:, :], bvf[:, :], 1.0, bvob[:, :],
                                   ALU.mult, ALU.subtract), s_dv, 1)
        m_bsplit0 = val(s_dv)

        if stage < 2:
            _cut()
            ln.close()
            return nc, dbg

        for t in range(NT):
            v.wait_ge(s_sy, m_xload0 + 16 * (t + 1))
            v.bn_stats(stats_all[:, t, 0:6], x_sb[:, t, 0:512])
            ins = v.bn_stats(stats_all[:, t, 6:12], x_sb[:, t, 512:1024])
        inc(ins, s_dv, 1)
        v.wait_ge(s_dv, val(s_dv))
        for t in range(NT):
            ins = v.bn_aggr(mv_all[:, 2 * t:2 * t + 2], stats_all[:, t, 0:12])
        inc(ins, s_dv, 1)
        m_aggr = val(s_dv)

        s.wait_ge(s_dv, m_aggr)
        s.wait_ge(s_dv, m_memsets)
        inc(s.activation(rstd_all[:, :], mv_all[:, 1:2 * NT:2], AF.Sqrt,
                         bias=EPS, scale=1.0), s_sc, 1)
        m_sqrt = val(s_sc)

        v.wait_ge(s_sc, m_sqrt)
        inc(v.reciprocal(rstd_all[:, :], rstd_all[:, :]), s_dv, 1)
        m_rstd = val(s_dv)

        if stage < 3:
            _cut()
            ln.close()
            return nc, dbg

        apply_m = [0] * NT
        xgrp_m = [[0, 0] for _ in range(NT)]
        evac_m = [[0, 0] for _ in range(NT)]
        for t in range(NT):
            b = t % 2

            if t == 0:
                v.wait_ge(s_dv, m_rstd)
            if t >= 2:
                v.wait_ge(s_pe, xgrp_m[t - 2][1])
            apply_m[t] = inc(
                v.tensor_scalar(xn_tm[:, b, :], x_sb[:, t, :],
                                mv_all[:, 2 * t:2 * t + 1], rstd_all[:, t:t + 1],
                                ALU.subtract, ALU.mult), s_dv, 1)

            tn.wait_ge(s_dv, apply_m[t])
            for gg in range(2):
                if t >= 1:
                    tn.wait_ge(s_dv, evac_m[t - 1][gg])
                for k in range(4):
                    dc = gg * 4 + k
                    ins = tn.matmul(PT[gg][:, k * 128:(k + 1) * 128],
                                    xn_tm[:, b, dc * 128:(dc + 1) * 128], ident[:, :],
                                    start=True, stop=True)
                xgrp_m[t][gg] = inc(ins, s_pe, 1)

            for gg in range(2):
                v.wait_ge(s_pe, xgrp_m[t][gg])
                evac_m[t][gg] = inc(
                    v.tensor_copy(xn_fm[:, 4 * gg:4 * gg + 4, t * 128:(t + 1) * 128],
                                  PT[gg][:, :].rearrange("p (k n) -> p k n", k=4)),
                    s_dv, 1)
        m_ln_done = evac_m[NT - 1][1]

        if dbg_out("xn", [128, 8, 256]) is not None:
            g.wait_ge(s_dv, m_ln_done)
            inc(g.dma_start(out=dbg["xn"], in_=xn_fm[:, :, 0:256]), s_gp, 16)

        br_ev = {}
        tn.wait_ge(s_dv, m_ln_done)
        tn.wait_ge(s_dv, m_bsplit0)
        for gi_ in range(4):
            bsl = gi_ % 2
            if gi_ >= 2:
                tn.wait_ge(s_dv, br_ev[gi_ - 2])
            tn.matmul(PT[bsl][:, :], ones_bf[0:1, 0:128],
                      bvob[0:1, gi_ * 512:(gi_ + 1) * 512], start=True, stop=False)
            inc(tn.matmul(PT[bsl][:, :], ones_bf[0:1, 0:128],
                          bvob_lo[0:1, gi_ * 512:(gi_ + 1) * 512], start=False, stop=True),
                s_pe, 1)
            mbr = val(s_pe)
            v.wait_ge(s_pe, mbr)
            br_ev[gi_] = inc(v.tensor_copy(bias_rep[:, gi_ * 512:(gi_ + 1) * 512],
                                           PT[bsl][:, :]), s_dv, 1)
        m_bias_rep = val(s_dv)

        if stage < 4:
            _cut()
            ln.close()
            return nc, dbg

        v.wait_ge(s_dv, m_ln_done)
        for dc in range(8):
            for j in range(8):
                ins = v.tensor_reduce(pooled[:, dc, j:j + 1],
                                      xn_fm[:, dc, HALO + 256 * j:HALO + 256 * (j + 1)],
                                      AX.X, ALU.add)
        inc(ins, s_dv, 1)
        v.wait_ge(s_dv, val(s_dv))
        inc(v.tensor_scalar(pooled_bf[:, :, :], pooled[:, :, :], 1.0 / 256.0, None,
                            ALU.mult), s_dv, 1)
        m_pooled = val(s_dv)

        tn.wait_ge(s_dv, m_pooled)
        tn.wait_ge(s_dv, m_memsets)
        tn.wait_ge(s_gp, m_smallw)
        tn.wait_ge(s_gp, m_consts_gp)
        tn.wait_ge(s_dv, m_bsplit0)
        for dc in range(8):
            tn.matmul(P[0][0:8, :], pooled_bf[:, dc, :], rkvw_sb[:, dc, :],
                      start=(dc == 0), stop=False)
        tn.matmul(P[0][0:8, :], ones_bf[0:1, 0:8], brkv[0:1, :],
                  start=False, stop=False)
        inc(tn.matmul(P[0][0:8, :], ones_bf[0:1, 0:8], brkv_lo[0:1, :],
                      start=False, stop=True), s_pe, 1)
        m_rkvmm = val(s_pe)

        v.wait_ge(s_pe, m_rkvmm)
        inc(v.tensor_copy(rkv_own[:, :], P[0][0:8, :]), s_dv, 1)
        m_rkvevac = val(s_dv)
        pfree[0] = (s_dv, m_rkvevac)

        if stage < 5:
            _cut()
            ln.close()
            return nc, dbg

        for h in range(4):
            for gg in range(4):
                bank = (h * 4 + gg) % 2

                tn.wait_ge(*pfree[bank])
                tn.wait_ge(s_dv, m_ln_done)
                for dc in range(8):
                    ins = tn.matmul(P[bank][0:64, :],
                                    rqw_sb[:, h, dc * 64:(dc + 1) * 64],
                                    xn_fm[:, dc, HALO + gg * 512:HALO + (gg + 1) * 512],
                                    start=(dc == 0), stop=(dc == 7))
                inc(ins, s_pe, 1)
                rq_mm = val(s_pe)

                s.wait_ge(s_pe, rq_mm)
                m = inc(s.activation(rq_sb[:, h, gg * 512:(gg + 1) * 512], P[bank][0:64, :],
                                     AF.Identity, bias=bcol[0:64, 32 + h:33 + h], scale=1.0),
                        s_sc, 1)
                pfree[bank] = (s_sc, m)
        m_rq_done = val(s_sc)

        if stage < 6:
            _cut()
            ln.close()
            return nc, dbg

        sy.wait_ge(s_dv, m_rkvevac)
        inc(sy.dma_start(out=gin[:, :], in_=rkv_own[:, :]), s_sy, 16)
        m_gin = val(s_sy)

        g.wait_ge(s_sy, m_gin)
        inc(g.collective_compute(
            "AllGather", ALU.bypass, replica_groups=[list(range(NCORES))],
            ins=[gin.ap().opt()], outs=[gout.ap().opt()]), s_cc, 1)
        m_cc = val(s_cc)

        sy.wait_ge(s_cc, m_cc)
        inc(sy.dma_start(out=rkv_all[:, :], in_=gout[:, :]), s_sy, 16)
        m_rkvall = val(s_sy)

        if dbg_out("rkv_all", [64, 512]) is not None:
            inc(sy.dma_start(out=dbg["rkv_all"], in_=rkv_all[:, :]), s_st, 16)
        if dbg_out("rq", [64, 4, 512]) is not None:
            g.wait_ge(s_sc, m_rq_done)
            inc(g.dma_start(out=dbg["rq"], in_=rq_sb[:, :, 0:512]), s_gp, 16)

        ln.close()

        arena_free = [(s_sc, m_rq_done), (s_dv, m_pooled), (s_sy, m_gin)]
        m_stm_free = [(s_pe, 0)] * 8

        for half in range(2):
            if "qkv" not in phases:
                break
            kbase = half * 896
            qbase = HALO + half * 1024

            att = ExitStack()
            q_fm = att.enter_context(nc.sbuf_tensor(f"q_fm{half}", [128, 8, 1024], BF16))
            k_fm = att.enter_context(nc.sbuf_tensor(f"k_fm{half}", [128, 8, 1280], BF16))
            v_tm = att.enter_context(nc.sbuf_tensor(f"v_tm{half}", [128, 10, 1024], BF16))
            ow_sb = att.enter_context(nc.sbuf_tensor(f"ow_sb{half}", [128, 8, 1024], BF16))
            rout_sb = att.enter_context(nc.sbuf_tensor(f"rout_sb{half}", [128, 8, 256], BF16))
            rkT = att.enter_context(nc.sbuf_tensor(f"rkT{half}", [64, 4, 64], BF16))
            rkbf = att.enter_context(nc.sbuf_tensor(f"rkbf{half}", [64, 256], BF16))
            rv_aug = att.enter_context(nc.sbuf_tensor(f"rv_aug{half}", [64, 4, 64], BF16))
            otcol = att.enter_context(nc.sbuf_tensor(f"otcol{half}", [65, 1], BF16))
            ebc = att.enter_context(nc.sbuf_tensor(f"ebc{half}", [16, 8, 128], BF16))
            P6 = att.enter_context(nc.psum_tensor(f"P6_{half}", [128, 512], F32))
            P7 = att.enter_context(nc.psum_tensor(f"P7_{half}", [128, 512], F32))
            pfree[6] = (s_pe, 0)
            pfree[7] = (s_pe, 0)

            wqkv = ExitStack()
            wqk_buf = wqkv.enter_context(nc.sbuf_tensor(f"wqk{half}", [128, 2, 1024], BF16))
            wv_sb = wqkv.enter_context(nc.sbuf_tensor(f"wv{half}", [128, 8, 1024], BF16))

            ld_qk = [0] * 16
            mm_qk = [0] * 16

            for sem, vv in arena_free:
                g.wait_ge(sem, vv)
            inc(g.dma_start(out=ebc[:, :, :], in_=ebc_in), s_gp, 16)
            m_bvf = val(s_gp)

            for j in range(2):
                ld_qk[j] = inc(g.dma_start(out=wqk_buf[:, j, :], in_=qkvw_qk[j]), s_gp, 16)

            for j in range(16):
                is_q = j < 8
                oc = j if is_q else j - 8
                ngroups = [(0, 512), (512, 1024)] if is_q else [(0, 512), (512, 1024), (1024, 1280)]
                base = qbase if is_q else kbase
                dest = q_fm if is_q else k_fm
                banks = [0, 1, 4] if (j % 2 == 0) else [2, 3, 4]
                grp_mm = [0] * len(ngroups)

                tn.wait_ge(s_gp, ld_qk[j])
                tn.wait_ge(s_dv, m_ln_done)
                for gi, (a0, a1) in enumerate(ngroups):
                    bank = banks[gi]
                    tn.wait_ge(*pfree[bank])
                    for dc in range(8):
                        ins = tn.matmul(P[bank][:, 0:a1 - a0],
                                        wqk_buf[:, j % 2, dc * 128:(dc + 1) * 128],
                                        xn_fm[:, dc, base + a0:base + a1],
                                        start=(dc == 0), stop=(dc == 7))
                    grp_mm[gi] = inc(ins, s_pe, 1)
                mm_qk[j] = grp_mm[-1]

                colidx = oc if j < 8 else 8 + oc
                for gi, (a0, a1) in enumerate(ngroups):
                    bank = banks[gi]
                    s.wait_ge(s_pe, grp_mm[gi])
                    m = inc(s.activation(dest[:, oc, a0:a1], P[bank][:, 0:a1 - a0],
                                         AF.Identity, bias=bcol[:, colidx:colidx + 1],
                                         scale=1.0), s_sc, 1)
                    pfree[bank] = (s_sc, m)

                if j + 2 < 16:
                    g.wait_ge(s_pe, mm_qk[j])
                    ld_qk[j + 2] = inc(g.dma_start(out=wqk_buf[:, j % 2, :],
                                                   in_=qkvw_qk[j + 2]), s_gp, 16)

            for dc in range(8):
                inc(g.dma_start(out=wv_sb[:, dc, :], in_=qkvw_v[dc]), s_gp, 16)
            m_wv = val(s_gp)

            for dc in range(8):
                inc(g.dma_start(out=ow_sb[:, dc, :], in_=outw_in[dc]), s_gp, 16)
            for oc in range(8):
                inc(g.dma_start(out=rout_sb[:, oc, :], in_=routw_in[oc]), s_gp, 16)
            m_ow = val(s_gp)

            for t in range(10):
                for gi in range(2):
                    bank = 4 + (t * 2 + gi) % 2
                    if bank == 4 + 1:
                        bank = 5

                    tn.wait_ge(s_gp, m_wv)
                    tn.wait_ge(s_dv, m_bias_rep)
                    tn.wait_ge(*pfree[bank])
                    for dc in range(8):
                        ins = tn.matmul(P[bank][:, :],
                                        xn_fm[:, dc, kbase + t * 128:kbase + (t + 1) * 128],
                                        wv_sb[:, dc, gi * 512:(gi + 1) * 512],
                                        start=(dc == 0), stop=(dc == 7))
                    inc(ins, s_pe, 1)
                    mmv = val(s_pe)

                    v.wait_ge(s_pe, mmv)
                    m = inc(v.tensor_tensor(v_tm[:, t, gi * 512:(gi + 1) * 512],
                                            P[bank][:, :],
                                            bias_rep[:, gi * 512:(gi + 1) * 512],
                                            ALU.add), s_dv, 1)
                    pfree[bank] = (s_dv, m)
            m_v_done = val(s_dv)

            if half == 0 and dbg_out("qkv", [128, 3, 512]) is not None:
                g.wait_ge(s_sc, val(s_sc))
                g.wait_ge(s_dv, m_v_done)
                inc(g.dma_start(out=dbg["qkv"][:, 0:1, :], in_=q_fm[:, 0:1, 0:512]), s_gp, 16)
                inc(g.dma_start(out=dbg["qkv"][:, 1:2, :], in_=k_fm[:, 0:1, 0:512]), s_gp, 16)
                inc(g.dma_start(out=dbg["qkv"][:, 2:3, :], in_=v_tm[:, 0:1, 0:512]),
                    s_gp, 16)

            wqkv.close()

            if "attn" not in phases:
                att.close()
                arena_free = [(s_dv, m_v_done), (s_sc, val(s_sc))]
                continue

            scr = ExitStack()
            pt_sb = scr.enter_context(nc.sbuf_tensor(f"pt{half}", [128, 2, 512], BF16))
            local_n = scr.enter_context(nc.sbuf_tensor(f"locn{half}", [128, 8, 256], BF16))
            remote_n = scr.enter_context(nc.sbuf_tensor(f"remn{half}", [128, 8, 256], BF16))
            mixed = scr.enter_context(nc.sbuf_tensor(f"mixed{half}", [128, 8, 256], BF16))
            avst = scr.enter_context(nc.sbuf_tensor(f"avst{half}", [128, 256], F32))
            rr = scr.enter_context(nc.sbuf_tensor(f"rr{half}", [1, 2048], BF16))
            rrL = scr.enter_context(nc.sbuf_tensor(f"rrL{half}", [1, 512], BF16))
            rrt = scr.enter_context(nc.sbuf_tensor(f"rrt{half}", [1, 512], F32))
            ptl = scr.enter_context(nc.sbuf_tensor(f"ptl{half}", [65, 512], BF16))
            latst = scr.enter_context(nc.sbuf_tensor(f"latst{half}", [128, 256], F32))
            latout = scr.enter_context(nc.sbuf_tensor(f"latout{half}", [128, 2, 256], BF16))
            gsb = scr.enter_context(nc.sbuf_tensor(f"gsb{half}", [16, 256], BF16))
            gtmp = scr.enter_context(nc.sbuf_tensor(f"gtmp{half}", [16, 256], F32))
            dtmp = scr.enter_context(nc.sbuf_tensor(f"dtmp{half}", [128, 256], BF16))

            v.wait_ge(s_sy, m_rkvall)
            inc(v.tensor_copy(rkbf[:, :], rkv_all[:, 0:256]), s_dv, 1)
            m_rkbf = val(s_dv)

            tn.wait_ge(s_dv, m_rkbf)
            tn.wait_ge(*pfree[6])
            for h in range(4):
                ins = tn.matmul(P6[0:64, h * 64:(h + 1) * 64],
                                rkbf[:, h * 64:(h + 1) * 64], ident[0:64, 0:64],
                                start=True, stop=True)
            inc(ins, s_pe, 1)
            m_rkt_mm = val(s_pe)

            v.wait_ge(s_pe, m_rkt_mm)
            m = inc(v.tensor_copy(rkT[:, :, :],
                                  P6[0:64, 0:256].rearrange("p (h n) -> p h n", h=4)),
                    s_dv, 1)
            pfree[6] = (s_dv, m)
            for h in range(4):
                v.tensor_copy(rv_aug[:, h, :],
                              rkv_all[:, 256 + h * 64:256 + (h + 1) * 64])
            v.memset(otcol[0:64, :], 1.0)
            v.memset(otcol[64:65, :], TINY)
            inc(v.memset(ptl[64:65, :], 1.0), s_dv, 1)
            m_lat_prep = val(s_dv)

            exp_m = {}
            av_m = {}
            avmult_m = {}
            recip_m = {}
            bc_m = {}
            lat_ms = {}
            m_gl_prev = 0
            m_mix_prev = 0
            m_oww_prev = 0

            for tq in range(attn_tq):
                w0 = tq + half
                tcol = (tq % 2) * 128
                use_F = (half == 0 and tq == 0)
                m0 = maskF if use_F else (maskA if tq % 2 == 0 else maskB)

                def emit_scores(bb_):
                    par_ = bb_ % 2
                    heads_ = [4 * bb_, 4 * bb_ + 2, 4 * bb_ + 1, 4 * bb_ + 3]
                    ins_ = None
                    for s_i_, hh_ in enumerate(heads_):
                        r0_ = 64 * (hh_ % 2)
                        dch_ = hh_ // 2
                        cb_ = [0, 1] if s_i_ < 2 else [2, 3]
                        scol_ = par_ * 256 + (s_i_ % 2) * 128
                        for c_ in range(2):
                            ko_ = 128 * (w0 + c_)
                            ins_ = tn.matmul(P[cb_[c_]][:, scol_:scol_ + 128],
                                             k_fm[r0_:r0_ + 64, dch_, ko_:ko_ + 128],
                                             q_fm[r0_:r0_ + 64, dch_, tq * 128:(tq + 1) * 128],
                                             start=True, stop=True)
                    inc(ins_, s_pe, 1)
                    return val(s_pe)

                sc_mm = {}
                for bb in range(4):
                    avbank = 4 + bb % 2
                    par = bb % 2
                    heads = [4 * bb, 4 * bb + 2, 4 * bb + 1, 4 * bb + 3]

                    if bb == 0:
                        if (tq, bb) == (0, 0):
                            tn.wait_ge(s_dv, m_v_done)
                            tn.wait_ge(s_sc, m_rq_done)
                        for b_ in range(4):
                            tn.wait_ge(*pfree[b_])
                        sc_mm[0] = emit_scores(0)
                        sc_mm[1] = emit_scores(1)
                    elif bb < 3:
                        tn.wait_ge(s_sc, exp_m[(tq, bb - 1)])
                        sc_mm[bb + 1] = emit_scores(bb + 1)
                    m_sc_mm = sc_mm[bb]

                    if attn_sub < 2:
                        exp_m[(tq, bb)] = val(s_sc)
                        av_m[(tq, bb)] = val(s_pe)
                        recip_m[(tq, bb)] = val(s_dv)
                        bc_m[(tq, bb)] = val(s_pe)
                        avmult_m[(tq, bb)] = val(s_dv)
                        continue

                    s.wait_ge(s_pe, m_sc_mm)
                    if (tq, bb) != (0, 0):
                        prev = (tq, bb - 1) if bb > 0 else (tq - 1, 3)
                        s.wait_ge(s_pe, av_m[prev])
                    pc = par * 256
                    s.activation(pt_sb[:, 0, 0:256], P[0][:, pc:pc + 256], AF.Exp,
                                 bias=BSH, scale=0.125)
                    s.activation(pt_sb[:, 0, 256:512], P[2][:, pc:pc + 256], AF.Exp,
                                 bias=BSH, scale=0.125)
                    s.activation(pt_sb[:, 1, 0:256], P[1][:, pc:pc + 256], AF.Exp,
                                 bias=BSH, scale=0.125)
                    inc(s.activation(pt_sb[:, 1, 256:512], P[3][:, pc:pc + 256], AF.Exp,
                                     bias=BSH, scale=0.125), s_sc, 1)
                    exp_m[(tq, bb)] = val(s_sc)
                    for b_ in range(4):
                        pfree[b_] = (s_sc, exp_m[(tq, bb)])

                    if attn_sub < 3:
                        av_m[(tq, bb)] = val(s_pe)
                        recip_m[(tq, bb)] = val(s_dv)
                        bc_m[(tq, bb)] = val(s_pe)
                        avmult_m[(tq, bb)] = val(s_dv)
                        continue

                    v.wait_ge(s_sc, exp_m[(tq, bb)])
                    v.tensor_tensor(pt_sb[:, 0, :], pt_sb[:, 0, :], m0[:, :], ALU.mult)
                    inc(v.tensor_tensor(pt_sb[:, 1, :], pt_sb[:, 1, :], maskC[:, :],
                                        ALU.mult), s_dv, 1)
                    m_mask = val(s_dv)

                    if attn_sub < 4:
                        av_m[(tq, bb)] = val(s_pe)
                        recip_m[(tq, bb)] = val(s_dv)
                        bc_m[(tq, bb)] = val(s_pe)
                        avmult_m[(tq, bb)] = val(s_dv)
                        continue

                    tn.wait_ge(s_dv, m_mask)
                    key2 = (tq, bb - 2) if bb >= 2 else (tq - 1, bb + 2)
                    if key2 in avmult_m:
                        tn.wait_ge(s_dv, avmult_m[key2])
                    prevd = (tq, bb - 1) if bb > 0 else (tq - 1, 3)
                    if prevd in recip_m:
                        tn.wait_ge(s_dv, recip_m[prevd])
                    else:
                        tn.wait_ge(*pfree[6])
                    for s_i, hh in enumerate(heads):
                        r0 = 64 * (hh % 2)
                        scol = (s_i % 2) * 128
                        for c in range(2):
                            tn.matmul(P[avbank][r0:r0 + 64, scol:scol + 128],
                                      v_tm[:, w0 + c, hh * 64:(hh + 1) * 64],
                                      pt_sb[:, c, s_i * 128:(s_i + 1) * 128],
                                      start=(c == 0), stop=(c == 1))
                    tn.matmul(P6[0:1, :], ones_bf[:, 0:1], pt_sb[:, 0, :],
                              start=True, stop=False)
                    inc(tn.matmul(P6[0:1, :], ones_bf[:, 0:1], pt_sb[:, 1, :],
                                  start=False, stop=True), s_pe, 1)
                    av_m[(tq, bb)] = val(s_pe)

                    if attn_sub < 5:
                        recip_m[(tq, bb)] = val(s_dv)
                        bc_m[(tq, bb)] = val(s_pe)
                        avmult_m[(tq, bb)] = val(s_dv)
                        continue

                    v.wait_ge(s_pe, av_m[(tq, bb)])
                    if (tq - 1, bb) in bc_m:
                        v.wait_ge(s_pe, bc_m[(tq - 1, bb)])
                    v.tensor_scalar(rrt[0:1, :], P6[0:1, :], TINY, None, ALU.add)
                    with nc.allow_low_precision(reason="softmax recip as one bf16 factor"):
                        inc(v.reciprocal(rr[0:1, bb * 512:(bb + 1) * 512], rrt[0:1, :]),
                            s_dv, 1)
                    recip_m[(tq, bb)] = val(s_dv)

                    if attn_sub < 6:
                        bc_m[(tq, bb)] = val(s_pe)
                        avmult_m[(tq, bb)] = val(s_dv)
                        continue

                    tn.wait_ge(s_dv, recip_m[(tq, bb)])
                    key2 = (tq, bb - 2) if bb >= 2 else (tq - 1, bb + 2)
                    if key2 in avmult_m:
                        tn.wait_ge(s_dv, avmult_m[key2])
                    if bb < 2 and (tq - 1) in lat_ms:
                        tn.wait_ge(s_dv, lat_ms[tq - 1]["mult"])
                    ins = None
                    for s_i, hh in enumerate(heads):
                        r0 = 64 * (hh % 2)
                        pcol = (bb % 2) * 256 + (s_i % 2) * 128
                        ins = tn.matmul(P7[r0:r0 + 64, pcol:pcol + 128],
                                        ones_bf[0:1, 0:64],
                                        rr[0:1, bb * 512 + s_i * 128:bb * 512 + (s_i + 1) * 128],
                                        start=True, stop=True)
                    inc(ins, s_pe, 1)
                    bc_m[(tq, bb)] = val(s_pe)

                    if attn_sub < 7:
                        avmult_m[(tq, bb)] = val(s_dv)
                        continue

                    s.wait_ge(s_pe, av_m[(tq, bb)])
                    key1 = (tq, bb - 1) if bb > 0 else (tq - 1, 3)
                    if key1 in avmult_m:
                        s.wait_ge(s_dv, avmult_m[key1])
                    inc(s.copy(avst[:, 0:256], P[avbank][:, 0:256]), s_sc, 1)
                    m_avevac = val(s_sc)
                    pfree[avbank] = (s_sc, m_avevac)

                    if attn_sub < 8:
                        avmult_m[(tq, bb)] = val(s_dv)
                        continue

                    v.wait_ge(s_pe, bc_m[(tq, bb)])
                    v.wait_ge(s_sc, m_avevac)
                    if m_gl_prev:
                        v.wait_ge(s_pe, m_gl_prev)
                    if m_mix_prev:
                        v.wait_ge(s_dv, m_mix_prev)
                    ins = None
                    for s_i, hh in enumerate(heads):
                        r0 = 64 * (hh % 2)
                        pcol = (bb % 2) * 256 + (s_i % 2) * 128
                        ins = v.tensor_tensor(local_n[r0:r0 + 64, hh // 2, tcol:tcol + 128],
                                              avst[r0:r0 + 64,
                                                   (s_i % 2) * 128:(s_i % 2) * 128 + 128],
                                              P7[r0:r0 + 64, pcol:pcol + 128], ALU.mult)
                    inc(ins, s_dv, 1)
                    avmult_m[(tq, bb)] = val(s_dv)

                if not attn_lat:
                    continue
                lm = {}

                tn.wait_ge(*pfree[0])
                tn.wait_ge(s_dv, m_lat_prep)
                ins = None
                for h in range(4):
                    ins = tn.matmul(P[0][0:64, h * 128:(h + 1) * 128], rkT[:, h, :],
                                    rq_sb[:, h, half * 1024 + tq * 128:half * 1024 + (tq + 1) * 128],
                                    start=True, stop=True)
                inc(ins, s_pe, 1)
                lm["lsc"] = val(s_pe)

                s.wait_ge(s_pe, lm["lsc"])
                if (tq - 1) in lat_ms:
                    s.wait_ge(s_pe, lat_ms[tq - 1]["av"])
                inc(s.activation(ptl[0:64, :], P[0][0:64, :], AF.Exp,
                                 bias=BSH[0:64, :], scale=0.125), s_sc, 1)
                lm["exp"] = val(s_sc)
                pfree[0] = (s_sc, lm["exp"])

                v.wait_ge(s_sc, lm["exp"])
                inc(v.tensor_scalar(ptl[0:64, :], ptl[0:64, :],
                                    latmask[:, half * 8 + tq:half * 8 + tq + 1], None,
                                    ALU.mult), s_dv, 1)
                lm["mask"] = val(s_dv)

                tn.wait_ge(s_dv, lm["mask"])
                tn.wait_ge(*pfree[1])
                tn.wait_ge(*pfree[3])
                if (tq - 1) in lat_ms:
                    tn.wait_ge(s_dv, lat_ms[tq - 1]["rec"])
                for h in range(4):
                    r0 = 64 * (h % 2)
                    sl = (256 if h % 2 else 0) + (h // 2) * 128
                    tn.matmul(P[1][r0:r0 + 64, sl:sl + 128], rv_aug[0:64, h, 0:64],
                              ptl[0:64, h * 128:(h + 1) * 128], start=True, stop=True)
                ins = None
                for h in range(4):
                    ins = tn.matmul(P[3][0:1, h * 128:(h + 1) * 128], otcol[0:65, 0:1],
                                    ptl[0:65, h * 128:(h + 1) * 128], start=True, stop=True)
                inc(ins, s_pe, 1)
                lm["av"] = val(s_pe)

                v.wait_ge(s_pe, lm["av"])
                if (tq - 1) in lat_ms:
                    v.wait_ge(s_pe, lat_ms[tq - 1]["bc"])
                with nc.allow_low_precision(reason="softmax recip as one bf16 factor"):
                    inc(v.reciprocal(rrL[0:1, 0:512], P[3][0:1, :]), s_dv, 1)
                lm["rec"] = val(s_dv)
                pfree[3] = (s_dv, lm["rec"])

                tn.wait_ge(s_dv, lm["rec"])
                tn.wait_ge(s_dv, avmult_m[(tq, 2)])
                if (tq - 1) in lat_ms:
                    tn.wait_ge(s_dv, lat_ms[tq - 1]["mult"])
                ins = None
                for h in range(4):
                    r0 = 64 * (h % 2)
                    pcol = (h // 2) * 128
                    ins = tn.matmul(P7[r0:r0 + 64, pcol:pcol + 128], ones_bf[0:1, 0:64],
                                    rrL[0:1, h * 128:(h + 1) * 128], start=True, stop=True)
                inc(ins, s_pe, 1)
                lm["bc"] = val(s_pe)

                s.wait_ge(s_pe, lm["av"])
                if (tq - 1) in lat_ms:
                    s.wait_ge(s_dv, lat_ms[tq - 1]["mult"])
                s.copy(latst[0:64, 0:256], P[1][0:64, 0:256])
                inc(s.copy(latst[64:128, 0:256], P[1][64:128, 256:512]), s_sc, 1)
                lm["evac"] = val(s_sc)
                pfree[1] = (s_sc, lm["evac"])

                v.wait_ge(s_pe, lm["bc"])
                v.wait_ge(s_sc, lm["evac"])
                ins = None
                for h in range(4):
                    r0 = 64 * (h % 2)
                    ins = v.tensor_tensor(latout[r0:r0 + 64, h // 2, tcol:tcol + 128],
                                          latst[r0:r0 + 64, (h // 2) * 128:(h // 2) * 128 + 128],
                                          P7[r0:r0 + 64, (h // 2) * 128:(h // 2) * 128 + 128],
                                          ALU.mult)
                inc(ins, s_dv, 1)
                lm["mult"] = val(s_dv)
                lat_ms[tq] = lm

                if tq % 2 == 1 and attn_grp:
                    grp_rout_mm = {}
                    for oc in range(8):
                        bank = 2 + oc % 2

                        tn.wait_ge(s_dv, lat_ms[tq]["mult"])
                        tn.wait_ge(s_dv, lat_ms[tq - 1]["mult"])
                        tn.wait_ge(*pfree[bank])
                        for dc in range(2):
                            ins = tn.matmul(P[bank][:, 0:256],
                                            rout_sb[:, oc, dc * 128:(dc + 1) * 128],
                                            latout[:, dc, 0:256], start=(dc == 0), stop=(dc == 1))
                        inc(ins, s_pe, 1)
                        mmv = val(s_pe)

                        s.wait_ge(s_pe, mmv)
                        if m_mix_prev:
                            s.wait_ge(s_dv, m_mix_prev)
                        m = inc(s.activation(remote_n[:, oc, :], P[bank][:, 0:256],
                                             AF.Identity, bias=bcol[:, 24 + oc:25 + oc],
                                             scale=1.0), s_sc, 1)
                        pfree[bank] = (s_sc, m)
                    m_remote = val(s_sc)

                    tn.wait_ge(s_sc, m_remote)
                    tn.wait_ge(s_dv, avmult_m[(tq, 3)])
                    tn.wait_ge(*pfree[5])
                    tn.wait_ge(s_dv, lat_ms[tq]["rec"])
                    for dc in range(8):
                        tn.matmul(P[5][0:16, 0:256], bdg[:, dc, 0:16], local_n[:, dc, :],
                                  start=(dc == 0), stop=False)
                    for dc in range(8):
                        ins = tn.matmul(P[5][0:16, 0:256], bdg[:, dc, 16:32], remote_n[:, dc, :],
                                        start=False, stop=(dc == 7))
                    inc(ins, s_pe, 1)
                    m_gl = val(s_pe)
                    m_gl_prev = m_gl

                    s.wait_ge(s_pe, m_gl)
                    inc(s.activation(gtmp[:, :], P[5][0:16, 0:256], AF.Exp,
                                     bias=ZCOL[0:16, :], scale=-1.0), s_sc, 1)
                    m_gexp = val(s_sc)
                    pfree[5] = (s_sc, m_gexp)

                    v.wait_ge(s_sc, m_gexp)
                    inc(v.tensor_scalar(gtmp[:, :], gtmp[:, :], 1.0, None, ALU.add), s_dv, 1)
                    v.wait_ge(s_dv, val(s_dv))
                    inc(v.reciprocal(gtmp[:, :], gtmp[:, :]), s_dv, 1)
                    v.wait_ge(s_dv, val(s_dv))
                    inc(v.tensor_copy(gsb[:, :], gtmp[:, :]), s_dv, 1)
                    m_gate = val(s_dv)

                    m_mix_rounds = []
                    for rnd in range(2):
                        tn.wait_ge(s_dv, m_gate)
                        tn.wait_ge(*pfree[0])
                        tn.wait_ge(*pfree[1])
                        for k in range(4):
                            dc = rnd * 4 + k
                            bank = k % 2
                            ins = tn.matmul(P[bank][:, (k // 2) * 256:(k // 2) * 256 + 256],
                                            ebc[:, dc, :], gsb[:, :], start=True, stop=True)
                        inc(ins, s_pe, 1)
                        m_gbc = val(s_pe)

                        v.wait_ge(s_pe, m_gbc)
                        if m_oww_prev:
                            v.wait_ge(s_pe, m_oww_prev)
                        for k in range(4):
                            dc = rnd * 4 + k
                            bank = k % 2
                            v.tensor_tensor(dtmp[:, :], local_n[:, dc, :],
                                            remote_n[:, dc, :], ALU.subtract)
                            v.tensor_tensor(dtmp[:, :], dtmp[:, :],
                                            P[bank][:, (k // 2) * 256:(k // 2) * 256 + 256],
                                            ALU.mult)
                            ins = v.tensor_tensor(mixed[:, dc, :], dtmp[:, :],
                                                  remote_n[:, dc, :], ALU.add)
                        inc(ins, s_dv, 1)
                        m_mix_rounds.append(val(s_dv))
                        pfree[0] = (s_dv, m_mix_rounds[-1])
                        pfree[1] = (s_dv, m_mix_rounds[-1])
                    m_mix_done = m_mix_rounds[-1]
                    m_mix_prev = m_mix_done

                    for tt in range(2):
                        tloc = (tq - 1) + tt
                        bank0, bank1 = (2, 3) if tt == 0 else (4, 5)

                        tn.wait_ge(s_dv, m_mix_done)
                        tn.wait_ge(s_gp, m_ow)
                        tn.wait_ge(*pfree[bank0])
                        tn.wait_ge(*pfree[bank1])
                        for dc in range(8):
                            tn.matmul(P[bank0][:, :], mixed[:, dc, tt * 128:(tt + 1) * 128],
                                      ow_sb[:, dc, 0:512], start=(dc == 0), stop=(dc == 7))
                        for dc in range(8):
                            ins = tn.matmul(P[bank1][:, :], mixed[:, dc, tt * 128:(tt + 1) * 128],
                                            ow_sb[:, dc, 512:1024], start=(dc == 0), stop=(dc == 7))
                        inc(ins, s_pe, 1)
                        m_oww = val(s_pe)

                        v.wait_ge(s_pe, m_oww)
                        if m_stm_free[tloc][1] > 0:
                            v.wait_ge(*m_stm_free[tloc])
                        v.tensor_tensor(s_tm[:, tloc, 0:512], P[bank0][:, :],
                                        bias_rep[:, 1024:1536], ALU.add)
                        m = inc(v.tensor_tensor(s_tm[:, tloc, 512:1024], P[bank1][:, :],
                                                bias_rep[:, 1536:2048], ALU.add), s_dv, 1)
                        pfree[bank0] = (s_dv, m)
                        pfree[bank1] = (s_dv, m)
                    m_oww_prev = m_oww

                    if half == 0 and tq == 1:
                        if dbg_out("g0_local", [128, 8, 256]) is not None:
                            g.wait_ge(s_dv, m_mix_done)
                            inc(g.dma_start(out=dbg["g0_local"], in_=local_n[:, :, :]), s_gp, 16)
                            inc(g.dma_start(out=dbg["g0_remote"], in_=remote_n[:, :, :]), s_gp, 16)
                            inc(g.dma_start(out=dbg["g0_mixed"], in_=mixed[:, :, :]), s_gp, 16)
                            inc(g.dma_start(out=dbg["g0_gate"], in_=gsb[:, :]), s_gp, 16)

            if attn_tq < 8 or not attn_lat or not attn_grp or attn_sub < 8:
                _cut()
                scr.close()
                att.close()
                return nc, dbg

            m_attn_sc = val(s_sc)
            m_attn_dv = val(s_dv)

            if half == 0 and dbg_out("s_tm", [128, 8, 1024]) is not None:
                sy.wait_ge(s_sc, m_attn_sc)
                inc(sy.dma_start(out=dbg["s_tm"], in_=s_tm[:, :, :]), s_st, 16)
            if half == 0 and dbg_out("local_n", [128, 8, 256]) is not None:
                g.wait_ge(s_dv, m_attn_dv)
                inc(g.dma_start(out=dbg["local_n"], in_=local_n[:, :, :]), s_gp, 16)
            if half == 0 and dbg_out("remote_n", [128, 8, 256]) is not None:
                g.wait_ge(s_dv, m_attn_dv)
                inc(g.dma_start(out=dbg["remote_n"], in_=remote_n[:, :, :]), s_gp, 16)

            scr.close()
            att.close()

            if "ffn" not in phases:
                arena_free = [(s_sc, m_attn_sc), (s_dv, m_attn_dv)]
                continue

            ffn = ExitStack()
            h_sb = ffn.enter_context(nc.sbuf_tensor(f"h_sb{half}", [128, 24, 1024], BF16))
            wd_sb = ffn.enter_context(nc.sbuf_tensor(f"wd_sb{half}", [128, 24, 1024], BF16))
            wg_buf = ffn.enter_context(nc.sbuf_tensor(f"wg_buf{half}", [128, 2, 1024], BF16))
            wu_buf = ffn.enter_context(nc.sbuf_tensor(f"wu_buf{half}", [128, 2, 1024], BF16))
            sg_buf = ffn.enter_context(nc.sbuf_tensor(f"sg_buf{half}", [128, 2, 512], BF16))
            g_buf = ffn.enter_context(nc.sbuf_tensor(f"g_buf{half}", [128, 512], F32))
            xre = ffn.enter_context(nc.sbuf_tensor(f"xre{half}", [128, 512], F32))
            PF = [ffn.enter_context(nc.psum_tensor(f"PF{i}_{half}", [128, 512], F32))
                  for i in range(2)]
            pfree[6] = (s_pe, 0)
            pfree[7] = (s_pe, 0)

            attn_done = [(s_sc, m_attn_sc), (s_dv, m_attn_dv)]

            ld_gu = [0] * 24
            ld_wd = [0] * 24
            mm_gu = [0] * 24

            for sem, vv in attn_done:
                g.wait_ge(sem, vv)
            for j in range(2):
                inc(g.dma_start(out=wg_buf[:, j, :], in_=wgt_in[j]), s_gp, 16)
                ld_gu[j] = inc(g.dma_start(out=wu_buf[:, j, :], in_=wut_in[j]), s_gp, 16)

            for hc in range(24):
                bsel = hc % 2
                if bsel == 0:
                    pg = [P[0], P[1], P[2], P[3]]
                    bk = [0, 1, 2, 3]
                else:
                    pg = [P[4], P[5], PF[0], PF[1]]
                    bk = [4, 5, 6, 7]

                tn.wait_ge(s_gp, ld_gu[hc])
                for b in bk:
                    tn.wait_ge(*pfree[b])
                for gi in range(2):
                    for dc in range(8):
                        tn.matmul(pg[gi][:, :], wg_buf[:, bsel, dc * 128:(dc + 1) * 128],
                                  xn_fm[:, dc, qbase + gi * 512:qbase + (gi + 1) * 512],
                                  start=(dc == 0), stop=(dc == 7))
                    for dc in range(8):
                        ins = tn.matmul(pg[2 + gi][:, :], wu_buf[:, bsel, dc * 128:(dc + 1) * 128],
                                        xn_fm[:, dc, qbase + gi * 512:qbase + (gi + 1) * 512],
                                        start=(dc == 0), stop=(dc == 7))
                inc(ins, s_pe, 1)
                mm_gu[hc] = val(s_pe)

                s.wait_ge(s_pe, mm_gu[hc])
                if hc >= 1:
                    pass
                s.activation(sg_buf[:, 0, :], pg[0][:, :], AF.Silu, scale=SILU1)
                inc(s.activation(sg_buf[:, 1, :], pg[1][:, :], AF.Silu, scale=SILU1),
                    s_sc, 1)
                m_silu = val(s_sc)
                pfree[bk[0]] = (s_sc, m_silu)
                pfree[bk[1]] = (s_sc, m_silu)

                v.wait_ge(s_sc, m_silu)
                v.tensor_tensor(h_sb[:, hc, 0:512], sg_buf[:, 0, :], pg[2][:, :], ALU.mult)
                inc(v.tensor_tensor(h_sb[:, hc, 512:1024], sg_buf[:, 1, :], pg[3][:, :],
                                    ALU.mult), s_dv, 1)
                m_h = val(s_dv)
                pfree[bk[2]] = (s_dv, m_h)
                pfree[bk[3]] = (s_dv, m_h)

                if hc + 2 < 24:
                    g.wait_ge(s_pe, mm_gu[hc])
                    inc(g.dma_start(out=wg_buf[:, hc % 2, :], in_=wgt_in[hc + 2]), s_gp, 16)
                    ld_gu[hc + 2] = inc(g.dma_start(out=wu_buf[:, hc % 2, :],
                                                    in_=wut_in[hc + 2]), s_gp, 16)
                ld_wd[hc] = inc(g.dma_start(out=wd_sb[:, hc, :], in_=wdt_in[hc]), s_gp, 16)
            m_h_all = val(s_dv)
            m_wd_all = ld_wd[23]

            m_epi_prev = 0
            for tt in range(8):
                bank0, bank1 = (0, 1) if tt % 2 == 0 else (2, 3)

                if tt == 0:
                    for sem, vv in attn_done:
                        sy.wait_ge(sem, vv)

                tn.wait_ge(s_dv, m_h_all)
                tn.wait_ge(s_gp, m_wd_all)
                tn.wait_ge(*pfree[bank0])
                tn.wait_ge(*pfree[bank1])
                for hc in range(24):
                    tn.matmul(P[bank0][:, :], h_sb[:, hc, tt * 128:(tt + 1) * 128],
                              wd_sb[:, hc, 0:512], start=(hc == 0), stop=(hc == 23))
                for hc in range(24):
                    ins = tn.matmul(P[bank1][:, :], h_sb[:, hc, tt * 128:(tt + 1) * 128],
                                    wd_sb[:, hc, 512:1024], start=(hc == 0), stop=(hc == 23))
                inc(ins, s_pe, 1)
                m_down = val(s_pe)

                for gi in range(2):
                    bank = bank0 if gi == 0 else bank1
                    if m_epi_prev:
                        sy.wait_ge(s_dv, m_epi_prev)
                    inc(sy.dma_start(
                        out=xre[:, :],
                        in_=x_in[qbase + tt * 128:qbase + (tt + 1) * 128,
                                 gi * 512:(gi + 1) * 512]), s_sy, 16)
                    m_xre = val(s_sy)

                    s.wait_ge(s_pe, m_down)
                    if m_epi_prev:
                        s.wait_ge(s_dv, m_epi_prev)
                    inc(s.activation(g_buf[:, :], P[bank][:, :], AF.Silu, scale=SILU2),
                        s_sc, 1)
                    m_g = val(s_sc)
                    pfree[bank] = (s_sc, m_g)

                    v.wait_ge(s_sc, m_g)
                    v.wait_ge(s_sy, m_xre)
                    v.tensor_tensor(g_buf[:, :], g_buf[:, :],
                                    s_tm[:, tt, gi * 512:(gi + 1) * 512], ALU.mult)
                    inc(v.tensor_tensor(xre[:, :], xre[:, :], g_buf[:, :], ALU.add),
                        s_dv, 1)
                    m_epi = val(s_dv)
                    m_epi_prev = m_epi

                    sy.wait_ge(s_dv, m_epi)
                    inc(sy.dma_start(
                        out=y_out[half * 1024 + tt * 128:half * 1024 + (tt + 1) * 128,
                                  gi * 512:(gi + 1) * 512],
                        in_=xre[:, :]), s_st, 16)
                m_stm_free[tt] = (s_dv, m_epi_prev)

            ffn.close()
            arena_free = [(s_dv, val(s_dv)), (s_sc, val(s_sc)), (s_pe, val(s_pe))]

        sy.wait_ge(s_st, val(s_st))
        sy.wait_ge(s_gp, val(s_gp))

    return nc, dbg


def _prep_inputs(inputs):
    x = np.asarray(inputs["L0_attn_x"], np.float32).reshape(S, D)
    qkv_w = np.asarray(inputs["L0_attn_qkv_w"], np.float32)
    qkv_b = np.asarray(inputs["L0_attn_qkv_b"], np.float32)
    rq_w = np.asarray(inputs["L0_attn_rq_w"], np.float32)
    rq_b = np.asarray(inputs["L0_attn_rq_b"], np.float32)
    rkv_w = np.asarray(inputs["L0_attn_rkv_w"], np.float32)
    rkv_b = np.asarray(inputs["L0_attn_rkv_b"], np.float32)
    rout_w = np.asarray(inputs["L0_attn_rout_w"], np.float32)
    rout_b = np.asarray(inputs["L0_attn_rout_b"], np.float32)
    out_w = np.asarray(inputs["L0_attn_out_w"], np.float32)
    out_b = np.asarray(inputs["L0_attn_out_b"], np.float32)
    gl_w = np.asarray(inputs["L0_attn_gl_w"], np.float32)
    gr_w = np.asarray(inputs["L0_attn_gr_w"], np.float32)
    w_gate = np.asarray(inputs["L0_ffn_w_gate"], np.int32)
    w_up = np.asarray(inputs["L0_ffn_w_up"], np.int32)
    w_down = np.asarray(inputs["L0_ffn_w_down"], np.int32)

    x_pad = np.concatenate([np.zeros((HALO, D), np.float32), x], 0)

    qkvw_qk = np.ascontiguousarray(
        qkv_w[:, 0:2048].reshape(8, 128, 16, 128).transpose(2, 1, 0, 3).reshape(16, 128, 1024))
    qkvw_v = np.ascontiguousarray(qkv_w[:, 2048:3072].reshape(8, 128, 1024))
    outw_r = np.ascontiguousarray(out_w.reshape(8, 128, 1024))
    rqw_r = np.ascontiguousarray(
        rq_w.reshape(8, 128, 4, 64).transpose(2, 1, 0, 3).reshape(4, 128, 512))
    rkvw_r = np.ascontiguousarray(rkv_w.reshape(8, 128, 512))
    routw_r = np.ascontiguousarray(
        rout_w.reshape(2, 128, 8, 128).transpose(2, 1, 0, 3).reshape(8, 128, 256))
    wgt_r = np.ascontiguousarray(
        w_gate.reshape(24, 128, 8, 128).transpose(0, 3, 2, 1).reshape(24, 128, 1024))
    wut_r = np.ascontiguousarray(
        w_up.reshape(24, 128, 8, 128).transpose(0, 3, 2, 1).reshape(24, 128, 1024))
    wdt_r = np.ascontiguousarray(w_down.T.reshape(24, 128, 1024))

    bcolv = np.zeros((128, 40), np.float32)
    bcolv[:, 0:24] = qkv_b.reshape(24, 128).T
    bcolv[:, 24:32] = rout_b.reshape(8, 128).T
    bcolv[0:64, 32:36] = rq_b.reshape(4, 64).T
    browv = np.zeros((1, 2560), np.float32)
    browv[0, 0:1024] = qkv_b[2048:3072]
    browv[0, 1024:2048] = out_b
    browv[0, 2048:2560] = rkv_b

    bdgv = np.zeros((1024, 32), np.float32)
    for h in range(16):
        bdgv[h * 64:(h + 1) * 64, h] = gl_w[:, 0]
        bdgv[h * 64:(h + 1) * 64, 16 + h] = gr_w[:, 0]
    bdgv = np.ascontiguousarray(bdgv.reshape(8, 128, 32).transpose(1, 0, 2))

    ebcv = np.zeros((16, 8, 128), ml_dtypes.bfloat16)
    for dc in range(8):
        for pp in range(128):
            ebcv[(dc * 128 + pp) // 64, dc, pp] = 1.0

    identv = np.eye(128, dtype=ml_dtypes.bfloat16)

    r = np.arange(128)
    c = np.arange(128)
    m0 = (c[:, None] >= r[None, :]).astype(np.float32)
    m0e = m0.copy()
    m0e[0, 0] = 0.0
    m1 = (c[:, None] < r[None, :]).astype(np.float32)
    maskAv = np.tile(m0e, (1, 4)).astype(ml_dtypes.bfloat16)
    maskBv = np.tile(m0, (1, 4)).astype(ml_dtypes.bfloat16)
    maskCv = np.tile(m1, (1, 4)).astype(ml_dtypes.bfloat16)
    maskF0 = np.zeros((128, 512), ml_dtypes.bfloat16)

    shared = dict(qkvw_qk=qkvw_qk, qkvw_v=qkvw_v, outw_in=outw_r, rqw_in=rqw_r,
                  rkvw_in=rkvw_r, routw_in=routw_r, wgt_in=wgt_r, wut_in=wut_r,
                  wdt_in=wdt_r, bcol_in=bcolv, brow_in=browv, bdg_in=bdgv,
                  ebc_in=ebcv, ident_in=identv, identf_in=np.eye(64, dtype=np.float32),
                  maskA_in=maskAv, maskB_in=maskBv,
                  maskC_in=maskCv)
    in_maps = []
    ll = np.arange(64)
    for cidx in range(NCORES):
        m = dict(shared)
        m["x_in"] = np.ascontiguousarray(x_pad[cidx * T:cidx * T + TB])
        m["maskF_in"] = maskF0 if cidx == 0 else maskAv
        lmv = np.zeros((64, 16), np.float32)
        for jj in range(16):
            lmv[:, jj] = (ll < cidx * 8 + jj // 2).astype(np.float32)
        m["latmask_in"] = lmv
        in_maps.append(m)
    return in_maps


def kernel(**inputs):
    global LAST_DEBUG
    in_maps = _prep_inputs(inputs)
    nc, dbg = build(phases=PHASES, debug=DEBUG)
    res = run_bass_kernel_spmd(nc, in_maps, core_ids=list(range(NCORES)))
    LAST_DEBUG = {}
    for name in dbg:
        LAST_DEBUG[name] = [res.results[cc].get("dbg_" + name) for cc in range(NCORES)]
    out = np.concatenate([res.results[cc]["y_out"] for cc in range(NCORES)], axis=0)
    return out.reshape(1, S, D).astype(np.float32)
